# version 11
# baseline (speedup 1.0000x reference)
import numpy as np
from contextlib import ExitStack

import concourse.bass as bass
import concourse.mybir as mybir
from concourse.bass_utils import run_bass_kernel_spmd
from concourse import library_config as LC

F32 = mybir.dt.float32
BF16 = mybir.dt.bfloat16
I16 = mybir.dt.int16
I32 = mybir.dt.int32
AF = mybir.ActivationFunctionType
AL = mybir.AluOpType
AX = mybir.AxisListType

D = 1024
NC8 = 8
H = 16
DH = 64
FF0 = 2816
DS = 128
E = 16
FEXP = 2048
FSH = 2048
NEG = -1.0e30

SWAP_MASK = [(i ^ 1) for i in range(32)]


class P:

    def __init__(self, nc, stack):
        self.nc = nc
        self.stack = stack
        self.q = {k: [] for k in ("t", "v", "s", "g", "y")}
        self.sem = {}
        self.cnt = {}
        self.gd_rr = 0
        self.yd_rr = 0

    def mksem(self, name):
        h = self.stack.enter_context(self.nc.semaphore(name))
        self.sem[name] = h
        self.cnt[name] = 0
        return h

    SELFWAIT = "v"

    def emit(self, eng, fn, waits=(), inc=None, by=1):
        h = None
        swl = []
        if eng in P.SELFWAIT and inc == eng and self.cnt[eng] > 0:
            swl = [(self.sem[eng], self.cnt[eng])]
        if inc is not None:
            self.cnt[inc] += by
            h = self.sem[inc]
        wl = swl + [(self.sem[n], v) for (n, v) in waits if v > 0]

        def op(e, fn=fn, wl=wl, h=h, by=by):
            for (s, v) in wl:
                e.wait_ge(s, v)
            inst = fn(e)
            if h is not None:
                inst.then_inc(h, by)
        self.q[eng].append(op)
        return self.cnt[inc] if inc is not None else None

    def t(self, fn, waits=()):
        return self.emit("t", fn, waits, inc="t")

    def v(self, fn, waits=()):
        return self.emit("v", fn, waits, inc="v")

    def s(self, fn, waits=()):
        return self.emit("s", fn, waits, inc="s")

    def g(self, fn, waits=()):
        return self.emit("g", fn, waits, inc="g")

    def gdma(self, fn, waits=()):
        k = self.gd_rr
        self.gd_rr = (k + 1) % 8
        name = "gd%d" % k
        prior = self.cnt[name]
        w = list(waits) + ([(name, prior)] if prior else [])
        self.emit("g", fn, w, inc=name, by=16)
        return (name, self.cnt[name])

    def ydma(self, fn, waits=()):
        k = self.yd_rr
        self.yd_rr = (k + 1) % 8
        name = "yd%d" % k
        prior = self.cnt[name]
        w = list(waits) + ([(name, prior)] if prior else [])
        self.emit("y", fn, w, inc=name, by=16)
        return (name, self.cnt[name])

    def wait(self, eng, waits):
        self.emit(eng, lambda e: None, waits, inc=None)

    def barrier(self):
        snap = [(n, self.cnt[n]) for n in self.cnt]
        for eng in ("t", "v", "s", "g", "y"):
            self.wait(eng, [(n, v) for (n, v) in snap if v > 0])

    def c(self, name):
        return self.cnt[name]


def _ap(t, offset, dims):
    return bass.AP(t, offset, dims)


def build_program(T=2048, blocks=(0, 1, 2, 3)):
    nc = bass.Bass("TRN2", target_bir_lowering=False)
    stack = ExitStack()
    NT = T // 512
    NTT = T // 128

    xT = nc.dram_tensor("xT", [D, T], F32, kind="ExternalInput").ap()
    out = nc.dram_tensor("out", [T, D], F32, kind="ExternalOutput").ap()
    wqkv = nc.dram_tensor("wqkv", [D, 3 * D], F32, kind="ExternalInput").ap()
    wout_d = nc.dram_tensor("wout", [D, D], F32, kind="ExternalInput").ap()
    trig_d = nc.dram_tensor("trig", [128, 2 * T], BF16, kind="ExternalInput").ap()
    cbf_d = nc.dram_tensor("cbf", [128, 640], BF16, kind="ExternalInput").ap()
    cf32_d = nc.dram_tensor("cf32", [128, 512], F32, kind="ExternalInput").ap()
    if 3 in blocks:
        gwT3_d = nc.dram_tensor("gatewT", [D, E], F32, kind="ExternalInput").ap()
        WgT_d = nc.dram_tensor("WgT", [E, D, FEXP], F32, kind="ExternalInput").ap()
        WuT_d = nc.dram_tensor("WuT", [E, D, FEXP], F32, kind="ExternalInput").ap()
        WdT_d = nc.dram_tensor("WdT", [E, FEXP, D], F32, kind="ExternalInput").ap()
        WsgT_d = nc.dram_tensor("WsgT", [D, FSH], F32, kind="ExternalInput").ap()
        WsuT_d = nc.dram_tensor("WsuT", [D, FSH], F32, kind="ExternalInput").ap()
        WsdT_d = nc.dram_tensor("WsdT", [FSH, D], F32, kind="ExternalInput").ap()
        xln_h = nc.dram_tensor("xln_h", [T, D], BF16).ap()
    if 2 in blocks:
        At_d = nc.dram_tensor("At", [DS, DS], F32, kind="ExternalInput").ap()
        BwT_d = nc.dram_tensor("BwT", [D, DS], F32, kind="ExternalInput").ap()
        CwT_d = nc.dram_tensor("CwT", [DS, D], F32, kind="ExternalInput").ap()
        DwT_d = nc.dram_tensor("DwT", [D, D], F32, kind="ExternalInput").ap()
        sw1T_d = nc.dram_tensor("sw1T", [D, 512], F32, kind="ExternalInput").ap()
        sw2T_d = nc.dram_tensor("sw2T", [512, DS], F32, kind="ExternalInput").ap()
        gwT_d = nc.dram_tensor("gwT", [D, D], F32, kind="ExternalInput").ap()
        bias2_d = nc.dram_tensor("bias2", [128, 32], F32, kind="ExternalInput").ap()
    if 1 in blocks:
        wgate_d = nc.dram_tensor("wgate", [FF0, D], F32, kind="ExternalInput").ap()
        wup_d = nc.dram_tensor("wup", [FF0, D], F32, kind="ExternalInput").ap()
        wdown_d = nc.dram_tensor("wdown", [D, FF0], F32, kind="ExternalInput").ap()
        wgq_h = nc.dram_tensor("wgq_h", [D, FF0], BF16).ap()
        wuq_h = nc.dram_tensor("wuq_h", [D, FF0], BF16).ap()
        wdq_h = nc.dram_tensor("wdq_h", [FF0, D], BF16).ap()

    BASE = (nc.sbuf_base + 31) // 32 * 32
    TOP = nc.sbuf_top
    cur = [BASE]
    offs = {}

    def at(name, shape, dt, offset=None):
        nbytes = int(np.prod(shape[1:])) * {F32: 4, BF16: 2, I32: 4, I16: 2}[dt]
        if offset is None:
            offset = cur[0]
            cur[0] = (offset + nbytes + 31) // 32 * 32
        assert offset + nbytes <= TOP, (name, offset + nbytes, TOP)
        offs[name] = offset
        return nc.alloc_sbuf_tensor_at(name, shape, dt, offset=offset)

    xres = at("xres", [128, NC8, T], F32)
    bufA = at("bufA", [128, NC8, T], BF16)
    bufB = at("bufB", [128, NC8, T], BF16)
    scrA = at("scrA", [128, 512], F32)
    scrB = at("scrB", [128, 512], F32)
    scrC = at("scrC", [128, 512], F32)
    cb = at("cb", [128, 640], BF16)
    cf = at("cf", [128, 512], F32)
    rows = at("rows", [1, 1536], F32)
    lbuf = at("lbuf", [128, 8], F32)
    rbuf = at("rbuf", [128, 8], F32)
    sqs = at("sqs", [128, 512], BF16)
    PH = cur[0]

    trg = at("trg", [128, 2, T], BF16, PH)
    qb = at("qb", [128, 2, T], BF16, PH + 8192)
    kb = at("kb", [128, 2, T], BF16, PH + 16384)
    vtm = at("vtm", [128, 2, NTT, 130], BF16, PH + 24576)
    pts = at("pts", [128, 2, 512], BF16, PH + 32960)
    oacc = at("oacc", [128, 512], F32, PH + 35008)
    wpair = at("wpair", [128, 2, 3, NC8, 128], BF16, PH + 37056)
    wobuf = at("wobuf", [128, NC8, 512], BF16, PH)
    tout = at("tout", [128, NC8, 128], F32, offs["bufB"])

    if 1 in blocks:
        qwin = at("qwin", [128, 2, 512], F32, PH + 49344)
        qsA = at("qsA", [128, 512], F32, PH + 53440)
        qsB = at("qsB", [128, 512], F32, PH + 55488)
        qsg = at("qsg", [128, 512], BF16, PH + 57536)
        qbm = at("qbm", [128, 5, 32], F32, PH + 58560)
        qiA = at("qiA", [128, 512], BF16, PH + 59200)
        qiB = at("qiB", [128, 512], BF16, PH + 60224)
        qow = at("qow", [128, 2, 512], BF16, PH + 61248)
        qstage = at("qstage", [128, 2, 512], BF16, (TOP - 2048) // 32 * 32)
        ysb = at("ysb", [128, 512], F32, PH)
        qyb = at("qyb", [128, 512], BF16, PH + 2048)
        sgbuf = at("sgbuf", [128, 512], F32, PH + 3072)
        hstage = at("hstage", [128, 512], F32, PH + 5120)
        wgbuf = at("wgbuf", [128, NC8, 512], BF16, PH + 7168)
        wubuf = at("wubuf", [128, NC8, 512], BF16, PH + 15360)
        wdbuf = at("wdbuf", [128, 2, 22, 128], BF16, PH + 23552)
        hfm_off = offs["bufA"]
        hfm = at("hfm", [128, 22, 512], BF16, hfm_off)
    if 2 in blocks:
        NCH = T // 64
        a1b = at("a1b", [128, 4, T], BF16, PH)
        ssb = at("ssb", [128, T], F32, PH + 8 * T)
        usb = at("usb", [128, T], F32, PH + 12 * T)
        hsf = at("hsf", [128, T], BF16, PH + 16 * T)
        sw1b = at("sw1b", [128, NC8, 512], BF16, PH + 18 * T)
        sw2b = at("sw2b", [128, 4, 128], BF16, PH + 18 * T + 8192)
        bwb = at("bwb", [128, NC8, 128], BF16, PH + 18 * T + 9216)
        cwb = at("cwb", [128, NC8, 128], BF16, PH + 18 * T + 11264)
        atb = at("atb", [128, 128], BF16, PH + 18 * T + 13312)
        a64b = at("a64b", [128, 128], BF16, PH + 18 * T + 13568)
        apw = at("apw", [128, 128], BF16, PH + 18 * T + 13824)
        b2b = at("b2b", [128, 32], F32, PH + 18 * T + 14080)
        cdb = at("cdb", [128, 8], F32, PH + 18 * T + 14208)
        w2s = at("w2s", [128, 2, 2, NC8, 128], BF16, PH + 18 * T + 14336)
        hstart = at("hstart", [128, NCH + 1], BF16, PH + 18 * T + 22528)
        sc2A = at("sc2A", [128, 512], F32, PH + 18 * T + 22656)
        sc2B = at("sc2B", [128, 512], F32, PH + 18 * T + 24704)
    if 3 in blocks:
        CAP = T // 8
        SG = T // 8
        NST3 = max(CAP // 128, 1)
        o3 = [PH]

        def nxt(nbytes):
            r = o3[0]
            o3[0] = (r + nbytes + 31) // 32 * 32
            return r
        wsgb = at("wsgb", [128, 2, NC8, 256], BF16, nxt(8192))
        wsub = at("wsub", [128, 2, NC8, 256], BF16, nxt(8192))
        wsdb = at("wsdb", [128, 2, 16, 128], BF16, nxt(8192))
        hshb = at("hshb", [128, 16, 512], BF16, nxt(16384))
        gwT3b = at("gwT3b", [128, NC8, 16], BF16, nxt(256))
        sctm = at("sctm", [128, T // 128, 16], F32, nxt((T // 128) * 64))
        slottm = at("slottm", [128, T // 128, 16], F32, nxt((T // 128) * 64))
        iotab = at("iotab", [128, 256], F32, nxt(1024))
        ttrow = at("ttrow", [128, T // 128], F32, nxt((T // 128) * 4))
        pcol = at("pcol", [128, 1], F32, nxt(32))
        rhs3 = at("rhs3", [128, T // 128, E, 3], BF16, nxt((T // 128) * 96))
        pbt = at("pbt", [128, 256], BF16, nxt(512))
        semb = at("semb", [16, T], F32, offs["bufA"])
        junkb = at("junkb", [16, T], F32, offs["bufA"] + 4 * T)
        mskb = at("mskb", [16, T], F32, offs["bufA"] + 8 * T)
        iu32 = at("iu32", [128, E, 2], I32, (TOP - 2048) // 32 * 32)
        idw = at("idw", [128, E, 2, 3], F32, (TOP - 4096) // 32 * 32)
        idc = at("idc", [128, E, 2], F32, (TOP - 5120) // 32 * 32)
        smls = at("smls", [16, 16], F32, (TOP - 5632) // 32 * 32)
        ob = [PH]

        def nxb(nbytes):
            r = ob[0]
            ob[0] = (r + nbytes + 31) // 32 * 32
            return r
        xtokt = at("xtokt", [128, 2, NST3, D], BF16, nxb(2 * NST3 * D * 2))
        xtokf = at("xtokf", [128, 2, NC8, CAP], BF16, nxb(2 * NC8 * CAP * 2))
        heb = at("heb", [128, 2, FEXP // 128, CAP], BF16, offs["bufA"])
        wdeb = at("wdeb", [128, 2, FEXP // 128, 256], BF16, nxb(2 * 16 * 256 * 2))
        eob = at("eob", [128, NST3, D], F32, nxb(NST3 * D * 4))
        wgeb = at("wgeb", [128, 2, NC8, 512], BF16, offs["bufB"])
        wueb = at("wueb", [128, 2, NC8, 512], BF16, offs["bufB"] + 16384)
    IDENT = cb[:, 0:128]
    MDIAG = cb[:, 128:256]
    MFULL = cb[:, 256:384]
    MDIAGT = cb[:, 512:640]
    ONESB = cb[:, 384:385]
    ONEROW = cf[0:1, 0:128]
    IDENTF = cf[:, 128:256]
    EPSAP = cf[0:1, 300:301]
    MBIAS = cf[:, 301:302]
    r_mm = rows[0:1, 0:512]
    r_v = rows[0:1, 512:1024]
    r_t1 = rows[0:1, 1024:1536]

    ps_s = stack.enter_context(nc.psum_tensor("ps_s", [128, 2048], F32))
    ps_a = stack.enter_context(nc.psum_tensor("ps_a", [128, 512], F32))
    ps_b = stack.enter_context(nc.psum_tensor("ps_b", [128, 512], F32))
    ps_c = stack.enter_context(nc.psum_tensor("ps_c", [128, 512], F32))
    ps_t = stack.enter_context(nc.psum_tensor("ps_t", [128, 1024], BF16))

    p = P(nc, stack)
    for name in ("t", "v", "s", "g"):
        p.mksem(name)
    for k in range(8):
        p.mksem("gd%d" % k)
        p.mksem("yd%d" % k)

    xsrc = _ap(xT.tensor, 0, [[T, 128], [128 * T, NC8], [1, T]])
    l1 = p.ydma(lambda e: e.dma_start(xres[:], xsrc))
    l2 = p.ydma(lambda e: e.dma_start(cb[:], cbf_d))
    l3 = p.ydma(lambda e: e.dma_start(cf[:], cf32_d))
    for eng in ("t", "v", "s", "g"):
        p.wait(eng, [l1, l2, l3])

    def layer_norm(dst, quant=False):
        for tc in range(NT):
            t0 = tc * 512
            for c in range(NC8):
                vv = p.v(lambda e, c=c, t0=t0: e.tensor_copy(
                    bufA[:, c, t0:t0 + 512], xres[:, c, t0:t0 + 512]))
                sv = p.s(lambda e, c=c, t0=t0: e.activation(
                    sqs[:], xres[:, c, t0:t0 + 512], AF.Square))
                p.t(lambda e, c=c, t0=t0: e.matmul(
                    ps_c[0:1, :], ONESB, bufA[:, c, t0:t0 + 512],
                    start=(c == 0), stop=(c == NC8 - 1)), waits=[("v", vv)])
                sv2 = p.t(lambda e, c=c: e.matmul(
                    ps_b[0:1, :], ONESB, sqs[:],
                    start=(c == 0), stop=(c == NC8 - 1)), waits=[("s", sv)])
                p.wait("s", [("t", sv2)])
            mmv = p.c("t")
            p.v(lambda e: e.tensor_scalar_mul(r_mm, ps_c[0:1, :], 1.0 / D),
                waits=[("t", mmv)])
            p.v(lambda e: e.tensor_scalar_mul(r_v, ps_b[0:1, :], 1.0 / D))
            p.v(lambda e: e.tensor_tensor(r_t1, r_mm, r_mm, AL.mult))
            vv2 = p.v(lambda e: e.tensor_tensor(r_v, r_v, r_t1, AL.subtract))
            sv2 = p.s(lambda e: e.activation(r_v, r_v, AF.Sqrt, bias=EPSAP),
                      waits=[("v", vv2)])
            p.v(lambda e: e.reciprocal(r_t1, r_v), waits=[("s", sv2)])
            vv3 = p.v(lambda e: e.scalar_tensor_tensor(
                r_v, r_mm, -1.0, r_t1, AL.mult, AL.mult))
            p.t(lambda e: e.matmul(ps_a[:], ONEROW, r_t1, start=True, stop=True),
                waits=[("v", vv3)])
            tb2 = p.t(lambda e: e.matmul(ps_b[:], ONEROW, r_v,
                                         start=True, stop=True))
            lastv = 0
            for c in range(NC8):
                p.v(lambda e, c=c, t0=t0: e.tensor_tensor(
                    scrA[:], xres[:, c, t0:t0 + 512], ps_a[:], AL.mult),
                    waits=[("t", tb2)])
                if not quant:
                    lastv = p.v(lambda e, c=c, t0=t0: e.tensor_tensor(
                        dst[:, c, t0:t0 + 512], scrA[:], ps_b[:], AL.add))
                else:
                    lastv = p.v(lambda e, c=c, t0=t0: e.tensor_tensor(
                        scrB[:], scrA[:], ps_b[:], AL.add))
                    fp4q_y(scrB[:], dst[:, c, t0:t0 + 512])
                    lastv = p.c("v")
            p.wait("t", [("v", lastv)])
        return p.c("v")

    def fp4q(src, dst, W, waits0=(), waitsd=()):
        NB = W // 16
        bm = qbm[:, 0, 0:NB]
        sc = qbm[:, 1, 0:NB]
        rr = qbm[:, 2, 0:NB]
        r2 = qbm[:, 3, 0:NB]
        hs = qbm[:, 4, 0:NB]
        qt_ = qbm.ap().tensor
        s3 = _ap(src.tensor, src.offset, [list(src.ap[0]), [16, NB], [1, 16]])
        p.v(lambda e: e.tensor_reduce(bm, s3, AX.X, AL.max,
                                      apply_absolute_value=True),
            waits=list(waits0))
        p.v(lambda e: e.tensor_scalar(sc, bm, 1.0 / 6.0, 1e-12,
                                      AL.mult, AL.max))
        p.v(lambda e: e.reciprocal(rr, sc))
        p.v(lambda e: e.tensor_scalar(r2, rr, 2.0, None, AL.mult))
        r2v = p.v(lambda e: e.tensor_scalar(hs, sc, 0.5, None, AL.mult))
        r2b = _ap(qt_, 3 * 32, [[160, 128], [1, NB], [0, 16]])
        hsb = _ap(qt_, 4 * 32, [[160, 128], [1, NB], [0, 16]])
        qsA_t = qsA.ap().tensor
        xnv = p.g(lambda e: e.tensor_tensor(
            _ap(qsA_t, 0, [[512, 128], [16, NB], [1, 16]]), s3, r2b, AL.mult),
            waits=[("v", r2v), ("s", p.c("s"))])
        sgv = p.s(lambda e: e.activation(qsg[:, 0:W], qsA[:, 0:W], AF.Sign),
                  waits=[("g", xnv)])
        av = p.v(lambda e: e.scalar_tensor_tensor(
            qsB[:, 0:W], qsA[:, 0:W], -1.0, qsA[:, 0:W], AL.mult, AL.max),
            waits=[("g", xnv)])
        p.v(lambda e: e.tensor_scalar(qiB[:, 0:W], qsB[:, 0:W], 5.0, None,
                                      AL.is_gt))
        for th in (7.0, 10.0, 10.0):
            p.v(lambda e, th=th: e.scalar_tensor_tensor(
                qiB[:, 0:W], qsB[:, 0:W], th, qiB[:, 0:W], AL.is_gt, AL.add))
        p.v(lambda e: e.tensor_scalar(qiA[:, 0:W], qsB[:, 0:W], 0.5, None,
                                      AL.is_gt))
        for th in (1.5, 2.5, 3.5):
            p.v(lambda e, th=th: e.scalar_tensor_tensor(
                qiA[:, 0:W], qsB[:, 0:W], th, qiA[:, 0:W], AL.is_gt, AL.add))
        cmb = p.v(lambda e: e.scalar_tensor_tensor(
            qsA[:, 0:W], qiB[:, 0:W], 2.0, qiA[:, 0:W], AL.mult, AL.add),
            waits=[("s", sgv)])
        smv = p.g(lambda e: e.tensor_tensor(qsB[:, 0:W], qsA[:, 0:W],
                                            qsg[:, 0:W], AL.mult),
                  waits=[("v", cmb)])
        d3 = _ap(dst.tensor, dst.offset, [list(dst.ap[0]), [16, NB], [1, 16]])
        return p.v(lambda e: e.tensor_tensor(
            d3, _ap(qsB.ap().tensor, 0, [[512, 128], [16, NB], [1, 16]]),
            hsb, AL.mult), waits=[("g", smv)] + list(waitsd))

    def fp4q_y(chunk_src, chunk_dst):
        p.v(lambda e: e.transpose(ysb[:], chunk_src))
        NB2 = 16
        bm = qbm[:, 0, 0:32]
        sc = qbm[:, 1, 0:32]
        rr = qbm[:, 2, 0:32]
        r2 = qbm[:, 3, 0:32]
        hs = qbm[:, 4, 0:32]
        qt_ = qbm.ap().tensor
        ys_t = ysb.ap().tensor
        y4 = _ap(ys_t, 0, [[512, 128], [32, NB2], [16, 2], [1, 16]])
        p.v(lambda e: e.tensor_reduce(
            _ap(qt_, 0, [[160, 128], [2, NB2], [1, 2]]), y4, AX.X, AL.max,
            apply_absolute_value=True))
        p.v(lambda e: e.tensor_scalar(sc, bm, 1.0 / 6.0, 1e-12, AL.mult, AL.max))
        p.v(lambda e: e.reciprocal(rr, sc))
        p.v(lambda e: e.tensor_scalar(r2, rr, 2.0, None, AL.mult))
        p.v(lambda e: e.tensor_scalar(hs, sc, 0.5, None, AL.mult))
        r2b = _ap(qt_, 3 * 32, [[160, 128], [2, NB2], [1, 2], [0, 16]])
        hsb = _ap(qt_, 4 * 32, [[160, 128], [2, NB2], [1, 2], [0, 16]])
        W = 512
        qsA_t = qsA.ap().tensor
        r2v = p.c("v")
        xnv = p.g(lambda e: e.tensor_tensor(
            qsA[:, 0:W], ysb[:], r2b, AL.mult),
            waits=[("v", r2v), ("s", p.c("s"))])
        sgv = p.s(lambda e: e.activation(qsg[:, 0:W], qsA[:, 0:W], AF.Sign),
                  waits=[("g", xnv)])
        av = p.v(lambda e: e.scalar_tensor_tensor(
            qsB[:, 0:W], qsA[:, 0:W], -1.0, qsA[:, 0:W], AL.mult, AL.max),
            waits=[("g", xnv)])
        p.v(lambda e: e.tensor_scalar(qiB[:, 0:W], qsB[:, 0:W], 5.0, None,
                                      AL.is_gt))
        for th in (7.0, 10.0, 10.0):
            p.v(lambda e, th=th: e.scalar_tensor_tensor(
                qiB[:, 0:W], qsB[:, 0:W], th, qiB[:, 0:W], AL.is_gt, AL.add))
        p.v(lambda e: e.tensor_scalar(qiA[:, 0:W], qsB[:, 0:W], 0.5, None,
                                      AL.is_gt))
        for th in (1.5, 2.5, 3.5):
            p.v(lambda e, th=th: e.scalar_tensor_tensor(
                qiA[:, 0:W], qsB[:, 0:W], th, qiA[:, 0:W], AL.is_gt, AL.add))
        cmb = p.v(lambda e: e.scalar_tensor_tensor(
            qsA[:, 0:W], qiB[:, 0:W], 2.0, qiA[:, 0:W], AL.mult, AL.add),
            waits=[("s", sgv)])
        smv = p.g(lambda e: e.tensor_tensor(qsB[:, 0:W], qsA[:, 0:W],
                                            qsg[:, 0:W], AL.mult),
                  waits=[("v", cmb)])
        p.v(lambda e: e.tensor_tensor(qyb[:], qsB[:, 0:W], hsb, AL.mult),
            waits=[("g", smv)])
        return p.v(lambda e: e.transpose(chunk_dst, qyb[:]))


    def qkv_dma(hp, buf):
        wl = []
        tw = [("t", p.c("t"))]
        for j, base in enumerate((128 * hp, D + 128 * hp, 2 * D + 128 * hp)):
            src = _ap(wqkv.tensor, base,
                      [[3 * D, 128], [128 * 3 * D, NC8], [1, 128]])
            wl.append(p.gdma(lambda e, j=j, src=src, buf=buf: e.dma_start(
                wpair[:, buf, j, :, :], src), waits=tw))
        return wl

    def block0_attn():
        lnv = layer_norm(bufB)
        p.wait("g", [("v", lnv)])
        NG = T // 512
        vtm_t = vtm.ap().tensor
        psc_t = ps_c.ap().tensor
        tgv = p.ydma(lambda e: e.dma_start(
            trg[:].rearrange("p a t -> p (a t)"),
            _ap(trig_d.tensor, 0, [[2 * T, 128], [T, 2], [1, T]])))
        dv = qkv_dma(0, 0)
        rope_v = {-1: lnv}
        sq_end = {}
        sc_t_end = {}
        exp_s = {}
        sc_par = 0
        fin_prev = 0
        bc_prev = 0
        for hp in range(8):
            par = hp % 2
            buf = hp % 2
            guard_t = sc_t_end.get(hp - 2, 0)
            qrv = 0
            krv = 0
            vtmv_prev = rope_v[hp - 1]
            first = True
            for tc in range(NT):
                t0 = tc * 512
                w0 = [("v", vtmv_prev)]
                if first:
                    w0 = list(dv) + w0
                    first = False
                for c in range(NC8):
                    p.t(lambda e, c=c, buf=buf, t0=t0: e.matmul(
                        ps_a[:], wpair[:, buf, 0, c, :],
                        bufB[:, c, t0:t0 + 512],
                        start=(c == 0), stop=(c == NC8 - 1)),
                        waits=(w0 if c == 0 else []))
                qmv = p.c("t")
                for c in range(NC8):
                    p.t(lambda e, c=c, buf=buf, t0=t0: e.matmul(
                        ps_b[:], wpair[:, buf, 1, c, :],
                        bufB[:, c, t0:t0 + 512],
                        start=(c == 0), stop=(c == NC8 - 1)))
                kmv = p.c("t")
                for tt2 in range(4):
                    tt = 4 * tc + tt2
                    for c in range(NC8):
                        p.t(lambda e, c=c, buf=buf, tt=tt, tt2=tt2: e.matmul(
                            ps_c[:, tt2 * 128:tt2 * 128 + 128],
                            bufB[:, c, tt * 128:tt * 128 + 128],
                            wpair[:, buf, 2, c, :],
                            start=(c == 0), stop=(c == NC8 - 1)))
                vmv = p.c("t")
                for (pspk, dstb, wv_) in ((ps_a, qb, qmv), (ps_b, kb, kmv)):
                    shw = [("t", wv_)]
                    if tc == 0 and pspk is ps_a:
                        shw += [("t", guard_t), tgv]
                    shv = p.v(lambda e, pspk=pspk: e.stream_shuffle(
                        scrC[:], pspk[:], SWAP_MASK), waits=shw)
                    gv = p.g(lambda e, t0=t0: e.tensor_tensor(
                        scrB[:], scrC[:], trg[:, 1, t0:t0 + 512], AL.mult),
                        waits=[("v", shv)])
                    p.v(lambda e, pspk=pspk, t0=t0: e.tensor_tensor(
                        scrA[:], pspk[:], trg[:, 0, t0:t0 + 512], AL.mult))
                    p.v(lambda e, dstb=dstb, t0=t0, par=par: e.tensor_tensor(
                        dstb[:, par, t0:t0 + 512], scrA[:], scrB[:], AL.add),
                        waits=[("g", gv)])
                p.v(lambda e, tc=tc, par=par: e.tensor_copy(
                    _ap(vtm_t, par * NTT * 130 + 4 * tc * 130,
                        [[2 * NTT * 130, 128], [130, 4], [65, 2], [1, 64]]),
                    _ap(psc_t, 0, [[512, 128], [128, 4], [64, 2], [1, 64]])),
                    waits=[("t", vmv)])
                vtmv_prev = p.c("v")
            p.v(lambda e, par=par: e.memset(
                _ap(vtm_t, par * NTT * 130 + 64,
                    [[2 * NTT * 130, 128], [130, NTT], [65, 2], [1, 1]]), 1.0))
            rope_v[hp] = p.c("v")
            pump(4)
            if hp < 7:
                dv = qkv_dma(hp + 1, (hp + 1) % 2)
            first_sc = True
            for h01 in range(2):
                base = 64 * h01
                for qg in range(NG):
                    ntile = 4 * qg + 4
                    pend = []
                    for t in range(ntile):
                        pr = sc_par % 2
                        psc = ps_s[:, pr * 512:pr * 512 + 512]
                        w0 = []
                        if first_sc:
                            w0.append(("v", rope_v[hp]))
                            first_sc = False
                        if sc_par - 2 in exp_s:
                            w0.append(("s", exp_s[sc_par - 2]))
                        if t == 0:
                            w0.append(("v", fin_prev))
                        band = t - 4 * qg
                        p.t(lambda e, t=t, qg=qg, base=base, par=par, psc=psc:
                            e.matmul(
                                psc, kb[base:base + 64, par,
                                        t * 128:t * 128 + 128],
                                qb[base:base + 64, par,
                                   qg * 512:qg * 512 + 512],
                                start=True, stop=True), waits=w0)
                        if band >= 0:
                            for jj in range(band):
                                p.t(lambda e, jj=jj, psc=psc: e.matmul(
                                    psc[:, jj * 128:jj * 128 + 128], IDENT,
                                    MFULL, start=False, stop=True,
                                    skip_group_check=True))
                            p.t(lambda e, band=band, psc=psc: e.matmul(
                                psc[:, band * 128:band * 128 + 128], IDENT,
                                MDIAGT, start=False, stop=True,
                                skip_group_check=True))
                        ex = p.s(lambda e, pr=pr, psc=psc:
                                 e.activation(
                                     pts[:, pr, :], psc, AF.Exp, scale=0.125,
                                     bias=MBIAS),
                                 waits=[("t", p.c("t"))])
                        exp_s[sc_par] = ex
                        pend.append((t, pr, ex))
                        if len(pend) > 1:
                            t_, pr_, ex_ = pend.pop(0)
                            p.t(lambda e, t=t_, par=par, h01=h01, pr=pr_:
                                e.matmul(
                                    ps_s[0:65, 1024:1536],
                                    vtm[:, par, t, 65 * h01:65 * h01 + 65],
                                    pts[:, pr, :],
                                    start=(t == 0), stop=False),
                                waits=[("s", ex_)])
                        sc_par += 1
                    t_, pr_, ex_ = pend.pop(0)
                    p.t(lambda e, t=t_, par=par, h01=h01, pr=pr_, nt=ntile:
                        e.matmul(
                            ps_s[0:65, 1024:1536],
                            vtm[:, par, t, 65 * h01:65 * h01 + 65],
                            pts[:, pr, :],
                            start=(t == 0), stop=True),
                        waits=[("s", ex_)])
                    pvend = p.c("t")
                    rv0 = p.s(lambda e: e.activation(
                        r_v, ps_s[64:65, 1024:1536], AF.Ln),
                        waits=[("t", pvend)])
                    rv = p.s(lambda e: e.activation(
                        r_t1, r_v, AF.Exp, scale=-1.0))
                    oev = p.v(lambda e: e.tensor_copy(
                        oacc[0:64, :], ps_s[0:64, 1024:1536]),
                        waits=[("t", pvend)])
                    bc_prev = p.t(lambda e: e.matmul(
                        ps_s[0:64, 1536:2048], cf[0:1, 0:64], r_t1,
                        start=True, stop=True), waits=[("s", rv)])
                    fin_prev = p.v(lambda e, h01=h01, hp=hp, qg=qg:
                        e.tensor_tensor(
                            bufA[64 * h01:64 * h01 + 64, hp,
                                 qg * 512:qg * 512 + 512],
                            oacc[0:64, :], ps_s[0:64, 1536:2048],
                            AL.mult), waits=[("t", bc_prev)])
                    pump(4)
            sc_t_end[hp] = p.c("t")
        p.barrier()
        for mh in range(2):
            src = _ap(wout_d.tensor, mh * 512,
                      [[D, 128], [128 * D, NC8], [1, 512]])
            wv = p.gdma(lambda e, src=src: e.dma_start(wobuf[:], src),
                        waits=[("t", p.c("t"))])
            p.wait("t", [wv])
            for mt in range(4):
                m0c = (mh * 512 + mt * 128) // 128
                for tc in range(NT):
                    t0 = tc * 512
                    for c in range(NC8):
                        p.t(lambda e, c=c, mt=mt, t0=t0: e.matmul(
                            ps_a[:], wobuf[:, c, mt * 128:mt * 128 + 128],
                            bufA[:, c, t0:t0 + 512],
                            start=(c == 0), stop=(c == NC8 - 1)))
                    omv = p.c("t")
                    rv = p.v(lambda e, m0c=m0c, t0=t0: e.tensor_tensor(
                        xres[:, m0c, t0:t0 + 512],
                        xres[:, m0c, t0:t0 + 512], ps_a[:], AL.add),
                        waits=[("t", omv)])
                    p.wait("t", [("v", rv)])

    def quant_gen():
        units = []
        for (wsrc, wdst, RS, LD) in ((wgate_d, wgq_h, D, FF0),
                                     (wup_d, wuq_h, D, FF0)):
            for rc in range(FF0 // 128):
                for scc in range(D // 512):
                    units.append((wsrc, wdst, RS, LD, rc, scc, 512))
        for rc in range(NC8):
            for scc in range((FF0 + 511) // 512):
                kw = min(512, FF0 - scc * 512)
                units.append((wdown_d, wdq_h, FF0, D, rc, scc, kw))
        N = len(units)
        ldtok = {}
        qvdone = {}
        trt = {}
        evq = {}
        sttok = {}

        def load(i):
            if i >= N:
                return
            wsrc, wdst, RS, LD, rc, scc, kw = units[i]
            par = i % 2
            ldtok[i] = p.ydma(lambda e, wsrc=wsrc, RS=RS, rc=rc, scc=scc,
                              kw=kw, par=par: e.dma_start(
                qwin[:, par, 0:kw],
                _ap(wsrc.tensor, rc * 128 * RS + scc * 512,
                    [[RS, 128], [1, kw]])),
                waits=[("v", qvdone.get(i - 2, 0))])
        load(0)
        load(1)
        for i in range(N):
            wsrc, wdst, RS, LD, rc, scc, kw = units[i]
            par = i % 2
            qvdone[i] = fp4q(
                qwin[:, par, 0:kw],
                _ap(qow.ap().tensor, par * 512, [[1024, 128], [1, kw]]),
                kw, waits0=[ldtok[i]],
                waitsd=[("t", trt.get(i - 2, 0))])
            load(i + 1)
            yield
            for j in range(kw // 128):
                w0 = []
                if j == 0:
                    w0 = [("v", qvdone[i]), ("v", evq.get(i - 1, 0))]
                p.t(lambda e, j=j, par=par: e.transpose(
                    ps_t[:, j * 128:j * 128 + 128],
                    qow[:, par, j * 128:j * 128 + 128], IDENT), waits=w0)
            trt[i] = p.c("t")
            evq[i] = p.v(lambda e, par=par, kw=kw: e.tensor_copy(
                qstage[:, par, 0:kw], ps_t[:, 0:kw]),
                waits=[("t", trt[i])] + ([sttok[i - 2]] if i - 2 in sttok
                                         else []))
            sttok[i] = p.ydma(lambda e, wdst=wdst, LD=LD, rc=rc, scc=scc,
                              kw=kw, par=par: e.dma_start(
                _ap(wdst.tensor, scc * 512 * LD + rc * 128,
                    [[LD, 128], [128 * LD, kw // 128], [1, 128]]),
                _ap(qstage.ap().tensor, par * 512,
                    [[1024, 128], [128, kw // 128], [1, 128]])),
                waits=[("v", evq[i])])
            yield

    qg_state = {"g": None}

    def pump(n=1):
        g = qg_state["g"]
        if g is None:
            return
        for _ in range(n):
            try:
                next(g)
            except StopIteration:
                qg_state["g"] = None
                return

    def block1_ffn():
        pump(10 ** 9)
        p.barrier()
        layer_norm(bufB, quant=True)
        p.barrier()
        FCW = [512] * 5 + [256]
        b1_dtok = {}
        b1_dcnt = {}
        b1_rv = {}
        for tc in range(NT):
            t0 = tc * 512
            f0 = 0
            for fc, fw in enumerate(FCW):
                dwg = p.gdma(lambda e, f0=f0, fw=fw: e.dma_start(
                    wgbuf[:, :, 0:fw],
                    _ap(wgq_h.tensor, f0,
                        [[FF0, 128], [128 * FF0, NC8], [1, fw]])),
                    waits=[("t", p.c("t"))])
                dwu = p.gdma(lambda e, f0=f0, fw=fw: e.dma_start(
                    wubuf[:, :, 0:fw],
                    _ap(wuq_h.tensor, f0,
                        [[FF0, 128], [128 * FF0, NC8], [1, fw]])),
                    waits=[("t", p.c("t"))])
                p.wait("t", [dwg, dwu])
                for tt2 in range(4):
                    tt = 4 * tc + tt2
                    for c in range(NC8):
                        p.t(lambda e, c=c, tt=tt, fw=fw: e.matmul(
                            ps_a[:, 0:fw], bufB[:, c, tt * 128:tt * 128 + 128],
                            wgbuf[:, c, 0:fw],
                            start=(c == 0), stop=(c == NC8 - 1)))
                    gv2 = p.c("t")
                    for c in range(NC8):
                        p.t(lambda e, c=c, tt=tt, fw=fw: e.matmul(
                            ps_b[:, 0:fw], bufB[:, c, tt * 128:tt * 128 + 128],
                            wubuf[:, c, 0:fw],
                            start=(c == 0), stop=(c == NC8 - 1)))
                    uv2 = p.c("t")
                    sv = p.s(lambda e, fw=fw: e.activation(
                        sgbuf[:, 0:fw], ps_a[:, 0:fw], AF.Silu),
                        waits=[("t", gv2)])
                    hv = p.v(lambda e, fw=fw: e.tensor_tensor(
                        hstage[:, 0:fw], sgbuf[:, 0:fw], ps_b[:, 0:fw],
                        AL.mult), waits=[("s", sv), ("t", uv2)])
                    p.wait("t", [("v", hv)])
                    fp4q(hstage[:, 0:fw], qow[:, 0, 0:fw], fw)
                    qv = p.c("v")
                    p.wait("t", [("v", qv)])
                    for j in range(fw // 128):
                        p.t(lambda e, j=j: e.transpose(
                            ps_t[:, j * 128:j * 128 + 128],
                            qow[:, 0, j * 128:j * 128 + 128], IDENT))
                    trv = p.c("t")
                    ev = p.v(lambda e, fc=fc, fw=fw, tt2=tt2: e.tensor_copy(
                        _ap(hfm.ap().tensor, (4 * fc) * 512 + tt2 * 128,
                            [[22 * 512, 128], [512, fw // 128], [1, 128]]),
                        ps_t[:, 0:fw]), waits=[("t", trv)])
                    p.wait("t", [("v", ev)])
                f0 += fw
            def dload(dt, tc=tc):
                if dt >= NC8:
                    return
                gd = tc * NC8 + dt
                b1_dtok[gd] = p.gdma(lambda e, dt=dt, gd=gd: e.dma_start(
                    wdbuf[:, gd % 2, :, :],
                    _ap(wdq_h.tensor, dt * 128,
                        [[D, 128], [128 * D, 22], [1, 128]])),
                    waits=[("t", b1_dcnt.get(gd - 2, 0))])
            dload(0)
            dload(1)
            for dt in range(NC8):
                gd = tc * NC8 + dt
                pbk = ps_a[:] if gd % 2 == 0 else ps_b[:]
                w0 = [b1_dtok[gd]]
                if gd - 2 in b1_rv:
                    w0.append(("v", b1_rv[gd - 2]))
                for fc2 in range(22):
                    p.t(lambda e, fc2=fc2, gd=gd, pbk=pbk: e.matmul(
                        pbk, wdbuf[:, gd % 2, fc2, :], hfm[:, fc2, :],
                        start=(fc2 == 0), stop=(fc2 == 21)),
                        waits=(w0 if fc2 == 0 else []))
                b1_dcnt[gd] = omv = p.c("t")
                dload(dt + 2)
                b1_rv[gd] = p.v(lambda e, dt=dt, t0=t0, pbk=pbk: e.tensor_tensor(
                    xres[:, dt, t0:t0 + 512],
                    xres[:, dt, t0:t0 + 512], pbk, AL.add),
                    waits=[("t", omv)])
            p.wait("t", [("v", b1_rv[tc * NC8 + NC8 - 1]),
                         ("v", b1_rv[tc * NC8 + NC8 - 2])])

    def block2_ssm():
        p.barrier()
        NCH = T // 64
        wl = []
        wl.append(p.gdma(lambda e: e.dma_start(
            sw1b[:], _ap(sw1T_d.tensor, 0,
                         [[512, 128], [128 * 512, NC8], [1, 512]]))))
        wl.append(p.gdma(lambda e: e.dma_start(
            sw2b[:], _ap(sw2T_d.tensor, 0,
                         [[DS, 128], [128 * DS, 4], [1, 128]]))))
        wl.append(p.gdma(lambda e: e.dma_start(
            bwb[:], _ap(BwT_d.tensor, 0,
                        [[DS, 128], [128 * DS, NC8], [1, 128]]))))
        wl.append(p.gdma(lambda e: e.dma_start(
            cwb[:], _ap(CwT_d.tensor, 0,
                        [[D, 128], [128, NC8], [1, 128]]))))
        wl.append(p.gdma(lambda e: e.dma_start(atb[:], At_d)))
        wl.append(p.ydma(lambda e: e.dma_start(b2b[:], bias2_d)))
        lnv = layer_norm(bufB)
        p.wait("t", wl + [("v", lnv)])
        p.wait("v", wl[-1:])
        cdv = p.v(lambda e: e.tensor_tensor(
            cdb[:], b2b[:, 6:14], b2b[:, 14:22], AL.add))
        for tc in range(NT):
            t0 = tc * 512
            for mt in range(4):
                for c in range(NC8):
                    p.t(lambda e, c=c, mt=mt, t0=t0: e.matmul(
                        ps_a[:], sw1b[:, c, mt * 128:mt * 128 + 128],
                        bufB[:, c, t0:t0 + 512],
                        start=(c == 0), stop=(c == NC8 - 1)))
                mv = p.c("t")
                sv = p.s(lambda e, mt=mt, t0=t0: e.activation(
                    a1b[:, mt, t0:t0 + 512], ps_a[:], AF.Silu,
                    bias=b2b[:, 2 + mt:3 + mt]), waits=[("t", mv)])
                p.wait("t", [("s", sv)])
            for c4 in range(4):
                p.t(lambda e, c4=c4, t0=t0: e.matmul(
                    ps_b[:], sw2b[:, c4, :], a1b[:, c4, t0:t0 + 512],
                    start=(c4 == 0), stop=(c4 == 3)))
            mv = p.c("t")
            sv = p.s(lambda e, t0=t0: e.activation(
                ssb[:, t0:t0 + 512], ps_b[:], AF.Sigmoid,
                bias=b2b[:, 1:2]), waits=[("t", mv)])
            p.wait("t", [("s", sv)])
            for c in range(NC8):
                p.t(lambda e, c=c, t0=t0: e.matmul(
                    ps_c[:], bwb[:, c, :], bufB[:, c, t0:t0 + 512],
                    start=(c == 0), stop=(c == NC8 - 1)))
            mv = p.c("t")
            uv = p.v(lambda e, t0=t0: e.scalar_tensor_tensor(
                usb[:, t0:t0 + 512], ps_c[:], b2b[:, 0:1],
                ssb[:, t0:t0 + 512], AL.add, AL.mult),
                waits=[("t", mv), ("s", sv)])
            p.wait("t", [("v", uv)])
        p.v(lambda e: e.tensor_copy(a64b[:], atb[:]))
        for it in range(6):
            tv = p.t(lambda e: e.transpose(ps_t[:, 0:128], a64b[:], IDENT),
                     waits=[("v", p.c("v"))])
            ev = p.v(lambda e: e.tensor_copy(apw[:], ps_t[:, 0:128]),
                     waits=[("t", tv)])
            mv = p.t(lambda e: e.matmul(ps_c[:, 0:128], apw[:], a64b[:],
                                        start=True, stop=True),
                     waits=[("v", ev)])
            ev2 = p.v(lambda e: e.tensor_copy(a64b[:], ps_c[:, 0:128]),
                      waits=[("t", mv)])
            p.wait("t", [("v", ev2)])
        zv = p.v(lambda e: e.memset(hstart[:], 0.0))
        p.wait("t", [("v", zv)])
        for tl in range(64):
            mv = p.t(lambda e: e.matmul(ps_c[:, 0:NCH], atb[:],
                                        hstart[:, 0:NCH],
                                        start=True, stop=True))
            uap = _ap(usb.ap().tensor, tl, [[T, 128], [64, NCH]])
            av = p.v(lambda e, uap=uap: e.tensor_tensor(
                hstart[:, 0:NCH], ps_c[:, 0:NCH], uap, AL.add),
                waits=[("t", mv)])
            p.wait("t", [("v", av)])
        bcv = p.v(lambda e: e.tensor_copy(a1b[:, 0, 0:NCH], hstart[:, 0:NCH]))
        zv = p.v(lambda e: e.memset(hstart[:], 0.0))
        p.wait("t", [("v", p.c("v"))])
        for c in range(NCH - 1):
            mv = p.t(lambda e, c=c: e.matmul(
                ps_c[:, 0:1], a64b[:], hstart[:, c:c + 1],
                start=True, stop=True))
            av = p.v(lambda e, c=c: e.tensor_tensor(
                hstart[:, c + 1:c + 2], ps_c[:, 0:1], a1b[:, 0, c:c + 1],
                AL.add), waits=[("t", mv)])
            p.wait("t", [("v", av)])
        for tl in range(64):
            if tl == 0:
                rhs = hstart[:, 0:NCH]
            else:
                rhs = _ap(hsf.ap().tensor, tl - 1, [[T, 128], [64, NCH]])
            mv = p.t(lambda e, rhs=rhs: e.matmul(
                ps_c[:, 0:NCH], atb[:], rhs, start=True, stop=True))
            uap = _ap(usb.ap().tensor, tl, [[T, 128], [64, NCH]])
            oap = _ap(hsf.ap().tensor, tl, [[T, 128], [64, NCH]])
            av = p.v(lambda e, uap=uap, oap=oap: e.tensor_tensor(
                oap, ps_c[:, 0:NCH], uap, AL.add), waits=[("t", mv)])
            p.wait("t", [("v", av)])
        scan_av = p.c("v")
        s2tok = {}
        s2cnt = {}
        y_rv = {}
        y_dv2 = {}

        def w2load(dt):
            if dt >= NC8:
                return
            j = dt % 2
            wt = [("t", s2cnt.get(dt - 2, 0))]
            wv1 = p.gdma(lambda e, dt=dt, j=j: e.dma_start(
                w2s[:, j, 0, :, :],
                _ap(DwT_d.tensor, dt * 128,
                    [[D, 128], [128 * D, NC8], [1, 128]])), waits=wt)
            wv2 = p.gdma(lambda e, dt=dt, j=j: e.dma_start(
                w2s[:, j, 1, :, :],
                _ap(gwT_d.tensor, dt * 128,
                    [[D, 128], [128 * D, NC8], [1, 128]])), waits=wt)
            s2tok[dt] = [wv1, wv2]
        w2load(0)
        w2load(1)
        YPB = [(ps_a[:, 0:512], ps_b[:, 0:512], scrA[:], scrB[:]),
               (ps_c[:, 0:512], ps_s[:, 0:512], sc2A[:], sc2B[:])]
        it = 0
        for dt in range(NC8):
            j = dt % 2
            for tc in range(NT):
                t0 = tc * 512
                pa, pb, sA, sB = YPB[it % 2]
                w0 = []
                if tc == 0:
                    w0 += s2tok[dt]
                if it < 2:
                    w0.append(("v", scan_av))
                if it - 2 in y_rv:
                    w0.append(("v", y_rv[it - 2]))
                p.t(lambda e, dt=dt, t0=t0, pa=pa: e.matmul(
                    pa, cwb[:, dt, :], hsf[:, t0:t0 + 512],
                    start=True, stop=False), waits=w0)
                for c in range(NC8):
                    p.t(lambda e, c=c, t0=t0, j=j, pa=pa: e.matmul(
                        pa, w2s[:, j, 0, c, :], bufB[:, c, t0:t0 + 512],
                        start=False, stop=(c == NC8 - 1)))
                for c in range(NC8):
                    p.t(lambda e, c=c, t0=t0, j=j, pb=pb: e.matmul(
                        pb, w2s[:, j, 1, c, :], bufB[:, c, t0:t0 + 512],
                        start=(c == 0), stop=(c == NC8 - 1)))
                mv = p.c("t")
                sw = [("t", mv)]
                if it - 2 in y_dv2:
                    sw.append(("v", y_dv2[it - 2]))
                sgv = p.s(lambda e, dt=dt, pb=pb, sB=sB: e.activation(
                    sB, pb, AF.Sigmoid,
                    bias=b2b[:, 22 + dt:23 + dt]), waits=sw)
                y_dv2[it] = p.v(lambda e, dt=dt, pa=pa, sA=sA, sB=sB:
                    e.scalar_tensor_tensor(
                        sA, pa, cdb[:, dt:dt + 1], sB,
                        AL.add, AL.mult), waits=[("s", sgv), ("v", cdv)])
                y_rv[it] = p.v(lambda e, dt=dt, t0=t0, sA=sA: e.tensor_tensor(
                    xres[:, dt, t0:t0 + 512], xres[:, dt, t0:t0 + 512],
                    sA, AL.add))
                it += 1
            s2cnt[dt] = p.c("t")
            w2load(dt + 2)
        p.wait("t", [("v", y_rv[it - 1]), ("v", y_rv[it - 2])])


    def block3_moe_pre():
        p.barrier()
        CAP = T // 8
        SG = T // 8
        lnv = layer_norm(bufB)
        gw = p.gdma(lambda e: e.dma_start(
            gwT3b[:], _ap(gwT3_d.tensor, 0, [[E, 128], [128 * E, NC8], [1, E]])))
        xst = hshb[:, 0:2, :].rearrange("p a b -> p (a b)")
        for tt in range(NTT):
            p.wait("t", [("v", p.c("v"))])
            for c in range(NC8):
                p.t(lambda e, tt=tt, c=c: e.transpose(
                    ps_t[:, c * 128:c * 128 + 128],
                    bufB[:, c, tt * 128:tt * 128 + 128], IDENT))
            trv = p.c("t")
            ev = p.v(lambda e: e.tensor_copy(xst, ps_t[:]), waits=[("t", trv)])
            dv2 = p.ydma(lambda e, tt=tt: e.dma_start(
                _ap(xln_h.tensor, tt * 128 * D, [[D, 128], [1, D]]), xst),
                waits=[("v", ev)])
            p.wait("v", [dv2])
        p.wait("t", [gw])
        for tt in range(NTT):
            for c in range(NC8):
                p.t(lambda e, tt=tt, c=c: e.matmul(
                    ps_c[:, 0:E], bufB[:, c, tt * 128:tt * 128 + 128],
                    gwT3b[:, c, :], start=(c == 0), stop=(c == NC8 - 1)))
            mv = p.c("t")
            p.v(lambda e: e.reduce_max(rbuf[:, 0:1], ps_c[:, 0:E], AX.X,
                                       negate=True), waits=[("t", mv)])
            vv = p.c("v")
            sv = p.s(lambda e, tt=tt: e.activation(
                sctm[:, tt, :], ps_c[:, 0:E], AF.Exp, bias=rbuf[:, 0:1],
                accum_out=lbuf[:, 0:1]), waits=[("v", vv), ("t", mv)])
            p.v(lambda e: e.reciprocal(rbuf[:, 1:2], lbuf[:, 0:1]),
                waits=[("s", sv)])
            scv = p.v(lambda e, tt=tt: e.tensor_scalar_mul(
                sctm[:, tt, :], sctm[:, tt, :], rbuf[:, 1:2]))
            p.wait("t", [("v", scv)])
        for tg in range(NTT // 4):
            for j in range(4):
                tt = tg * 4 + j
                p.t(lambda e, tt=tt, j=j: e.transpose(
                    ps_b[0:16, j * 128:j * 128 + 128],
                    sctm[:, tt, :], IDENTF))
            trv = p.c("t")
            ev = p.v(lambda e, tg=tg: e.tensor_copy(
                semb[:, tg * 512:tg * 512 + 512], ps_b[0:16, :]),
                waits=[("t", trv)])
            p.wait("t", [("v", ev)])
        sh_tok = {}
        sh_cnt = {}
        sd_tok = {}
        sd_cnt = {}
        chain = {}
        NQ = FSH // 256

        def shload(gq):
            if gq >= NT * NQ:
                return
            qq = gq % NQ
            j = gq % 2
            wt = [("t", sh_cnt.get(gq - 2, 0))]
            w1 = p.gdma(lambda e, qq=qq, j=j: e.dma_start(
                wsgb[:, j, :, :], _ap(WsgT_d.tensor, qq * 256,
                    [[FSH, 128], [128 * FSH, NC8], [1, 256]])), waits=wt)
            w2 = p.gdma(lambda e, qq=qq, j=j: e.dma_start(
                wsub[:, j, :, :], _ap(WsuT_d.tensor, qq * 256,
                    [[FSH, 128], [128 * FSH, NC8], [1, 256]])), waits=wt)
            sh_tok[gq] = [w1, w2]

        def sdload(gd):
            if gd >= NT * NC8:
                return
            dt = gd % NC8
            j = gd % 2
            sd_tok[gd] = [p.gdma(lambda e, dt=dt, j=j: e.dma_start(
                wsdb[:, j, :, :], _ap(WsdT_d.tensor, dt * 128,
                    [[D, 128], [128 * D, 16], [1, 128]])),
                waits=[("t", sd_cnt.get(gd - 2, 0))])]
        shload(0)
        shload(1)
        sdload(0)
        sdload(1)
        SPB = [(ps_a[:, 0:512], ps_b[:, 0:512], scrB[:]),
               (ps_c[:, 0:512], ps_s[:, 0:512], scrC[:])]
        gch = 0
        for tc in range(NT):
            t0 = tc * 512
            last_hv = 0
            for qq in range(NQ):
                gq = tc * NQ + qq
                j = gq % 2
                for ft in range(2):
                    pg, pu, sB = SPB[gch % 2]
                    w0 = []
                    if ft == 0:
                        w0 += sh_tok[gq]
                    if gch - 2 in chain:
                        w0.append(("v", chain[gch - 2]))
                    for c in range(NC8):
                        p.t(lambda e, c=c, ft=ft, j=j, t0=t0, pg=pg: e.matmul(
                            pg, wsgb[:, j, c, ft * 128:ft * 128 + 128],
                            bufB[:, c, t0:t0 + 512],
                            start=(c == 0), stop=(c == NC8 - 1)),
                            waits=(w0 if c == 0 else []))
                    gv2 = p.c("t")
                    for c in range(NC8):
                        p.t(lambda e, c=c, ft=ft, j=j, t0=t0, pu=pu: e.matmul(
                            pu, wsub[:, j, c, ft * 128:ft * 128 + 128],
                            bufB[:, c, t0:t0 + 512],
                            start=(c == 0), stop=(c == NC8 - 1)))
                    uv2 = p.c("t")
                    sw = [("t", gv2)]
                    if gch - 2 in chain:
                        sw.append(("v", chain[gch - 2]))
                    sv = p.s(lambda e, pg=pg, sB=sB: e.activation(
                        sB, pg, AF.Gelu), waits=sw)
                    hv = p.v(lambda e, qq=qq, ft=ft, sB=sB, pu=pu:
                        e.tensor_tensor(
                            hshb[:, qq * 2 + ft, :], sB, pu, AL.mult),
                        waits=[("s", sv), ("t", uv2)])
                    chain[gch] = hv
                    last_hv = hv
                    gch += 1
                sh_cnt[gq] = p.c("t")
                shload(gq + 2)
            for dt in range(NC8):
                gd = tc * NC8 + dt
                pg, pu, sB = SPB[gch % 2]
                w0 = list(sd_tok[gd])
                if dt == 0:
                    w0.append(("v", last_hv))
                if gch - 2 in chain:
                    w0.append(("v", chain[gch - 2]))
                for fc in range(16):
                    p.t(lambda e, fc=fc, gd=gd, pg=pg: e.matmul(
                        pg, wsdb[:, gd % 2, fc, :], hshb[:, fc, :],
                        start=(fc == 0), stop=(fc == 15)),
                        waits=(w0 if fc == 0 else []))
                mv = p.c("t")
                rv = p.v(lambda e, dt=dt, t0=t0, pg=pg: e.tensor_tensor(
                    xres[:, dt, t0:t0 + 512], xres[:, dt, t0:t0 + 512],
                    pg, AL.add), waits=[("t", mv)])
                chain[gch] = rv
                gch += 1
                sd_cnt[gd] = p.c("t")
                sdload(gd + 2)
        p.wait("t", [("v", chain[gch - 1]), ("v", chain[gch - 2])])
        zv = p.v(lambda e: e.memset(smls[:], 0.0))
        lo_col = 0
        hk = 0.5
        for it in range(44):
            lo = smls[0:16, lo_col:lo_col + 1]
            lo2 = smls[0:16, 4 - lo_col:5 - lo_col]
            p.v(lambda e, lo=lo, hk=hk: e.tensor_scalar(
                smls[0:16, 3:4], lo, hk, None, AL.add))
            p.v(lambda e: e.tensor_scalar(
                junkb[:], semb[:], smls[0:16, 3:4], None, AL.is_ge, AL.add,
                accum_out=smls[0:16, 1:2]))
            p.v(lambda e: e.tensor_scalar(
                smls[0:16, 2:3], smls[0:16, 1:2], CAP - 0.5, None, AL.is_ge))
            p.v(lambda e, lo=lo, lo2=lo2, hk=hk: e.scalar_tensor_tensor(
                lo2, smls[0:16, 2:3], hk, lo, AL.mult, AL.add))
            lo_col = 4 - lo_col
            hk *= 0.5
        LO = smls[0:16, lo_col:lo_col + 1]
        p.v(lambda e: e.tensor_scalar(mskb[:], semb[:], LO, None, AL.is_ge))
        zv = p.v(lambda e: e.memset(junkb[:], 0.0))
        p.v(lambda e: e.tensor_tensor_scan(
            semb[:], junkb[:], mskb[:], 0.0, AL.max, AL.add))
        p.v(lambda e: e.tensor_tensor(mskb[:], mskb[:], semb[:], AL.mult))
        p.v(lambda e: e.tensor_scalar(
            junkb[:], mskb[:], CAP + 0.5, -30000.0, AL.is_gt, AL.mult))
        sl2 = p.v(lambda e: e.scalar_tensor_tensor(
            mskb[:], mskb[:], -1.0, junkb[:], AL.add, AL.add))
        p.wait("t", [("v", sl2)])
        for tg in range(NTT // 4):
            for j in range(4):
                tt = tg * 4 + j
                p.t(lambda e, tt=tt, j=j: e.transpose(
                    ps_c[:, j * 16:j * 16 + 16],
                    mskb[:, tt * 128:tt * 128 + 128], IDENTF[0:16, 0:16]))
            trv = p.c("t")
            ev = p.v(lambda e, tg=tg: e.tensor_copy(
                _ap(slottm.ap().tensor, tg * 4 * 16,
                    [[(T // 128) * 16, 128], [1, 64]]),
                ps_c[:, 0:64]), waits=[("t", trv)])
            p.wait("t", [("v", ev)])
        g1 = p.g(lambda e: e.iota(iotab[:], [[1, 256]], base=0,
                                  channel_multiplier=0,
                                  allow_small_or_imprecise_dtypes=True))
        g2 = p.g(lambda e: e.iota(ttrow[:], [[1, T // 128]], base=0,
                                  channel_multiplier=0,
                                  allow_small_or_imprecise_dtypes=True))
        g3 = p.g(lambda e: e.iota(pcol[:], [[0, 1]], base=0,
                                  channel_multiplier=1,
                                  allow_small_or_imprecise_dtypes=True))
        NTTl = T // 128
        r3t = rhs3.ap().tensor
        p.v(lambda e: e.tensor_copy(
            _ap(r3t, 0, [[NTTl * E * 3, 128], [E * 3, NTTl], [3, E]]),
            _ap(pcol.ap().tensor, 0, [[1, 128], [0, NTTl], [0, E]])),
            waits=[("g", g3)])
        p.v(lambda e: e.tensor_copy(
            _ap(r3t, 1, [[NTTl * E * 3, 128], [E * 3, NTTl], [3, E]]),
            _ap(ttrow.ap().tensor, 0, [[NTTl, 128], [1, NTTl], [0, E]])),
            waits=[("g", g2)])
        r3v = p.v(lambda e: e.tensor_copy(
            _ap(r3t, 2, [[NTTl * E * 3, 128], [E * 3, NTTl], [3, E]]),
            sctm[:].rearrange("p a b -> p (a b)")))
        NH = max(CAP // 128, 1)
        p.v(lambda e: e.memset(idw[:], 0.0))
        p.v(lambda e: e.memset(idc[:], 0.0))
        for ex in range(E):
            for tt in range(NTTl):
                pv2 = p.v(lambda e, ex=ex, tt=tt: e.tensor_scalar(
                    pbt[:, 0:CAP], iotab[:, 0:CAP],
                    slottm[:, tt, ex:ex + 1], None, AL.is_equal),
                    waits=[("g", g1)])
                p.wait("t", [("v", pv2)])
                for hf in range(NH):
                    tgt = ps_c if hf == 0 else ps_b
                    p.t(lambda e, ex=ex, tt=tt, hf=hf, tgt=tgt: e.matmul(
                        tgt[0:min(CAP, 128), 0:3],
                        pbt[:, hf * 128:hf * 128 + min(CAP, 128)],
                        rhs3[:, tt, ex, :],
                        start=(tt == 0), stop=(tt == NTTl - 1)))
                mmv = p.c("t")
                if tt < NTTl - 1:
                    p.wait("v", [("t", mmv)])
            mmv = p.c("t")
            for hf in range(NH):
                tgt = ps_c if hf == 0 else ps_b
                p.v(lambda e, ex=ex, hf=hf, tgt=tgt: e.tensor_copy(
                    idw[0:min(CAP, 128), ex, hf, :], tgt[0:min(CAP, 128), 0:3]),
                    waits=[("t", mmv)])
            p.wait("t", [("v", p.c("v"))])
        idwt = idw.ap().tensor
        idcv = p.v(lambda e: e.scalar_tensor_tensor(
            idc[:].rearrange("p a b -> p (a b)"),
            _ap(idwt, 1, [[E * 6, 128], [3, E * 2]]), 128.0,
            _ap(idwt, 0, [[E * 6, 128], [3, E * 2]]), AL.mult, AL.add))
        uv32 = p.v(lambda e: e.tensor_copy(
            iu32[:].rearrange("p a b -> p (a b)"),
            idc[:].rearrange("p a b -> p (a b)")), waits=[("v", idcv)])
        if CAP < 128:
            uv32 = p.v(lambda e: e.tensor_copy(
                iu32[64:128, :, :].rearrange("p a b -> p (a b)"),
                _ap(pcol.ap().tensor, 64 * 1, [[1, 64], [0, E * 2]])))

    def block3_moe_experts():
        p.barrier()
        CAP = T // 8
        NST = CAP // 128
        assert CAP == 256 and NST == 2
        xlnap = _ap(xln_h.tensor, 0, [[D, T], [1, D]])
        outap = _ap(out.tensor, 0, [[D, T], [1, D]])
        IU32V = p.c("v")
        gath_toks = {}
        tfree = {}
        mm_tcnt = {}
        dn_tcnt = {}
        wl_toks = {}
        wd_toks = {}
        gbank_ev = {}
        dbank_ev = {}
        sc_toks = []
        last_hv = {}
        last_ev = {}

        def gath(ex):
            if ex >= E:
                return
            toks = []
            b = ex % 2
            for st in range(NST):
                toks.append(p.gdma(lambda e, ex=ex, st=st, b=b:
                    e.indirect_dma_start(
                        xtokt[:, b, st, :], None, xlnap,
                        bass.IndirectOffsetOnAxis(iu32[0:128, ex, st:st + 1], 0)),
                    waits=[("v", IU32V), ("t", tfree.get(ex - 2, 0))]))
            gath_toks[ex] = toks

        def wl(ex, i):
            if ex >= E:
                return
            q = 4 * ex + i
            j = q % 2
            wt = [("t", mm_tcnt.get(q - 2, 0))]
            t1 = p.gdma(lambda e, ex=ex, i=i, j=j: e.dma_start(
                wgeb[:, j, :, :],
                _ap(WgT_d.tensor, ex * D * FEXP + i * 512,
                    [[FEXP, 128], [128 * FEXP, NC8], [1, 512]])), waits=wt)
            t2 = p.gdma(lambda e, ex=ex, i=i, j=j: e.dma_start(
                wueb[:, j, :, :],
                _ap(WuT_d.tensor, ex * D * FEXP + i * 512,
                    [[FEXP, 128], [128 * FEXP, NC8], [1, 512]])), waits=wt)
            wl_toks[q] = [t1, t2]

        def wd(ex, j):
            if ex >= E:
                return
            r = 4 * ex + j
            jj = r % 2
            wd_toks[r] = [p.gdma(lambda e, ex=ex, j=j, jj=jj: e.dma_start(
                wdeb[:, jj, :, :],
                _ap(WdT_d.tensor, ex * FEXP * D + j * 256,
                    [[D, 128], [128 * D, 16], [1, 256]])),
                waits=[("t", dn_tcnt.get(r - 2, 0))])]

        def tstage(ex):
            b = ex % 2
            ev = 0
            for st in range(NST):
                if st == 0:
                    w = list(gath_toks[ex]) + [("v", last_ev.get(ex - 1, 0))]
                else:
                    w = [("v", ev)]
                for c in range(NC8):
                    p.t(lambda e, st=st, c=c, b=b: e.transpose(
                        ps_t[:, c * 128:c * 128 + 128],
                        xtokt[:, b, st, c * 128:c * 128 + 128], IDENT),
                        waits=(w if c == 0 else []))
                trv = p.c("t")
                ev = p.v(lambda e, st=st, b=b: e.tensor_copy(
                    _ap(xtokf.ap().tensor, b * (NC8 * CAP) + st * 128,
                        [[2 * NC8 * CAP, 128], [CAP, NC8], [1, 128]]),
                    _ap(ps_t.ap().tensor, 0,
                        [[1024, 128], [128, NC8], [1, 128]])),
                    waits=[("t", trv)])
            tfree[ex] = p.c("t")
            last_ev[ex] = ev

        GB = [(ps_a[:, 0:CAP], ps_b[:, 0:CAP]),
              (ps_s[:, 0:CAP], ps_s[:, 512:512 + CAP])]

        def mm(ex, i):
            b = ex % 2
            q = 4 * ex + i
            j = q % 2
            for ft in range(4):
                g = 4 * i + ft
                gg = 16 * ex + g
                pg, pu = GB[gg % 2]
                w0 = []
                if ft == 0:
                    w0 = wl_toks[q] + [("v", last_ev[ex])]
                if gg - 2 in gbank_ev:
                    w0 = w0 + [("v", gbank_ev[gg - 2])]
                for c in range(NC8):
                    p.t(lambda e, c=c, ft=ft, j=j, b=b, pg=pg: e.matmul(
                        pg, wgeb[:, j, c, ft * 128:ft * 128 + 128],
                        xtokf[:, b, c, :], start=(c == 0),
                        stop=(c == NC8 - 1)), waits=(w0 if c == 0 else []))
                gv2 = p.c("t")
                for c in range(NC8):
                    p.t(lambda e, c=c, ft=ft, j=j, b=b, pu=pu: e.matmul(
                        pu, wueb[:, j, c, ft * 128:ft * 128 + 128],
                        xtokf[:, b, c, :], start=(c == 0),
                        stop=(c == NC8 - 1)))
                uv2 = p.c("t")
                sv = p.s(lambda e, pg=pg: e.activation(
                    scrB[:, 0:CAP], pg, AF.Gelu),
                    waits=[("t", gv2)])
                hv = p.v(lambda e, g=g, b=b, pu=pu: e.tensor_tensor(
                    heb[:, b, g, :], scrB[:, 0:CAP],
                    pu, AL.mult), waits=[("s", sv), ("t", uv2)])
                gbank_ev[gg] = hv
                last_hv[ex] = hv
            mm_tcnt[q] = p.c("t")

        DB = [ps_c[:, 0:256], ps_s[:, 1024:1280]]

        def dn(ex, j):
            b = ex % 2
            r = 4 * ex + j
            jj = r % 2
            for st in range(NST):
                gd = 2 * r + st
                pd = DB[gd % 2]
                w0 = []
                if st == 0:
                    w0 = wd_toks[r] + [("v", last_hv[ex])]
                if gd - 2 in dbank_ev:
                    w0 = w0 + [("v", dbank_ev[gd - 2])]
                for fc in range(16):
                    p.t(lambda e, fc=fc, st=st, jj=jj, b=b, pd=pd: e.matmul(
                        pd, heb[:, b, fc, st * 128:st * 128 + 128],
                        wdeb[:, jj, fc, :], start=(fc == 0), stop=(fc == 15)),
                        waits=(w0 if fc == 0 else []))
                mv = p.c("t")
                wv = [("t", mv)]
                if j == 0 and st == 0 and sc_toks:
                    wv = wv + sc_toks[-2:]
                evv = p.v(lambda e, st=st, j=j, ex=ex, pd=pd: e.tensor_scalar_mul(
                    eob[0:128, st, j * 256:j * 256 + 256], pd,
                    idw[0:128, ex, st, 2:3]), waits=wv)
                dbank_ev[gd] = evv
            dn_tcnt[r] = p.c("t")

        def sc(ex):
            vlast = p.c("v")
            for st in range(NST):
                wv = [("v", vlast)] + (sc_toks[-1:] if sc_toks else [])
                sc_toks.append(p.gdma(lambda e, ex=ex, st=st:
                    e.indirect_dma_start(
                        outap,
                        bass.IndirectOffsetOnAxis(iu32[0:128, ex, st:st + 1], 0),
                        eob[0:128, st, :], None, compute_op=AL.add),
                    waits=wv))

        gath(0)
        wl(0, 0)
        wl(0, 1)
        for ex in range(E):
            tstage(ex)
            mm(ex, 0)
            wl(ex, 2)
            mm(ex, 1)
            wl(ex, 3)
            wd(ex, 0)
            wd(ex, 1)
            mm(ex, 2)
            mm(ex, 3)
            wl(ex + 1, 0)
            wl(ex + 1, 1)
            dn(ex, 0)
            wd(ex, 2)
            dn(ex, 1)
            wd(ex, 3)
            dn(ex, 2)
            gath(ex + 1)
            dn(ex, 3)
            sc(ex)

    if 1 in blocks:
        qg_state["g"] = quant_gen()
    if 0 in blocks:
        block0_attn()
    if 1 in blocks:
        block1_ffn()
    if 2 in blocks:
        block2_ssm()
    if 3 in blocks:
        block3_moe_pre()

    def store_output():
        p.barrier()
        for tt in range(NTT):
            vv = 0
            for c in range(NC8):
                cc = c % 4
                trv = p.t(lambda e, tt=tt, c=c, cc=cc: e.transpose(
                    ps_s[:, cc * 512:cc * 512 + 128],
                    xres[:, c, tt * 128:tt * 128 + 128], IDENTF))
                vv = p.v(lambda e, c=c, cc=cc: e.tensor_copy(
                    tout[:, c, :], ps_s[:, cc * 512:cc * 512 + 128]),
                    waits=[("t", trv)])
                p.wait("t", [("v", vv)])
            ov = p.ydma(lambda e, tt=tt: e.dma_start(
                _ap(out.tensor, tt * 128 * D, [[D, 128], [1, D]]),
                tout[:].rearrange("p c d -> p (c d)")),
                waits=[("v", vv)])
            p.wait("v", [ov])
        p.wait("y", [("yd%d" % k, p.c("yd%d" % k)) for k in range(8)])

    store_output()
    if 3 in blocks:
        block3_moe_experts()
        snap = [(n, p.cnt[n]) for n in p.cnt if p.cnt[n] > 0]
        p.wait("y", snap)

    with nc.Block() as block:
        @block.tensor
        def _(e):
            for f in p.q["t"]:
                f(e)

        @block.vector
        def _(e):
            for f in p.q["v"]:
                f(e)

        @block.scalar
        def _(e):
            for f in p.q["s"]:
                f(e)

        @block.gpsimd
        def _(e):
            for f in p.q["g"]:
                f(e)

        @block.sync
        def _(e):
            for f in p.q["y"]:
                f(e)

    stack.close()
    return nc


def make_consts(T):
    import ml_dtypes
    ident = np.eye(128, dtype=np.float32)
    cbf = np.zeros((128, 640), np.float32)
    cbf[:, 0:128] = ident
    mdiag = np.zeros((128, 128), np.float32)
    iu = np.triu_indices(128, 1)
    mdiag[iu] = NEG
    cbf[:, 128:256] = mdiag
    cbf[:, 256:384] = NEG
    cbf[:, 384] = 1.0
    cbf[:, 512:640] = mdiag.T
    cf32 = np.zeros((128, 512), np.float32)
    cf32[0, 0:128] = 1.0
    cf32[:, 128:256] = ident
    cf32[:, 300] = 1e-5
    cf32[:, 301] = -16.0
    inv = 1.0 / (10000.0 ** (np.arange(0, DH, 2, dtype=np.float64) / DH))
    tt = np.arange(T, dtype=np.float64)
    fr = np.outer(inv, tt)
    cos = np.cos(fr)
    sin = np.sin(fr)
    trig = np.zeros((128, 2 * T), np.float32)
    for pp in range(128):
        i = (pp % 64) // 2
        trig[pp, 0:T] = cos[i]
        trig[pp, T:2 * T] = (-sin[i]) if pp % 2 == 0 else sin[i]
    return cbf.astype(ml_dtypes.bfloat16), cf32, trig.astype(ml_dtypes.bfloat16)


def kernel(**inputs):
    T = 2048
    B = 8
    x = np.asarray(inputs["L0_attn_x"])
    cbf, cf32, trig = make_consts(T)
    SG = T // 8
    cf32[:, 302] = SG * (np.arange(128) // 16)
    for p_ in range(128):
        cf32[p_, 304 + (p_ % 16)] = 1.0
    bias2 = np.zeros((128, 32), np.float32)
    bias2[:, 0] = np.asarray(inputs["L1_attn_B_b"])
    bias2[:, 1] = np.asarray(inputs["L1_attn_s_b2"])
    bias2[:, 2:6] = np.asarray(inputs["L1_attn_s_b1"]).reshape(4, 128).T
    bias2[:, 6:14] = np.asarray(inputs["L1_attn_C_b"]).reshape(8, 128).T
    bias2[:, 14:22] = np.asarray(inputs["L1_attn_D_b"]).reshape(8, 128).T
    bias2[:, 22:30] = np.asarray(inputs["L1_attn_g_b"]).reshape(8, 128).T
    nc = build_program(T=T, blocks=(0, 1, 2, 3))
    shared = {
        "wqkv": np.ascontiguousarray(inputs["L0_attn_w_qkv"]),
        "wout": np.ascontiguousarray(inputs["L0_attn_w_out"]),
        "wgate": np.ascontiguousarray(inputs["L0_ffn_w_gate"]),
        "wup": np.ascontiguousarray(inputs["L0_ffn_w_up"]),
        "wdown": np.ascontiguousarray(inputs["L0_ffn_w_down"]),
        "At": np.ascontiguousarray(np.asarray(inputs["L1_attn_A"]).T),
        "BwT": np.ascontiguousarray(np.asarray(inputs["L1_attn_B_w"]).T),
        "CwT": np.ascontiguousarray(np.asarray(inputs["L1_attn_C_w"]).T),
        "DwT": np.ascontiguousarray(np.asarray(inputs["L1_attn_D_w"]).T),
        "sw1T": np.ascontiguousarray(np.asarray(inputs["L1_attn_s_w1"]).T),
        "sw2T": np.ascontiguousarray(np.asarray(inputs["L1_attn_s_w2"]).T),
        "gwT": np.ascontiguousarray(np.asarray(inputs["L1_attn_g_w"]).T),
        "bias2": bias2,
        "gatewT": np.ascontiguousarray(np.asarray(inputs["L1_ffn_gate_w"]).T),
        "WgT": np.ascontiguousarray(np.asarray(inputs["L1_ffn_Wg"]).transpose(0, 2, 1)),
        "WuT": np.ascontiguousarray(np.asarray(inputs["L1_ffn_Wu"]).transpose(0, 2, 1)),
        "WdT": np.ascontiguousarray(np.asarray(inputs["L1_ffn_Wd"]).transpose(0, 2, 1)),
        "WsgT": np.ascontiguousarray(np.asarray(inputs["L1_ffn_Ws_g"]).T),
        "WsuT": np.ascontiguousarray(np.asarray(inputs["L1_ffn_Ws_u"]).T),
        "WsdT": np.ascontiguousarray(np.asarray(inputs["L1_ffn_Ws_d"]).T),
        "trig": trig, "cbf": cbf, "cf32": cf32,
    }
    in_maps = []
    for b in range(B):
        m = dict(shared)
        m["xT"] = np.ascontiguousarray(x[b].T)
        in_maps.append(m)
    res = run_bass_kernel_spmd(nc, in_maps, core_ids=list(range(B)))
    return np.stack([r["out"] for r in res.results], axis=0)


# version 12
# speedup vs baseline: 1.0073x; 1.0073x over previous
import numpy as np
from contextlib import ExitStack

import concourse.bass as bass
import concourse.mybir as mybir
from concourse.bass_utils import run_bass_kernel_spmd
from concourse import library_config as LC

F32 = mybir.dt.float32
BF16 = mybir.dt.bfloat16
I16 = mybir.dt.int16
I32 = mybir.dt.int32
AF = mybir.ActivationFunctionType
AL = mybir.AluOpType
AX = mybir.AxisListType

D = 1024
NC8 = 8
H = 16
DH = 64
FF0 = 2816
DS = 128
E = 16
FEXP = 2048
FSH = 2048
NEG = -1.0e30

SWAP_MASK = [(i ^ 1) for i in range(32)]


class P:

    def __init__(self, nc, stack):
        self.nc = nc
        self.stack = stack
        self.q = {k: [] for k in ("t", "v", "s", "g", "y")}
        self.sem = {}
        self.cnt = {}
        self.gd_rr = 0
        self.yd_rr = 0
        self.prev_cx = False

    def mksem(self, name):
        h = self.stack.enter_context(self.nc.semaphore(name))
        self.sem[name] = h
        self.cnt[name] = 0
        return h

    SELFWAIT = "v"

    def emit(self, eng, fn, waits=(), inc=None, by=1, cx=False):
        h = None
        swl = []
        if eng == "v" and inc == eng and self.cnt[eng] > 0 and (
                cx or self.prev_cx):
            swl = [(self.sem[eng], self.cnt[eng])]
        if eng == "v":
            self.prev_cx = cx
        if eng in P.SELFWAIT and inc == eng and self.cnt[eng] > 0:
            swl = [(self.sem[eng], self.cnt[eng])]
        if inc is not None:
            self.cnt[inc] += by
            h = self.sem[inc]
        wl = swl + [(self.sem[n], v) for (n, v) in waits if v > 0]

        def op(e, fn=fn, wl=wl, h=h, by=by):
            for (s, v) in wl:
                e.wait_ge(s, v)
            inst = fn(e)
            if h is not None:
                inst.then_inc(h, by)
        self.q[eng].append(op)
        return self.cnt[inc] if inc is not None else None

    def t(self, fn, waits=()):
        return self.emit("t", fn, waits, inc="t")

    def v(self, fn, waits=(), cx=False):
        return self.emit("v", fn, waits, inc="v", cx=cx)

    def s(self, fn, waits=()):
        return self.emit("s", fn, waits, inc="s")

    def g(self, fn, waits=()):
        return self.emit("g", fn, waits, inc="g")

    def gdma(self, fn, waits=()):
        k = self.gd_rr
        self.gd_rr = (k + 1) % 8
        name = "gd%d" % k
        prior = self.cnt[name]
        w = list(waits) + ([(name, prior)] if prior else [])
        self.emit("g", fn, w, inc=name, by=16)
        return (name, self.cnt[name])

    def ydma(self, fn, waits=()):
        k = self.yd_rr
        self.yd_rr = (k + 1) % 8
        name = "yd%d" % k
        prior = self.cnt[name]
        w = list(waits) + ([(name, prior)] if prior else [])
        self.emit("y", fn, w, inc=name, by=16)
        return (name, self.cnt[name])

    def wait(self, eng, waits):
        self.emit(eng, lambda e: None, waits, inc=None)

    def barrier(self):
        snap = [(n, self.cnt[n]) for n in self.cnt]
        for eng in ("t", "v", "s", "g", "y"):
            self.wait(eng, [(n, v) for (n, v) in snap if v > 0])

    def c(self, name):
        return self.cnt[name]


def _ap(t, offset, dims):
    return bass.AP(t, offset, dims)


def build_program(T=2048, blocks=(0, 1, 2, 3)):
    nc = bass.Bass("TRN2", target_bir_lowering=False)
    stack = ExitStack()
    NT = T // 512
    NTT = T // 128

    xT = nc.dram_tensor("xT", [D, T], F32, kind="ExternalInput").ap()
    out = nc.dram_tensor("out", [T, D], F32, kind="ExternalOutput").ap()
    wqkv = nc.dram_tensor("wqkv", [D, 3 * D], F32, kind="ExternalInput").ap()
    wout_d = nc.dram_tensor("wout", [D, D], F32, kind="ExternalInput").ap()
    trig_d = nc.dram_tensor("trig", [128, 2 * T], BF16, kind="ExternalInput").ap()
    cbf_d = nc.dram_tensor("cbf", [128, 640], BF16, kind="ExternalInput").ap()
    cf32_d = nc.dram_tensor("cf32", [128, 512], F32, kind="ExternalInput").ap()
    if 3 in blocks:
        gwT3_d = nc.dram_tensor("gatewT", [D, E], F32, kind="ExternalInput").ap()
        WgT_d = nc.dram_tensor("WgT", [E, D, FEXP], F32, kind="ExternalInput").ap()
        WuT_d = nc.dram_tensor("WuT", [E, D, FEXP], F32, kind="ExternalInput").ap()
        WdT_d = nc.dram_tensor("WdT", [E, FEXP, D], F32, kind="ExternalInput").ap()
        WsgT_d = nc.dram_tensor("WsgT", [D, FSH], F32, kind="ExternalInput").ap()
        WsuT_d = nc.dram_tensor("WsuT", [D, FSH], F32, kind="ExternalInput").ap()
        WsdT_d = nc.dram_tensor("WsdT", [FSH, D], F32, kind="ExternalInput").ap()
        xln_h = nc.dram_tensor("xln_h", [T, D], BF16).ap()
    if 2 in blocks:
        At_d = nc.dram_tensor("At", [DS, DS], F32, kind="ExternalInput").ap()
        BwT_d = nc.dram_tensor("BwT", [D, DS], F32, kind="ExternalInput").ap()
        CwT_d = nc.dram_tensor("CwT", [DS, D], F32, kind="ExternalInput").ap()
        DwT_d = nc.dram_tensor("DwT", [D, D], F32, kind="ExternalInput").ap()
        sw1T_d = nc.dram_tensor("sw1T", [D, 512], F32, kind="ExternalInput").ap()
        sw2T_d = nc.dram_tensor("sw2T", [512, DS], F32, kind="ExternalInput").ap()
        gwT_d = nc.dram_tensor("gwT", [D, D], F32, kind="ExternalInput").ap()
        bias2_d = nc.dram_tensor("bias2", [128, 32], F32, kind="ExternalInput").ap()
    if 1 in blocks:
        wgate_d = nc.dram_tensor("wgate", [FF0, D], F32, kind="ExternalInput").ap()
        wup_d = nc.dram_tensor("wup", [FF0, D], F32, kind="ExternalInput").ap()
        wdown_d = nc.dram_tensor("wdown", [D, FF0], F32, kind="ExternalInput").ap()
        wgq_h = nc.dram_tensor("wgq_h", [D, FF0], BF16).ap()
        wuq_h = nc.dram_tensor("wuq_h", [D, FF0], BF16).ap()
        wdq_h = nc.dram_tensor("wdq_h", [FF0, D], BF16).ap()

    BASE = (nc.sbuf_base + 31) // 32 * 32
    TOP = nc.sbuf_top
    cur = [BASE]
    offs = {}

    def at(name, shape, dt, offset=None):
        nbytes = int(np.prod(shape[1:])) * {F32: 4, BF16: 2, I32: 4, I16: 2}[dt]
        if offset is None:
            offset = cur[0]
            cur[0] = (offset + nbytes + 31) // 32 * 32
        assert offset + nbytes <= TOP, (name, offset + nbytes, TOP)
        offs[name] = offset
        return nc.alloc_sbuf_tensor_at(name, shape, dt, offset=offset)

    xres = at("xres", [128, NC8, T], F32)
    bufA = at("bufA", [128, NC8, T], BF16)
    bufB = at("bufB", [128, NC8, T], BF16)
    scrA = at("scrA", [128, 512], F32)
    scrB = at("scrB", [128, 512], F32)
    scrC = at("scrC", [128, 512], F32)
    cb = at("cb", [128, 640], BF16)
    cf = at("cf", [128, 512], F32)
    rows = at("rows", [1, 1536], F32)
    lbuf = at("lbuf", [128, 8], F32)
    rbuf = at("rbuf", [128, 8], F32)
    sqs = at("sqs", [128, 512], BF16)
    PH = cur[0]

    trg = at("trg", [128, 2, T], BF16, PH)
    qb = at("qb", [128, 2, T], BF16, PH + 8192)
    kb = at("kb", [128, 2, T], BF16, PH + 16384)
    vtm = at("vtm", [128, 2, NTT, 130], BF16, PH + 24576)
    pts = at("pts", [128, 2, 512], BF16, PH + 32960)
    oacc = at("oacc", [128, 512], F32, PH + 35008)
    wpair = at("wpair", [128, 2, 3, NC8, 128], BF16, PH + 37056)
    wobuf = at("wobuf", [128, NC8, 512], BF16, PH)
    tout = at("tout", [128, NC8, 128], F32, offs["bufB"])

    if 1 in blocks:
        qwin = at("qwin", [128, 2, 512], F32, PH + 49344)
        qsA = at("qsA", [128, 512], F32, PH + 53440)
        qsB = at("qsB", [128, 512], F32, PH + 55488)
        qsg = at("qsg", [128, 512], BF16, PH + 57536)
        qbm = at("qbm", [128, 5, 32], F32, PH + 58560)
        qiA = at("qiA", [128, 512], BF16, PH + 59200)
        qiB = at("qiB", [128, 512], BF16, PH + 60224)
        qow = at("qow", [128, 2, 512], BF16, PH + 61248)
        qstage = at("qstage", [128, 2, 512], BF16, (TOP - 2048) // 32 * 32)
        ysb = at("ysb", [128, 512], F32, PH)
        qyb = at("qyb", [128, 512], BF16, PH + 2048)
        sgbuf = at("sgbuf", [128, 512], F32, PH + 3072)
        hstage = at("hstage", [128, 512], F32, PH + 5120)
        wgbuf = at("wgbuf", [128, NC8, 512], BF16, PH + 7168)
        wubuf = at("wubuf", [128, NC8, 512], BF16, PH + 15360)
        wdbuf = at("wdbuf", [128, 2, 22, 128], BF16, PH + 23552)
        hfm_off = offs["bufA"]
        hfm = at("hfm", [128, 22, 512], BF16, hfm_off)
    if 2 in blocks:
        NCH = T // 64
        a1b = at("a1b", [128, 4, T], BF16, PH)
        ssb = at("ssb", [128, T], F32, PH + 8 * T)
        usb = at("usb", [128, T], F32, PH + 12 * T)
        hsf = at("hsf", [128, T], BF16, PH + 16 * T)
        sw1b = at("sw1b", [128, NC8, 512], BF16, PH + 18 * T)
        sw2b = at("sw2b", [128, 4, 128], BF16, PH + 18 * T + 8192)
        bwb = at("bwb", [128, NC8, 128], BF16, PH + 18 * T + 9216)
        cwb = at("cwb", [128, NC8, 128], BF16, PH + 18 * T + 11264)
        atb = at("atb", [128, 128], BF16, PH + 18 * T + 13312)
        a64b = at("a64b", [128, 128], BF16, PH + 18 * T + 13568)
        apw = at("apw", [128, 128], BF16, PH + 18 * T + 13824)
        b2b = at("b2b", [128, 32], F32, PH + 18 * T + 14080)
        cdb = at("cdb", [128, 8], F32, PH + 18 * T + 14208)
        w2s = at("w2s", [128, 2, 2, NC8, 128], BF16, PH + 18 * T + 14336)
        hstart = at("hstart", [128, NCH + 1], BF16, PH + 18 * T + 22528)
        sc2A = at("sc2A", [128, 512], F32, PH + 18 * T + 22656)
        sc2B = at("sc2B", [128, 512], F32, PH + 18 * T + 24704)
    if 3 in blocks:
        CAP = T // 8
        SG = T // 8
        NST3 = max(CAP // 128, 1)
        o3 = [PH]

        def nxt(nbytes):
            r = o3[0]
            o3[0] = (r + nbytes + 31) // 32 * 32
            return r
        wsgb = at("wsgb", [128, 2, NC8, 256], BF16, nxt(8192))
        wsub = at("wsub", [128, 2, NC8, 256], BF16, nxt(8192))
        wsdb = at("wsdb", [128, 2, 16, 128], BF16, nxt(8192))
        hshb = at("hshb", [128, 16, 512], BF16, nxt(16384))
        gwT3b = at("gwT3b", [128, NC8, 16], BF16, nxt(256))
        sctm = at("sctm", [128, T // 128, 16], F32, nxt((T // 128) * 64))
        slottm = at("slottm", [128, T // 128, 16], F32, nxt((T // 128) * 64))
        iotab = at("iotab", [128, 256], F32, nxt(1024))
        ttrow = at("ttrow", [128, T // 128], F32, nxt((T // 128) * 4))
        pcol = at("pcol", [128, 1], F32, nxt(32))
        rhs3 = at("rhs3", [128, T // 128, E, 3], BF16, nxt((T // 128) * 96))
        pbt = at("pbt", [128, 256], BF16, nxt(512))
        semb = at("semb", [16, T], F32, offs["bufA"])
        junkb = at("junkb", [16, T], F32, offs["bufA"] + 4 * T)
        mskb = at("mskb", [16, T], F32, offs["bufA"] + 8 * T)
        iu32 = at("iu32", [128, E, 2], I32, (TOP - 2048) // 32 * 32)
        idw = at("idw", [128, E, 2, 3], F32, (TOP - 4096) // 32 * 32)
        idc = at("idc", [128, E, 2], F32, (TOP - 5120) // 32 * 32)
        smls = at("smls", [16, 16], F32, (TOP - 5632) // 32 * 32)
        ob = [PH]

        def nxb(nbytes):
            r = ob[0]
            ob[0] = (r + nbytes + 31) // 32 * 32
            return r
        xtokt = at("xtokt", [128, 2, NST3, D], BF16, nxb(2 * NST3 * D * 2))
        xtokf = at("xtokf", [128, 2, NC8, CAP], BF16, nxb(2 * NC8 * CAP * 2))
        heb = at("heb", [128, 2, FEXP // 128, CAP], BF16, offs["bufA"])
        wdeb = at("wdeb", [128, 2, FEXP // 128, 256], BF16, nxb(2 * 16 * 256 * 2))
        eob = at("eob", [128, NST3, D], F32, nxb(NST3 * D * 4))
        wgeb = at("wgeb", [128, 2, NC8, 512], BF16, offs["bufB"])
        wueb = at("wueb", [128, 2, NC8, 512], BF16, offs["bufB"] + 16384)
    IDENT = cb[:, 0:128]
    MDIAG = cb[:, 128:256]
    MFULL = cb[:, 256:384]
    MDIAGT = cb[:, 512:640]
    ONESB = cb[:, 384:385]
    ONEROW = cf[0:1, 0:128]
    IDENTF = cf[:, 128:256]
    EPSAP = cf[0:1, 300:301]
    MBIAS = cf[:, 301:302]
    r_mm = rows[0:1, 0:512]
    r_v = rows[0:1, 512:1024]
    r_t1 = rows[0:1, 1024:1536]

    ps_s = stack.enter_context(nc.psum_tensor("ps_s", [128, 2048], F32))
    ps_a = stack.enter_context(nc.psum_tensor("ps_a", [128, 512], F32))
    ps_b = stack.enter_context(nc.psum_tensor("ps_b", [128, 512], F32))
    ps_c = stack.enter_context(nc.psum_tensor("ps_c", [128, 512], F32))
    ps_t = stack.enter_context(nc.psum_tensor("ps_t", [128, 1024], BF16))

    p = P(nc, stack)
    for name in ("t", "v", "s", "g"):
        p.mksem(name)
    for k in range(8):
        p.mksem("gd%d" % k)
        p.mksem("yd%d" % k)

    xsrc = _ap(xT.tensor, 0, [[T, 128], [128 * T, NC8], [1, T]])
    l1 = p.ydma(lambda e: e.dma_start(xres[:], xsrc))
    l2 = p.ydma(lambda e: e.dma_start(cb[:], cbf_d))
    l3 = p.ydma(lambda e: e.dma_start(cf[:], cf32_d))
    for eng in ("t", "v", "s", "g"):
        p.wait(eng, [l1, l2, l3])

    def layer_norm(dst, quant=False):
        for tc in range(NT):
            t0 = tc * 512
            for c in range(NC8):
                vv = p.v(lambda e, c=c, t0=t0: e.tensor_copy(
                    bufA[:, c, t0:t0 + 512], xres[:, c, t0:t0 + 512]))
                sv = p.s(lambda e, c=c, t0=t0: e.activation(
                    sqs[:], xres[:, c, t0:t0 + 512], AF.Square))
                p.t(lambda e, c=c, t0=t0: e.matmul(
                    ps_c[0:1, :], ONESB, bufA[:, c, t0:t0 + 512],
                    start=(c == 0), stop=(c == NC8 - 1)), waits=[("v", vv)])
                sv2 = p.t(lambda e, c=c: e.matmul(
                    ps_b[0:1, :], ONESB, sqs[:],
                    start=(c == 0), stop=(c == NC8 - 1)), waits=[("s", sv)])
                p.wait("s", [("t", sv2)])
            mmv = p.c("t")
            p.v(lambda e: e.tensor_scalar_mul(r_mm, ps_c[0:1, :], 1.0 / D),
                waits=[("t", mmv)])
            p.v(lambda e: e.tensor_scalar_mul(r_v, ps_b[0:1, :], 1.0 / D))
            p.v(lambda e: e.tensor_tensor(r_t1, r_mm, r_mm, AL.mult))
            vv2 = p.v(lambda e: e.tensor_tensor(r_v, r_v, r_t1, AL.subtract))
            sv2 = p.s(lambda e: e.activation(r_v, r_v, AF.Sqrt, bias=EPSAP),
                      waits=[("v", vv2)])
            p.v(lambda e: e.reciprocal(r_t1, r_v), waits=[("s", sv2)])
            vv3 = p.v(lambda e: e.scalar_tensor_tensor(
                r_v, r_mm, -1.0, r_t1, AL.mult, AL.mult))
            p.t(lambda e: e.matmul(ps_a[:], ONEROW, r_t1, start=True, stop=True),
                waits=[("v", vv3)])
            tb2 = p.t(lambda e: e.matmul(ps_b[:], ONEROW, r_v,
                                         start=True, stop=True))
            lastv = 0
            for c in range(NC8):
                p.v(lambda e, c=c, t0=t0: e.tensor_tensor(
                    scrA[:], xres[:, c, t0:t0 + 512], ps_a[:], AL.mult),
                    waits=[("t", tb2)])
                if not quant:
                    lastv = p.v(lambda e, c=c, t0=t0: e.tensor_tensor(
                        dst[:, c, t0:t0 + 512], scrA[:], ps_b[:], AL.add))
                else:
                    lastv = p.v(lambda e, c=c, t0=t0: e.tensor_tensor(
                        scrB[:], scrA[:], ps_b[:], AL.add))
                    fp4q_y(scrB[:], dst[:, c, t0:t0 + 512])
                    lastv = p.c("v")
            p.wait("t", [("v", lastv)])
        return p.c("v")

    def fp4q(src, dst, W, waits0=(), waitsd=()):
        NB = W // 16
        bm = qbm[:, 0, 0:NB]
        sc = qbm[:, 1, 0:NB]
        rr = qbm[:, 2, 0:NB]
        r2 = qbm[:, 3, 0:NB]
        hs = qbm[:, 4, 0:NB]
        qt_ = qbm.ap().tensor
        s3 = _ap(src.tensor, src.offset, [list(src.ap[0]), [16, NB], [1, 16]])
        p.v(lambda e: e.tensor_reduce(bm, s3, AX.X, AL.max,
                                      apply_absolute_value=True),
            waits=list(waits0), cx=True)
        p.v(lambda e: e.tensor_scalar(sc, bm, 1.0 / 6.0, 1e-12,
                                      AL.mult, AL.max))
        p.v(lambda e: e.reciprocal(rr, sc))
        p.v(lambda e: e.tensor_scalar(r2, rr, 2.0, None, AL.mult))
        r2v = p.v(lambda e: e.tensor_scalar(hs, sc, 0.5, None, AL.mult))
        r2b = _ap(qt_, 3 * 32, [[160, 128], [1, NB], [0, 16]])
        hsb = _ap(qt_, 4 * 32, [[160, 128], [1, NB], [0, 16]])
        qsA_t = qsA.ap().tensor
        xnv = p.v(lambda e: e.tensor_tensor(
            _ap(qsA_t, 0, [[512, 128], [16, NB], [1, 16]]), s3, r2b, AL.mult),
            waits=[("s", p.c("s"))])
        sgv = p.s(lambda e: e.activation(qsg[:, 0:W], qsA[:, 0:W], AF.Sign),
                  waits=[("v", xnv)])
        av = p.v(lambda e: e.scalar_tensor_tensor(
            qsB[:, 0:W], qsA[:, 0:W], -1.0, qsA[:, 0:W], AL.mult, AL.max))
        p.v(lambda e: e.tensor_scalar(qiB[:, 0:W], qsB[:, 0:W], 5.0, None,
                                      AL.is_gt))
        for th in (7.0, 10.0, 10.0):
            p.v(lambda e, th=th: e.scalar_tensor_tensor(
                qiB[:, 0:W], qsB[:, 0:W], th, qiB[:, 0:W], AL.is_gt, AL.add))
        p.v(lambda e: e.tensor_scalar(qiA[:, 0:W], qsB[:, 0:W], 0.5, None,
                                      AL.is_gt))
        for th in (1.5, 2.5, 3.5):
            p.v(lambda e, th=th: e.scalar_tensor_tensor(
                qiA[:, 0:W], qsB[:, 0:W], th, qiA[:, 0:W], AL.is_gt, AL.add))
        cmb = p.v(lambda e: e.scalar_tensor_tensor(
            qsA[:, 0:W], qiB[:, 0:W], 2.0, qiA[:, 0:W], AL.mult, AL.add),
            waits=[("s", sgv)])
        p.v(lambda e: e.tensor_tensor(qsB[:, 0:W], qsA[:, 0:W],
                                      qsg[:, 0:W], AL.mult))
        d3 = _ap(dst.tensor, dst.offset, [list(dst.ap[0]), [16, NB], [1, 16]])
        return p.v(lambda e: e.tensor_tensor(
            d3, _ap(qsB.ap().tensor, 0, [[512, 128], [16, NB], [1, 16]]),
            hsb, AL.mult), waits=list(waitsd))

    def fp4q_y(chunk_src, chunk_dst):
        p.v(lambda e: e.transpose(ysb[:], chunk_src), cx=True)
        NB2 = 16
        bm = qbm[:, 0, 0:32]
        sc = qbm[:, 1, 0:32]
        rr = qbm[:, 2, 0:32]
        r2 = qbm[:, 3, 0:32]
        hs = qbm[:, 4, 0:32]
        qt_ = qbm.ap().tensor
        ys_t = ysb.ap().tensor
        y4 = _ap(ys_t, 0, [[512, 128], [32, NB2], [16, 2], [1, 16]])
        p.v(lambda e: e.tensor_reduce(
            _ap(qt_, 0, [[160, 128], [2, NB2], [1, 2]]), y4, AX.X, AL.max,
            apply_absolute_value=True), cx=True)
        p.v(lambda e: e.tensor_scalar(sc, bm, 1.0 / 6.0, 1e-12, AL.mult, AL.max))
        p.v(lambda e: e.reciprocal(rr, sc))
        p.v(lambda e: e.tensor_scalar(r2, rr, 2.0, None, AL.mult))
        p.v(lambda e: e.tensor_scalar(hs, sc, 0.5, None, AL.mult))
        r2b = _ap(qt_, 3 * 32, [[160, 128], [2, NB2], [1, 2], [0, 16]])
        hsb = _ap(qt_, 4 * 32, [[160, 128], [2, NB2], [1, 2], [0, 16]])
        W = 512
        qsA_t = qsA.ap().tensor
        xnv = p.v(lambda e: e.tensor_tensor(
            qsA[:, 0:W], ysb[:], r2b, AL.mult), waits=[("s", p.c("s"))])
        sgv = p.s(lambda e: e.activation(qsg[:, 0:W], qsA[:, 0:W], AF.Sign),
                  waits=[("v", xnv)])
        av = p.v(lambda e: e.scalar_tensor_tensor(
            qsB[:, 0:W], qsA[:, 0:W], -1.0, qsA[:, 0:W], AL.mult, AL.max))
        p.v(lambda e: e.tensor_scalar(qiB[:, 0:W], qsB[:, 0:W], 5.0, None,
                                      AL.is_gt))
        for th in (7.0, 10.0, 10.0):
            p.v(lambda e, th=th: e.scalar_tensor_tensor(
                qiB[:, 0:W], qsB[:, 0:W], th, qiB[:, 0:W], AL.is_gt, AL.add))
        p.v(lambda e: e.tensor_scalar(qiA[:, 0:W], qsB[:, 0:W], 0.5, None,
                                      AL.is_gt))
        for th in (1.5, 2.5, 3.5):
            p.v(lambda e, th=th: e.scalar_tensor_tensor(
                qiA[:, 0:W], qsB[:, 0:W], th, qiA[:, 0:W], AL.is_gt, AL.add))
        cmb = p.v(lambda e: e.scalar_tensor_tensor(
            qsA[:, 0:W], qiB[:, 0:W], 2.0, qiA[:, 0:W], AL.mult, AL.add),
            waits=[("s", sgv)])
        p.v(lambda e: e.tensor_tensor(qsB[:, 0:W], qsA[:, 0:W],
                                      qsg[:, 0:W], AL.mult))
        p.v(lambda e: e.tensor_tensor(qyb[:], qsB[:, 0:W], hsb, AL.mult))
        return p.v(lambda e: e.transpose(chunk_dst, qyb[:]), cx=True)


    def qkv_dma(hp, buf):
        wl = []
        tw = [("t", p.c("t"))]
        for j, base in enumerate((128 * hp, D + 128 * hp, 2 * D + 128 * hp)):
            src = _ap(wqkv.tensor, base,
                      [[3 * D, 128], [128 * 3 * D, NC8], [1, 128]])
            wl.append(p.gdma(lambda e, j=j, src=src, buf=buf: e.dma_start(
                wpair[:, buf, j, :, :], src), waits=tw))
        return wl

    def block0_attn():
        lnv = layer_norm(bufB)
        p.wait("g", [("v", lnv)])
        NG = T // 512
        vtm_t = vtm.ap().tensor
        psc_t = ps_c.ap().tensor
        tgv = p.ydma(lambda e: e.dma_start(
            trg[:].rearrange("p a t -> p (a t)"),
            _ap(trig_d.tensor, 0, [[2 * T, 128], [T, 2], [1, T]])))
        dv = qkv_dma(0, 0)
        rope_v = {-1: lnv}
        sq_end = {}
        sc_t_end = {}
        exp_s = {}
        sc_par = 0
        fin_prev = 0
        bc_prev = 0
        for hp in range(8):
            par = hp % 2
            buf = hp % 2
            guard_t = sc_t_end.get(hp - 2, 0)
            qrv = 0
            krv = 0
            vtmv_prev = rope_v[hp - 1]
            first = True
            for tc in range(NT):
                t0 = tc * 512
                w0 = [("v", vtmv_prev)]
                if first:
                    w0 = list(dv) + w0
                    first = False
                for c in range(NC8):
                    p.t(lambda e, c=c, buf=buf, t0=t0: e.matmul(
                        ps_a[:], wpair[:, buf, 0, c, :],
                        bufB[:, c, t0:t0 + 512],
                        start=(c == 0), stop=(c == NC8 - 1)),
                        waits=(w0 if c == 0 else []))
                qmv = p.c("t")
                for c in range(NC8):
                    p.t(lambda e, c=c, buf=buf, t0=t0: e.matmul(
                        ps_b[:], wpair[:, buf, 1, c, :],
                        bufB[:, c, t0:t0 + 512],
                        start=(c == 0), stop=(c == NC8 - 1)))
                kmv = p.c("t")
                for tt2 in range(4):
                    tt = 4 * tc + tt2
                    for c in range(NC8):
                        p.t(lambda e, c=c, buf=buf, tt=tt, tt2=tt2: e.matmul(
                            ps_c[:, tt2 * 128:tt2 * 128 + 128],
                            bufB[:, c, tt * 128:tt * 128 + 128],
                            wpair[:, buf, 2, c, :],
                            start=(c == 0), stop=(c == NC8 - 1)))
                vmv = p.c("t")
                for (pspk, dstb, wv_) in ((ps_a, qb, qmv), (ps_b, kb, kmv)):
                    shw = [("t", wv_)]
                    if tc == 0 and pspk is ps_a:
                        shw += [("t", guard_t), tgv]
                    shv = p.v(lambda e, pspk=pspk: e.stream_shuffle(
                        scrC[:], pspk[:], SWAP_MASK), waits=shw, cx=True)
                    p.v(lambda e, t0=t0: e.tensor_tensor(
                        scrB[:], scrC[:], trg[:, 1, t0:t0 + 512], AL.mult))
                    p.v(lambda e, pspk=pspk, t0=t0: e.tensor_tensor(
                        scrA[:], pspk[:], trg[:, 0, t0:t0 + 512], AL.mult))
                    p.v(lambda e, dstb=dstb, t0=t0, par=par: e.tensor_tensor(
                        dstb[:, par, t0:t0 + 512], scrA[:], scrB[:], AL.add))
                p.v(lambda e, tc=tc, par=par: e.tensor_copy(
                    _ap(vtm_t, par * NTT * 130 + 4 * tc * 130,
                        [[2 * NTT * 130, 128], [130, 4], [65, 2], [1, 64]]),
                    _ap(psc_t, 0, [[512, 128], [128, 4], [64, 2], [1, 64]])),
                    waits=[("t", vmv)])
                vtmv_prev = p.c("v")
            p.v(lambda e, par=par: e.memset(
                _ap(vtm_t, par * NTT * 130 + 64,
                    [[2 * NTT * 130, 128], [130, NTT], [65, 2], [1, 1]]), 1.0))
            rope_v[hp] = p.c("v")
            pump(4)
            if hp < 7:
                dv = qkv_dma(hp + 1, (hp + 1) % 2)
            first_sc = True
            for h01 in range(2):
                base = 64 * h01
                for qg in range(NG):
                    ntile = 4 * qg + 4
                    pend = []
                    for t in range(ntile):
                        pr = sc_par % 2
                        psc = ps_s[:, pr * 512:pr * 512 + 512]
                        w0 = []
                        if first_sc:
                            w0.append(("v", rope_v[hp]))
                            first_sc = False
                        if sc_par - 2 in exp_s:
                            w0.append(("s", exp_s[sc_par - 2]))
                        if t == 0:
                            w0.append(("v", fin_prev))
                        band = t - 4 * qg
                        p.t(lambda e, t=t, qg=qg, base=base, par=par, psc=psc:
                            e.matmul(
                                psc, kb[base:base + 64, par,
                                        t * 128:t * 128 + 128],
                                qb[base:base + 64, par,
                                   qg * 512:qg * 512 + 512],
                                start=True, stop=True), waits=w0)
                        if band >= 0:
                            for jj in range(band):
                                p.t(lambda e, jj=jj, psc=psc: e.matmul(
                                    psc[:, jj * 128:jj * 128 + 128], IDENT,
                                    MFULL, start=False, stop=True,
                                    skip_group_check=True))
                            p.t(lambda e, band=band, psc=psc: e.matmul(
                                psc[:, band * 128:band * 128 + 128], IDENT,
                                MDIAGT, start=False, stop=True,
                                skip_group_check=True))
                        ex = p.s(lambda e, pr=pr, psc=psc:
                                 e.activation(
                                     pts[:, pr, :], psc, AF.Exp, scale=0.125,
                                     bias=MBIAS),
                                 waits=[("t", p.c("t"))])
                        exp_s[sc_par] = ex
                        pend.append((t, pr, ex))
                        if len(pend) > 1:
                            t_, pr_, ex_ = pend.pop(0)
                            p.t(lambda e, t=t_, par=par, h01=h01, pr=pr_:
                                e.matmul(
                                    ps_s[0:65, 1024:1536],
                                    vtm[:, par, t, 65 * h01:65 * h01 + 65],
                                    pts[:, pr, :],
                                    start=(t == 0), stop=False),
                                waits=[("s", ex_)])
                        sc_par += 1
                    t_, pr_, ex_ = pend.pop(0)
                    p.t(lambda e, t=t_, par=par, h01=h01, pr=pr_, nt=ntile:
                        e.matmul(
                            ps_s[0:65, 1024:1536],
                            vtm[:, par, t, 65 * h01:65 * h01 + 65],
                            pts[:, pr, :],
                            start=(t == 0), stop=True),
                        waits=[("s", ex_)])
                    pvend = p.c("t")
                    rv0 = p.s(lambda e: e.activation(
                        r_v, ps_s[64:65, 1024:1536], AF.Ln),
                        waits=[("t", pvend)])
                    rv = p.s(lambda e: e.activation(
                        r_t1, r_v, AF.Exp, scale=-1.0))
                    oev = p.v(lambda e: e.tensor_copy(
                        oacc[0:64, :], ps_s[0:64, 1024:1536]),
                        waits=[("t", pvend)])
                    bc_prev = p.t(lambda e: e.matmul(
                        ps_s[0:64, 1536:2048], cf[0:1, 0:64], r_t1,
                        start=True, stop=True), waits=[("s", rv)])
                    fin_prev = p.v(lambda e, h01=h01, hp=hp, qg=qg:
                        e.tensor_tensor(
                            bufA[64 * h01:64 * h01 + 64, hp,
                                 qg * 512:qg * 512 + 512],
                            oacc[0:64, :], ps_s[0:64, 1536:2048],
                            AL.mult), waits=[("t", bc_prev)])
                    pump(4)
            sc_t_end[hp] = p.c("t")
        p.barrier()
        for mh in range(2):
            src = _ap(wout_d.tensor, mh * 512,
                      [[D, 128], [128 * D, NC8], [1, 512]])
            wv = p.gdma(lambda e, src=src: e.dma_start(wobuf[:], src),
                        waits=[("t", p.c("t"))])
            p.wait("t", [wv])
            for mt in range(4):
                m0c = (mh * 512 + mt * 128) // 128
                for tc in range(NT):
                    t0 = tc * 512
                    for c in range(NC8):
                        p.t(lambda e, c=c, mt=mt, t0=t0: e.matmul(
                            ps_a[:], wobuf[:, c, mt * 128:mt * 128 + 128],
                            bufA[:, c, t0:t0 + 512],
                            start=(c == 0), stop=(c == NC8 - 1)))
                    omv = p.c("t")
                    rv = p.v(lambda e, m0c=m0c, t0=t0: e.tensor_tensor(
                        xres[:, m0c, t0:t0 + 512],
                        xres[:, m0c, t0:t0 + 512], ps_a[:], AL.add),
                        waits=[("t", omv)])
                    p.wait("t", [("v", rv)])

    def quant_gen():
        units = []
        for (wsrc, wdst, RS, LD) in ((wgate_d, wgq_h, D, FF0),
                                     (wup_d, wuq_h, D, FF0)):
            for rc in range(FF0 // 128):
                for scc in range(D // 512):
                    units.append((wsrc, wdst, RS, LD, rc, scc, 512))
        for rc in range(NC8):
            for scc in range((FF0 + 511) // 512):
                kw = min(512, FF0 - scc * 512)
                units.append((wdown_d, wdq_h, FF0, D, rc, scc, kw))
        N = len(units)
        ldtok = {}
        qvdone = {}
        trt = {}
        evq = {}
        sttok = {}

        def load(i):
            if i >= N:
                return
            wsrc, wdst, RS, LD, rc, scc, kw = units[i]
            par = i % 2
            ldtok[i] = p.ydma(lambda e, wsrc=wsrc, RS=RS, rc=rc, scc=scc,
                              kw=kw, par=par: e.dma_start(
                qwin[:, par, 0:kw],
                _ap(wsrc.tensor, rc * 128 * RS + scc * 512,
                    [[RS, 128], [1, kw]])),
                waits=[("v", qvdone.get(i - 2, 0))])
        load(0)
        load(1)
        for i in range(N):
            wsrc, wdst, RS, LD, rc, scc, kw = units[i]
            par = i % 2
            qvdone[i] = fp4q(
                qwin[:, par, 0:kw],
                _ap(qow.ap().tensor, par * 512, [[1024, 128], [1, kw]]),
                kw, waits0=[ldtok[i]],
                waitsd=[("t", trt.get(i - 2, 0))])
            load(i + 1)
            yield
            for j in range(kw // 128):
                w0 = []
                if j == 0:
                    w0 = [("v", qvdone[i]), ("v", evq.get(i - 1, 0))]
                p.t(lambda e, j=j, par=par: e.transpose(
                    ps_t[:, j * 128:j * 128 + 128],
                    qow[:, par, j * 128:j * 128 + 128], IDENT), waits=w0)
            trt[i] = p.c("t")
            evq[i] = p.v(lambda e, par=par, kw=kw: e.tensor_copy(
                qstage[:, par, 0:kw], ps_t[:, 0:kw]),
                waits=[("t", trt[i])] + ([sttok[i - 2]] if i - 2 in sttok
                                         else []))
            sttok[i] = p.ydma(lambda e, wdst=wdst, LD=LD, rc=rc, scc=scc,
                              kw=kw, par=par: e.dma_start(
                _ap(wdst.tensor, scc * 512 * LD + rc * 128,
                    [[LD, 128], [128 * LD, kw // 128], [1, 128]]),
                _ap(qstage.ap().tensor, par * 512,
                    [[1024, 128], [128, kw // 128], [1, 128]])),
                waits=[("v", evq[i])])
            yield

    qg_state = {"g": None}

    def pump(n=1):
        g = qg_state["g"]
        if g is None:
            return
        for _ in range(n):
            try:
                next(g)
            except StopIteration:
                qg_state["g"] = None
                return

    def block1_ffn():
        pump(10 ** 9)
        p.barrier()
        layer_norm(bufB, quant=True)
        p.barrier()
        FCW = [512] * 5 + [256]
        b1_dtok = {}
        b1_dcnt = {}
        b1_rv = {}
        for tc in range(NT):
            t0 = tc * 512
            f0 = 0
            for fc, fw in enumerate(FCW):
                dwg = p.gdma(lambda e, f0=f0, fw=fw: e.dma_start(
                    wgbuf[:, :, 0:fw],
                    _ap(wgq_h.tensor, f0,
                        [[FF0, 128], [128 * FF0, NC8], [1, fw]])),
                    waits=[("t", p.c("t"))])
                dwu = p.gdma(lambda e, f0=f0, fw=fw: e.dma_start(
                    wubuf[:, :, 0:fw],
                    _ap(wuq_h.tensor, f0,
                        [[FF0, 128], [128 * FF0, NC8], [1, fw]])),
                    waits=[("t", p.c("t"))])
                p.wait("t", [dwg, dwu])
                for tt2 in range(4):
                    tt = 4 * tc + tt2
                    for c in range(NC8):
                        p.t(lambda e, c=c, tt=tt, fw=fw: e.matmul(
                            ps_a[:, 0:fw], bufB[:, c, tt * 128:tt * 128 + 128],
                            wgbuf[:, c, 0:fw],
                            start=(c == 0), stop=(c == NC8 - 1)))
                    gv2 = p.c("t")
                    for c in range(NC8):
                        p.t(lambda e, c=c, tt=tt, fw=fw: e.matmul(
                            ps_b[:, 0:fw], bufB[:, c, tt * 128:tt * 128 + 128],
                            wubuf[:, c, 0:fw],
                            start=(c == 0), stop=(c == NC8 - 1)))
                    uv2 = p.c("t")
                    sv = p.s(lambda e, fw=fw: e.activation(
                        sgbuf[:, 0:fw], ps_a[:, 0:fw], AF.Silu),
                        waits=[("t", gv2)])
                    hv = p.v(lambda e, fw=fw: e.tensor_tensor(
                        hstage[:, 0:fw], sgbuf[:, 0:fw], ps_b[:, 0:fw],
                        AL.mult), waits=[("s", sv), ("t", uv2)])
                    p.wait("t", [("v", hv)])
                    fp4q(hstage[:, 0:fw], qow[:, 0, 0:fw], fw)
                    qv = p.c("v")
                    p.wait("t", [("v", qv)])
                    for j in range(fw // 128):
                        p.t(lambda e, j=j: e.transpose(
                            ps_t[:, j * 128:j * 128 + 128],
                            qow[:, 0, j * 128:j * 128 + 128], IDENT))
                    trv = p.c("t")
                    ev = p.v(lambda e, fc=fc, fw=fw, tt2=tt2: e.tensor_copy(
                        _ap(hfm.ap().tensor, (4 * fc) * 512 + tt2 * 128,
                            [[22 * 512, 128], [512, fw // 128], [1, 128]]),
                        ps_t[:, 0:fw]), waits=[("t", trv)])
                    p.wait("t", [("v", ev)])
                f0 += fw
            def dload(dt, tc=tc):
                if dt >= NC8:
                    return
                gd = tc * NC8 + dt
                b1_dtok[gd] = p.gdma(lambda e, dt=dt, gd=gd: e.dma_start(
                    wdbuf[:, gd % 2, :, :],
                    _ap(wdq_h.tensor, dt * 128,
                        [[D, 128], [128 * D, 22], [1, 128]])),
                    waits=[("t", b1_dcnt.get(gd - 2, 0))])
            dload(0)
            dload(1)
            for dt in range(NC8):
                gd = tc * NC8 + dt
                pbk = ps_a[:] if gd % 2 == 0 else ps_b[:]
                w0 = [b1_dtok[gd]]
                if gd - 2 in b1_rv:
                    w0.append(("v", b1_rv[gd - 2]))
                for fc2 in range(22):
                    p.t(lambda e, fc2=fc2, gd=gd, pbk=pbk: e.matmul(
                        pbk, wdbuf[:, gd % 2, fc2, :], hfm[:, fc2, :],
                        start=(fc2 == 0), stop=(fc2 == 21)),
                        waits=(w0 if fc2 == 0 else []))
                b1_dcnt[gd] = omv = p.c("t")
                dload(dt + 2)
                b1_rv[gd] = p.v(lambda e, dt=dt, t0=t0, pbk=pbk: e.tensor_tensor(
                    xres[:, dt, t0:t0 + 512],
                    xres[:, dt, t0:t0 + 512], pbk, AL.add),
                    waits=[("t", omv)])
            p.wait("t", [("v", b1_rv[tc * NC8 + NC8 - 1]),
                         ("v", b1_rv[tc * NC8 + NC8 - 2])])

    def block2_ssm():
        p.barrier()
        NCH = T // 64
        wl = []
        wl.append(p.gdma(lambda e: e.dma_start(
            sw1b[:], _ap(sw1T_d.tensor, 0,
                         [[512, 128], [128 * 512, NC8], [1, 512]]))))
        wl.append(p.gdma(lambda e: e.dma_start(
            sw2b[:], _ap(sw2T_d.tensor, 0,
                         [[DS, 128], [128 * DS, 4], [1, 128]]))))
        wl.append(p.gdma(lambda e: e.dma_start(
            bwb[:], _ap(BwT_d.tensor, 0,
                        [[DS, 128], [128 * DS, NC8], [1, 128]]))))
        wl.append(p.gdma(lambda e: e.dma_start(
            cwb[:], _ap(CwT_d.tensor, 0,
                        [[D, 128], [128, NC8], [1, 128]]))))
        wl.append(p.gdma(lambda e: e.dma_start(atb[:], At_d)))
        wl.append(p.ydma(lambda e: e.dma_start(b2b[:], bias2_d)))
        lnv = layer_norm(bufB)
        p.wait("t", wl + [("v", lnv)])
        p.wait("v", wl[-1:])
        cdv = p.v(lambda e: e.tensor_tensor(
            cdb[:], b2b[:, 6:14], b2b[:, 14:22], AL.add))
        for tc in range(NT):
            t0 = tc * 512
            for mt in range(4):
                for c in range(NC8):
                    p.t(lambda e, c=c, mt=mt, t0=t0: e.matmul(
                        ps_a[:], sw1b[:, c, mt * 128:mt * 128 + 128],
                        bufB[:, c, t0:t0 + 512],
                        start=(c == 0), stop=(c == NC8 - 1)))
                mv = p.c("t")
                sv = p.s(lambda e, mt=mt, t0=t0: e.activation(
                    a1b[:, mt, t0:t0 + 512], ps_a[:], AF.Silu,
                    bias=b2b[:, 2 + mt:3 + mt]), waits=[("t", mv)])
                p.wait("t", [("s", sv)])
            for c4 in range(4):
                p.t(lambda e, c4=c4, t0=t0: e.matmul(
                    ps_b[:], sw2b[:, c4, :], a1b[:, c4, t0:t0 + 512],
                    start=(c4 == 0), stop=(c4 == 3)))
            mv = p.c("t")
            sv = p.s(lambda e, t0=t0: e.activation(
                ssb[:, t0:t0 + 512], ps_b[:], AF.Sigmoid,
                bias=b2b[:, 1:2]), waits=[("t", mv)])
            p.wait("t", [("s", sv)])
            for c in range(NC8):
                p.t(lambda e, c=c, t0=t0: e.matmul(
                    ps_c[:], bwb[:, c, :], bufB[:, c, t0:t0 + 512],
                    start=(c == 0), stop=(c == NC8 - 1)))
            mv = p.c("t")
            uv = p.v(lambda e, t0=t0: e.scalar_tensor_tensor(
                usb[:, t0:t0 + 512], ps_c[:], b2b[:, 0:1],
                ssb[:, t0:t0 + 512], AL.add, AL.mult),
                waits=[("t", mv), ("s", sv)])
            p.wait("t", [("v", uv)])
        p.v(lambda e: e.tensor_copy(a64b[:], atb[:]))
        for it in range(6):
            tv = p.t(lambda e: e.transpose(ps_t[:, 0:128], a64b[:], IDENT),
                     waits=[("v", p.c("v"))])
            ev = p.v(lambda e: e.tensor_copy(apw[:], ps_t[:, 0:128]),
                     waits=[("t", tv)])
            mv = p.t(lambda e: e.matmul(ps_c[:, 0:128], apw[:], a64b[:],
                                        start=True, stop=True),
                     waits=[("v", ev)])
            ev2 = p.v(lambda e: e.tensor_copy(a64b[:], ps_c[:, 0:128]),
                      waits=[("t", mv)])
            p.wait("t", [("v", ev2)])
        zv = p.v(lambda e: e.memset(hstart[:], 0.0))
        p.wait("t", [("v", zv)])
        for tl in range(64):
            mv = p.t(lambda e: e.matmul(ps_c[:, 0:NCH], atb[:],
                                        hstart[:, 0:NCH],
                                        start=True, stop=True))
            uap = _ap(usb.ap().tensor, tl, [[T, 128], [64, NCH]])
            av = p.v(lambda e, uap=uap: e.tensor_tensor(
                hstart[:, 0:NCH], ps_c[:, 0:NCH], uap, AL.add),
                waits=[("t", mv)])
            p.wait("t", [("v", av)])
        bcv = p.v(lambda e: e.tensor_copy(a1b[:, 0, 0:NCH], hstart[:, 0:NCH]))
        zv = p.v(lambda e: e.memset(hstart[:], 0.0))
        p.wait("t", [("v", p.c("v"))])
        for c in range(NCH - 1):
            mv = p.t(lambda e, c=c: e.matmul(
                ps_c[:, 0:1], a64b[:], hstart[:, c:c + 1],
                start=True, stop=True))
            av = p.v(lambda e, c=c: e.tensor_tensor(
                hstart[:, c + 1:c + 2], ps_c[:, 0:1], a1b[:, 0, c:c + 1],
                AL.add), waits=[("t", mv)])
            p.wait("t", [("v", av)])
        for tl in range(64):
            if tl == 0:
                rhs = hstart[:, 0:NCH]
            else:
                rhs = _ap(hsf.ap().tensor, tl - 1, [[T, 128], [64, NCH]])
            mv = p.t(lambda e, rhs=rhs: e.matmul(
                ps_c[:, 0:NCH], atb[:], rhs, start=True, stop=True))
            uap = _ap(usb.ap().tensor, tl, [[T, 128], [64, NCH]])
            oap = _ap(hsf.ap().tensor, tl, [[T, 128], [64, NCH]])
            av = p.v(lambda e, uap=uap, oap=oap: e.tensor_tensor(
                oap, ps_c[:, 0:NCH], uap, AL.add), waits=[("t", mv)])
            p.wait("t", [("v", av)])
        scan_av = p.c("v")
        s2tok = {}
        s2cnt = {}
        y_rv = {}
        y_dv2 = {}

        def w2load(dt):
            if dt >= NC8:
                return
            j = dt % 2
            wt = [("t", s2cnt.get(dt - 2, 0))]
            wv1 = p.gdma(lambda e, dt=dt, j=j: e.dma_start(
                w2s[:, j, 0, :, :],
                _ap(DwT_d.tensor, dt * 128,
                    [[D, 128], [128 * D, NC8], [1, 128]])), waits=wt)
            wv2 = p.gdma(lambda e, dt=dt, j=j: e.dma_start(
                w2s[:, j, 1, :, :],
                _ap(gwT_d.tensor, dt * 128,
                    [[D, 128], [128 * D, NC8], [1, 128]])), waits=wt)
            s2tok[dt] = [wv1, wv2]
        w2load(0)
        w2load(1)
        YPB = [(ps_a[:, 0:512], ps_b[:, 0:512], scrA[:], scrB[:]),
               (ps_c[:, 0:512], ps_s[:, 0:512], sc2A[:], sc2B[:])]
        it = 0
        for dt in range(NC8):
            j = dt % 2
            for tc in range(NT):
                t0 = tc * 512
                pa, pb, sA, sB = YPB[it % 2]
                w0 = []
                if tc == 0:
                    w0 += s2tok[dt]
                if it < 2:
                    w0.append(("v", scan_av))
                if it - 2 in y_rv:
                    w0.append(("v", y_rv[it - 2]))
                p.t(lambda e, dt=dt, t0=t0, pa=pa: e.matmul(
                    pa, cwb[:, dt, :], hsf[:, t0:t0 + 512],
                    start=True, stop=False), waits=w0)
                for c in range(NC8):
                    p.t(lambda e, c=c, t0=t0, j=j, pa=pa: e.matmul(
                        pa, w2s[:, j, 0, c, :], bufB[:, c, t0:t0 + 512],
                        start=False, stop=(c == NC8 - 1)))
                for c in range(NC8):
                    p.t(lambda e, c=c, t0=t0, j=j, pb=pb: e.matmul(
                        pb, w2s[:, j, 1, c, :], bufB[:, c, t0:t0 + 512],
                        start=(c == 0), stop=(c == NC8 - 1)))
                mv = p.c("t")
                sw = [("t", mv)]
                if it - 2 in y_dv2:
                    sw.append(("v", y_dv2[it - 2]))
                sgv = p.s(lambda e, dt=dt, pb=pb, sB=sB: e.activation(
                    sB, pb, AF.Sigmoid,
                    bias=b2b[:, 22 + dt:23 + dt]), waits=sw)
                y_dv2[it] = p.v(lambda e, dt=dt, pa=pa, sA=sA, sB=sB:
                    e.scalar_tensor_tensor(
                        sA, pa, cdb[:, dt:dt + 1], sB,
                        AL.add, AL.mult), waits=[("s", sgv), ("v", cdv)])
                y_rv[it] = p.v(lambda e, dt=dt, t0=t0, sA=sA: e.tensor_tensor(
                    xres[:, dt, t0:t0 + 512], xres[:, dt, t0:t0 + 512],
                    sA, AL.add))
                it += 1
            s2cnt[dt] = p.c("t")
            w2load(dt + 2)
        p.wait("t", [("v", y_rv[it - 1]), ("v", y_rv[it - 2])])


    def block3_moe_pre():
        p.barrier()
        CAP = T // 8
        SG = T // 8
        lnv = layer_norm(bufB)
        gw = p.gdma(lambda e: e.dma_start(
            gwT3b[:], _ap(gwT3_d.tensor, 0, [[E, 128], [128 * E, NC8], [1, E]])))
        xst = hshb[:, 0:2, :].rearrange("p a b -> p (a b)")
        for tt in range(NTT):
            p.wait("t", [("v", p.c("v"))])
            for c in range(NC8):
                p.t(lambda e, tt=tt, c=c: e.transpose(
                    ps_t[:, c * 128:c * 128 + 128],
                    bufB[:, c, tt * 128:tt * 128 + 128], IDENT))
            trv = p.c("t")
            ev = p.v(lambda e: e.tensor_copy(xst, ps_t[:]), waits=[("t", trv)])
            dv2 = p.ydma(lambda e, tt=tt: e.dma_start(
                _ap(xln_h.tensor, tt * 128 * D, [[D, 128], [1, D]]), xst),
                waits=[("v", ev)])
            p.wait("v", [dv2])
        p.wait("t", [gw])
        for tt in range(NTT):
            for c in range(NC8):
                p.t(lambda e, tt=tt, c=c: e.matmul(
                    ps_c[:, 0:E], bufB[:, c, tt * 128:tt * 128 + 128],
                    gwT3b[:, c, :], start=(c == 0), stop=(c == NC8 - 1)))
            mv = p.c("t")
            p.v(lambda e: e.reduce_max(rbuf[:, 0:1], ps_c[:, 0:E], AX.X,
                                       negate=True), waits=[("t", mv)],
                cx=True)
            vv = p.c("v")
            sv = p.s(lambda e, tt=tt: e.activation(
                sctm[:, tt, :], ps_c[:, 0:E], AF.Exp, bias=rbuf[:, 0:1],
                accum_out=lbuf[:, 0:1]), waits=[("v", vv), ("t", mv)])
            p.v(lambda e: e.reciprocal(rbuf[:, 1:2], lbuf[:, 0:1]),
                waits=[("s", sv)])
            scv = p.v(lambda e, tt=tt: e.tensor_scalar_mul(
                sctm[:, tt, :], sctm[:, tt, :], rbuf[:, 1:2]))
            p.wait("t", [("v", scv)])
        for tg in range(NTT // 4):
            for j in range(4):
                tt = tg * 4 + j
                p.t(lambda e, tt=tt, j=j: e.transpose(
                    ps_b[0:16, j * 128:j * 128 + 128],
                    sctm[:, tt, :], IDENTF))
            trv = p.c("t")
            ev = p.v(lambda e, tg=tg: e.tensor_copy(
                semb[:, tg * 512:tg * 512 + 512], ps_b[0:16, :]),
                waits=[("t", trv)])
            p.wait("t", [("v", ev)])
        sh_tok = {}
        sh_cnt = {}
        sd_tok = {}
        sd_cnt = {}
        chain = {}
        NQ = FSH // 256

        def shload(gq):
            if gq >= NT * NQ:
                return
            qq = gq % NQ
            j = gq % 2
            wt = [("t", sh_cnt.get(gq - 2, 0))]
            w1 = p.gdma(lambda e, qq=qq, j=j: e.dma_start(
                wsgb[:, j, :, :], _ap(WsgT_d.tensor, qq * 256,
                    [[FSH, 128], [128 * FSH, NC8], [1, 256]])), waits=wt)
            w2 = p.gdma(lambda e, qq=qq, j=j: e.dma_start(
                wsub[:, j, :, :], _ap(WsuT_d.tensor, qq * 256,
                    [[FSH, 128], [128 * FSH, NC8], [1, 256]])), waits=wt)
            sh_tok[gq] = [w1, w2]

        def sdload(gd):
            if gd >= NT * NC8:
                return
            dt = gd % NC8
            j = gd % 2
            sd_tok[gd] = [p.gdma(lambda e, dt=dt, j=j: e.dma_start(
                wsdb[:, j, :, :], _ap(WsdT_d.tensor, dt * 128,
                    [[D, 128], [128 * D, 16], [1, 128]])),
                waits=[("t", sd_cnt.get(gd - 2, 0))])]
        shload(0)
        shload(1)
        sdload(0)
        sdload(1)
        SPB = [(ps_a[:, 0:512], ps_b[:, 0:512], scrB[:]),
               (ps_c[:, 0:512], ps_s[:, 0:512], scrC[:])]
        gch = 0
        for tc in range(NT):
            t0 = tc * 512
            last_hv = 0
            for qq in range(NQ):
                gq = tc * NQ + qq
                j = gq % 2
                for ft in range(2):
                    pg, pu, sB = SPB[gch % 2]
                    w0 = []
                    if ft == 0:
                        w0 += sh_tok[gq]
                    if gch - 2 in chain:
                        w0.append(("v", chain[gch - 2]))
                    for c in range(NC8):
                        p.t(lambda e, c=c, ft=ft, j=j, t0=t0, pg=pg: e.matmul(
                            pg, wsgb[:, j, c, ft * 128:ft * 128 + 128],
                            bufB[:, c, t0:t0 + 512],
                            start=(c == 0), stop=(c == NC8 - 1)),
                            waits=(w0 if c == 0 else []))
                    gv2 = p.c("t")
                    for c in range(NC8):
                        p.t(lambda e, c=c, ft=ft, j=j, t0=t0, pu=pu: e.matmul(
                            pu, wsub[:, j, c, ft * 128:ft * 128 + 128],
                            bufB[:, c, t0:t0 + 512],
                            start=(c == 0), stop=(c == NC8 - 1)))
                    uv2 = p.c("t")
                    sw = [("t", gv2)]
                    if gch - 2 in chain:
                        sw.append(("v", chain[gch - 2]))
                    sv = p.s(lambda e, pg=pg, sB=sB: e.activation(
                        sB, pg, AF.Gelu), waits=sw)
                    hv = p.v(lambda e, qq=qq, ft=ft, sB=sB, pu=pu:
                        e.tensor_tensor(
                            hshb[:, qq * 2 + ft, :], sB, pu, AL.mult),
                        waits=[("s", sv), ("t", uv2)])
                    chain[gch] = hv
                    last_hv = hv
                    gch += 1
                sh_cnt[gq] = p.c("t")
                shload(gq + 2)
            for dt in range(NC8):
                gd = tc * NC8 + dt
                pg, pu, sB = SPB[gch % 2]
                w0 = list(sd_tok[gd])
                if dt == 0:
                    w0.append(("v", last_hv))
                if gch - 2 in chain:
                    w0.append(("v", chain[gch - 2]))
                for fc in range(16):
                    p.t(lambda e, fc=fc, gd=gd, pg=pg: e.matmul(
                        pg, wsdb[:, gd % 2, fc, :], hshb[:, fc, :],
                        start=(fc == 0), stop=(fc == 15)),
                        waits=(w0 if fc == 0 else []))
                mv = p.c("t")
                rv = p.v(lambda e, dt=dt, t0=t0, pg=pg: e.tensor_tensor(
                    xres[:, dt, t0:t0 + 512], xres[:, dt, t0:t0 + 512],
                    pg, AL.add), waits=[("t", mv)])
                chain[gch] = rv
                gch += 1
                sd_cnt[gd] = p.c("t")
                sdload(gd + 2)
        p.wait("t", [("v", chain[gch - 1]), ("v", chain[gch - 2])])
        zv = p.v(lambda e: e.memset(smls[:], 0.0))
        lo_col = 0
        hk = 0.5
        for it in range(44):
            lo = smls[0:16, lo_col:lo_col + 1]
            lo2 = smls[0:16, 4 - lo_col:5 - lo_col]
            p.v(lambda e, lo=lo, hk=hk: e.tensor_scalar(
                smls[0:16, 3:4], lo, hk, None, AL.add))
            p.v(lambda e: e.tensor_scalar(
                junkb[:], semb[:], smls[0:16, 3:4], None, AL.is_ge, AL.add,
                accum_out=smls[0:16, 1:2]), cx=True)
            p.v(lambda e: e.tensor_scalar(
                smls[0:16, 2:3], smls[0:16, 1:2], CAP - 0.5, None, AL.is_ge))
            p.v(lambda e, lo=lo, lo2=lo2, hk=hk: e.scalar_tensor_tensor(
                lo2, smls[0:16, 2:3], hk, lo, AL.mult, AL.add))
            lo_col = 4 - lo_col
            hk *= 0.5
        LO = smls[0:16, lo_col:lo_col + 1]
        p.v(lambda e: e.tensor_scalar(mskb[:], semb[:], LO, None, AL.is_ge))
        zv = p.v(lambda e: e.memset(junkb[:], 0.0))
        p.v(lambda e: e.tensor_tensor_scan(
            semb[:], junkb[:], mskb[:], 0.0, AL.max, AL.add), cx=True)
        p.v(lambda e: e.tensor_tensor(mskb[:], mskb[:], semb[:], AL.mult))
        p.v(lambda e: e.tensor_scalar(
            junkb[:], mskb[:], CAP + 0.5, -30000.0, AL.is_gt, AL.mult))
        sl2 = p.v(lambda e: e.scalar_tensor_tensor(
            mskb[:], mskb[:], -1.0, junkb[:], AL.add, AL.add))
        p.wait("t", [("v", sl2)])
        for tg in range(NTT // 4):
            for j in range(4):
                tt = tg * 4 + j
                p.t(lambda e, tt=tt, j=j: e.transpose(
                    ps_c[:, j * 16:j * 16 + 16],
                    mskb[:, tt * 128:tt * 128 + 128], IDENTF[0:16, 0:16]))
            trv = p.c("t")
            ev = p.v(lambda e, tg=tg: e.tensor_copy(
                _ap(slottm.ap().tensor, tg * 4 * 16,
                    [[(T // 128) * 16, 128], [1, 64]]),
                ps_c[:, 0:64]), waits=[("t", trv)])
            p.wait("t", [("v", ev)])
        g1 = p.g(lambda e: e.iota(iotab[:], [[1, 256]], base=0,
                                  channel_multiplier=0,
                                  allow_small_or_imprecise_dtypes=True))
        g2 = p.g(lambda e: e.iota(ttrow[:], [[1, T // 128]], base=0,
                                  channel_multiplier=0,
                                  allow_small_or_imprecise_dtypes=True))
        g3 = p.g(lambda e: e.iota(pcol[:], [[0, 1]], base=0,
                                  channel_multiplier=1,
                                  allow_small_or_imprecise_dtypes=True))
        NTTl = T // 128
        r3t = rhs3.ap().tensor
        p.v(lambda e: e.tensor_copy(
            _ap(r3t, 0, [[NTTl * E * 3, 128], [E * 3, NTTl], [3, E]]),
            _ap(pcol.ap().tensor, 0, [[1, 128], [0, NTTl], [0, E]])),
            waits=[("g", g3)])
        p.v(lambda e: e.tensor_copy(
            _ap(r3t, 1, [[NTTl * E * 3, 128], [E * 3, NTTl], [3, E]]),
            _ap(ttrow.ap().tensor, 0, [[NTTl, 128], [1, NTTl], [0, E]])),
            waits=[("g", g2)])
        r3v = p.v(lambda e: e.tensor_copy(
            _ap(r3t, 2, [[NTTl * E * 3, 128], [E * 3, NTTl], [3, E]]),
            sctm[:].rearrange("p a b -> p (a b)")))
        NH = max(CAP // 128, 1)
        p.v(lambda e: e.memset(idw[:], 0.0))
        p.v(lambda e: e.memset(idc[:], 0.0))
        for ex in range(E):
            for tt in range(NTTl):
                pv2 = p.v(lambda e, ex=ex, tt=tt: e.tensor_scalar(
                    pbt[:, 0:CAP], iotab[:, 0:CAP],
                    slottm[:, tt, ex:ex + 1], None, AL.is_equal),
                    waits=[("g", g1)])
                p.wait("t", [("v", pv2)])
                for hf in range(NH):
                    tgt = ps_c if hf == 0 else ps_b
                    p.t(lambda e, ex=ex, tt=tt, hf=hf, tgt=tgt: e.matmul(
                        tgt[0:min(CAP, 128), 0:3],
                        pbt[:, hf * 128:hf * 128 + min(CAP, 128)],
                        rhs3[:, tt, ex, :],
                        start=(tt == 0), stop=(tt == NTTl - 1)))
                mmv = p.c("t")
                if tt < NTTl - 1:
                    p.wait("v", [("t", mmv)])
            mmv = p.c("t")
            for hf in range(NH):
                tgt = ps_c if hf == 0 else ps_b
                p.v(lambda e, ex=ex, hf=hf, tgt=tgt: e.tensor_copy(
                    idw[0:min(CAP, 128), ex, hf, :], tgt[0:min(CAP, 128), 0:3]),
                    waits=[("t", mmv)])
            p.wait("t", [("v", p.c("v"))])
        idwt = idw.ap().tensor
        idcv = p.v(lambda e: e.scalar_tensor_tensor(
            idc[:].rearrange("p a b -> p (a b)"),
            _ap(idwt, 1, [[E * 6, 128], [3, E * 2]]), 128.0,
            _ap(idwt, 0, [[E * 6, 128], [3, E * 2]]), AL.mult, AL.add))
        uv32 = p.v(lambda e: e.tensor_copy(
            iu32[:].rearrange("p a b -> p (a b)"),
            idc[:].rearrange("p a b -> p (a b)")), waits=[("v", idcv)])
        if CAP < 128:
            uv32 = p.v(lambda e: e.tensor_copy(
                iu32[64:128, :, :].rearrange("p a b -> p (a b)"),
                _ap(pcol.ap().tensor, 64 * 1, [[1, 64], [0, E * 2]])))

    def block3_moe_experts():
        p.barrier()
        CAP = T // 8
        NST = CAP // 128
        assert CAP == 256 and NST == 2
        xlnap = _ap(xln_h.tensor, 0, [[D, T], [1, D]])
        outap = _ap(out.tensor, 0, [[D, T], [1, D]])
        IU32V = p.c("v")
        gath_toks = {}
        tfree = {}
        mm_tcnt = {}
        dn_tcnt = {}
        wl_toks = {}
        wd_toks = {}
        gbank_ev = {}
        dbank_ev = {}
        sc_toks = []
        last_hv = {}
        last_ev = {}

        def gath(ex):
            if ex >= E:
                return
            toks = []
            b = ex % 2
            for st in range(NST):
                toks.append(p.gdma(lambda e, ex=ex, st=st, b=b:
                    e.indirect_dma_start(
                        xtokt[:, b, st, :], None, xlnap,
                        bass.IndirectOffsetOnAxis(iu32[0:128, ex, st:st + 1], 0)),
                    waits=[("v", IU32V), ("t", tfree.get(ex - 2, 0))]))
            gath_toks[ex] = toks

        def wl(ex, i):
            if ex >= E:
                return
            q = 4 * ex + i
            j = q % 2
            wt = [("t", mm_tcnt.get(q - 2, 0))]
            t1 = p.gdma(lambda e, ex=ex, i=i, j=j: e.dma_start(
                wgeb[:, j, :, :],
                _ap(WgT_d.tensor, ex * D * FEXP + i * 512,
                    [[FEXP, 128], [128 * FEXP, NC8], [1, 512]])), waits=wt)
            t2 = p.gdma(lambda e, ex=ex, i=i, j=j: e.dma_start(
                wueb[:, j, :, :],
                _ap(WuT_d.tensor, ex * D * FEXP + i * 512,
                    [[FEXP, 128], [128 * FEXP, NC8], [1, 512]])), waits=wt)
            wl_toks[q] = [t1, t2]

        def wd(ex, j):
            if ex >= E:
                return
            r = 4 * ex + j
            jj = r % 2
            wd_toks[r] = [p.gdma(lambda e, ex=ex, j=j, jj=jj: e.dma_start(
                wdeb[:, jj, :, :],
                _ap(WdT_d.tensor, ex * FEXP * D + j * 256,
                    [[D, 128], [128 * D, 16], [1, 256]])),
                waits=[("t", dn_tcnt.get(r - 2, 0))])]

        def tstage(ex):
            b = ex % 2
            ev = 0
            for st in range(NST):
                if st == 0:
                    w = list(gath_toks[ex]) + [("v", last_ev.get(ex - 1, 0))]
                else:
                    w = [("v", ev)]
                for c in range(NC8):
                    p.t(lambda e, st=st, c=c, b=b: e.transpose(
                        ps_t[:, c * 128:c * 128 + 128],
                        xtokt[:, b, st, c * 128:c * 128 + 128], IDENT),
                        waits=(w if c == 0 else []))
                trv = p.c("t")
                ev = p.v(lambda e, st=st, b=b: e.tensor_copy(
                    _ap(xtokf.ap().tensor, b * (NC8 * CAP) + st * 128,
                        [[2 * NC8 * CAP, 128], [CAP, NC8], [1, 128]]),
                    _ap(ps_t.ap().tensor, 0,
                        [[1024, 128], [128, NC8], [1, 128]])),
                    waits=[("t", trv)])
            tfree[ex] = p.c("t")
            last_ev[ex] = ev

        GB = [(ps_a[:, 0:CAP], ps_b[:, 0:CAP]),
              (ps_s[:, 0:CAP], ps_s[:, 512:512 + CAP])]

        def mm(ex, i):
            b = ex % 2
            q = 4 * ex + i
            j = q % 2
            for ft in range(4):
                g = 4 * i + ft
                gg = 16 * ex + g
                pg, pu = GB[gg % 2]
                w0 = []
                if ft == 0:
                    w0 = wl_toks[q] + [("v", last_ev[ex])]
                if gg - 2 in gbank_ev:
                    w0 = w0 + [("v", gbank_ev[gg - 2])]
                for c in range(NC8):
                    p.t(lambda e, c=c, ft=ft, j=j, b=b, pg=pg: e.matmul(
                        pg, wgeb[:, j, c, ft * 128:ft * 128 + 128],
                        xtokf[:, b, c, :], start=(c == 0),
                        stop=(c == NC8 - 1)), waits=(w0 if c == 0 else []))
                gv2 = p.c("t")
                for c in range(NC8):
                    p.t(lambda e, c=c, ft=ft, j=j, b=b, pu=pu: e.matmul(
                        pu, wueb[:, j, c, ft * 128:ft * 128 + 128],
                        xtokf[:, b, c, :], start=(c == 0),
                        stop=(c == NC8 - 1)))
                uv2 = p.c("t")
                sv = p.s(lambda e, pg=pg: e.activation(
                    scrB[:, 0:CAP], pg, AF.Gelu),
                    waits=[("t", gv2)])
                hv = p.v(lambda e, g=g, b=b, pu=pu: e.tensor_tensor(
                    heb[:, b, g, :], scrB[:, 0:CAP],
                    pu, AL.mult), waits=[("s", sv), ("t", uv2)])
                gbank_ev[gg] = hv
                last_hv[ex] = hv
            mm_tcnt[q] = p.c("t")

        DB = [ps_c[:, 0:256], ps_s[:, 1024:1280]]

        def dn(ex, j):
            b = ex % 2
            r = 4 * ex + j
            jj = r % 2
            for st in range(NST):
                gd = 2 * r + st
                pd = DB[gd % 2]
                w0 = []
                if st == 0:
                    w0 = wd_toks[r] + [("v", last_hv[ex])]
                if gd - 2 in dbank_ev:
                    w0 = w0 + [("v", dbank_ev[gd - 2])]
                for fc in range(16):
                    p.t(lambda e, fc=fc, st=st, jj=jj, b=b, pd=pd: e.matmul(
                        pd, heb[:, b, fc, st * 128:st * 128 + 128],
                        wdeb[:, jj, fc, :], start=(fc == 0), stop=(fc == 15)),
                        waits=(w0 if fc == 0 else []))
                mv = p.c("t")
                wv = [("t", mv)]
                if j == 0 and st == 0 and sc_toks:
                    wv = wv + sc_toks[-2:]
                evv = p.v(lambda e, st=st, j=j, ex=ex, pd=pd: e.tensor_scalar_mul(
                    eob[0:128, st, j * 256:j * 256 + 256], pd,
                    idw[0:128, ex, st, 2:3]), waits=wv)
                dbank_ev[gd] = evv
            dn_tcnt[r] = p.c("t")

        def sc(ex):
            vlast = p.c("v")
            for st in range(NST):
                wv = [("v", vlast)] + (sc_toks[-1:] if sc_toks else [])
                sc_toks.append(p.gdma(lambda e, ex=ex, st=st:
                    e.indirect_dma_start(
                        outap,
                        bass.IndirectOffsetOnAxis(iu32[0:128, ex, st:st + 1], 0),
                        eob[0:128, st, :], None, compute_op=AL.add),
                    waits=wv))

        gath(0)
        wl(0, 0)
        wl(0, 1)
        for ex in range(E):
            tstage(ex)
            mm(ex, 0)
            wl(ex, 2)
            mm(ex, 1)
            wl(ex, 3)
            wd(ex, 0)
            wd(ex, 1)
            mm(ex, 2)
            mm(ex, 3)
            wl(ex + 1, 0)
            wl(ex + 1, 1)
            dn(ex, 0)
            wd(ex, 2)
            dn(ex, 1)
            wd(ex, 3)
            dn(ex, 2)
            gath(ex + 1)
            dn(ex, 3)
            sc(ex)

    if 1 in blocks:
        qg_state["g"] = quant_gen()
    if 0 in blocks:
        block0_attn()
    if 1 in blocks:
        block1_ffn()
    if 2 in blocks:
        block2_ssm()
    if 3 in blocks:
        block3_moe_pre()

    def store_output():
        p.barrier()
        for tt in range(NTT):
            vv = 0
            for c in range(NC8):
                cc = c % 4
                trv = p.t(lambda e, tt=tt, c=c, cc=cc: e.transpose(
                    ps_s[:, cc * 512:cc * 512 + 128],
                    xres[:, c, tt * 128:tt * 128 + 128], IDENTF))
                vv = p.v(lambda e, c=c, cc=cc: e.tensor_copy(
                    tout[:, c, :], ps_s[:, cc * 512:cc * 512 + 128]),
                    waits=[("t", trv)])
                p.wait("t", [("v", vv)])
            ov = p.ydma(lambda e, tt=tt: e.dma_start(
                _ap(out.tensor, tt * 128 * D, [[D, 128], [1, D]]),
                tout[:].rearrange("p c d -> p (c d)")),
                waits=[("v", vv)])
            p.wait("v", [ov])
        p.wait("y", [("yd%d" % k, p.c("yd%d" % k)) for k in range(8)])

    store_output()
    if 3 in blocks:
        block3_moe_experts()
        snap = [(n, p.cnt[n]) for n in p.cnt if p.cnt[n] > 0]
        p.wait("y", snap)

    with nc.Block() as block:
        @block.tensor
        def _(e):
            for f in p.q["t"]:
                f(e)

        @block.vector
        def _(e):
            for f in p.q["v"]:
                f(e)

        @block.scalar
        def _(e):
            for f in p.q["s"]:
                f(e)

        @block.gpsimd
        def _(e):
            for f in p.q["g"]:
                f(e)

        @block.sync
        def _(e):
            for f in p.q["y"]:
                f(e)

    stack.close()
    return nc


def make_consts(T):
    import ml_dtypes
    ident = np.eye(128, dtype=np.float32)
    cbf = np.zeros((128, 640), np.float32)
    cbf[:, 0:128] = ident
    mdiag = np.zeros((128, 128), np.float32)
    iu = np.triu_indices(128, 1)
    mdiag[iu] = NEG
    cbf[:, 128:256] = mdiag
    cbf[:, 256:384] = NEG
    cbf[:, 384] = 1.0
    cbf[:, 512:640] = mdiag.T
    cf32 = np.zeros((128, 512), np.float32)
    cf32[0, 0:128] = 1.0
    cf32[:, 128:256] = ident
    cf32[:, 300] = 1e-5
    cf32[:, 301] = -16.0
    inv = 1.0 / (10000.0 ** (np.arange(0, DH, 2, dtype=np.float64) / DH))
    tt = np.arange(T, dtype=np.float64)
    fr = np.outer(inv, tt)
    cos = np.cos(fr)
    sin = np.sin(fr)
    trig = np.zeros((128, 2 * T), np.float32)
    for pp in range(128):
        i = (pp % 64) // 2
        trig[pp, 0:T] = cos[i]
        trig[pp, T:2 * T] = (-sin[i]) if pp % 2 == 0 else sin[i]
    return cbf.astype(ml_dtypes.bfloat16), cf32, trig.astype(ml_dtypes.bfloat16)


def kernel(**inputs):
    T = 2048
    B = 8
    x = np.asarray(inputs["L0_attn_x"])
    cbf, cf32, trig = make_consts(T)
    SG = T // 8
    cf32[:, 302] = SG * (np.arange(128) // 16)
    for p_ in range(128):
        cf32[p_, 304 + (p_ % 16)] = 1.0
    bias2 = np.zeros((128, 32), np.float32)
    bias2[:, 0] = np.asarray(inputs["L1_attn_B_b"])
    bias2[:, 1] = np.asarray(inputs["L1_attn_s_b2"])
    bias2[:, 2:6] = np.asarray(inputs["L1_attn_s_b1"]).reshape(4, 128).T
    bias2[:, 6:14] = np.asarray(inputs["L1_attn_C_b"]).reshape(8, 128).T
    bias2[:, 14:22] = np.asarray(inputs["L1_attn_D_b"]).reshape(8, 128).T
    bias2[:, 22:30] = np.asarray(inputs["L1_attn_g_b"]).reshape(8, 128).T
    nc = build_program(T=T, blocks=(0, 1, 2, 3))
    shared = {
        "wqkv": np.ascontiguousarray(inputs["L0_attn_w_qkv"]),
        "wout": np.ascontiguousarray(inputs["L0_attn_w_out"]),
        "wgate": np.ascontiguousarray(inputs["L0_ffn_w_gate"]),
        "wup": np.ascontiguousarray(inputs["L0_ffn_w_up"]),
        "wdown": np.ascontiguousarray(inputs["L0_ffn_w_down"]),
        "At": np.ascontiguousarray(np.asarray(inputs["L1_attn_A"]).T),
        "BwT": np.ascontiguousarray(np.asarray(inputs["L1_attn_B_w"]).T),
        "CwT": np.ascontiguousarray(np.asarray(inputs["L1_attn_C_w"]).T),
        "DwT": np.ascontiguousarray(np.asarray(inputs["L1_attn_D_w"]).T),
        "sw1T": np.ascontiguousarray(np.asarray(inputs["L1_attn_s_w1"]).T),
        "sw2T": np.ascontiguousarray(np.asarray(inputs["L1_attn_s_w2"]).T),
        "gwT": np.ascontiguousarray(np.asarray(inputs["L1_attn_g_w"]).T),
        "bias2": bias2,
        "gatewT": np.ascontiguousarray(np.asarray(inputs["L1_ffn_gate_w"]).T),
        "WgT": np.ascontiguousarray(np.asarray(inputs["L1_ffn_Wg"]).transpose(0, 2, 1)),
        "WuT": np.ascontiguousarray(np.asarray(inputs["L1_ffn_Wu"]).transpose(0, 2, 1)),
        "WdT": np.ascontiguousarray(np.asarray(inputs["L1_ffn_Wd"]).transpose(0, 2, 1)),
        "WsgT": np.ascontiguousarray(np.asarray(inputs["L1_ffn_Ws_g"]).T),
        "WsuT": np.ascontiguousarray(np.asarray(inputs["L1_ffn_Ws_u"]).T),
        "WsdT": np.ascontiguousarray(np.asarray(inputs["L1_ffn_Ws_d"]).T),
        "trig": trig, "cbf": cbf, "cf32": cf32,
    }
    in_maps = []
    for b in range(B):
        m = dict(shared)
        m["xT"] = np.ascontiguousarray(x[b].T)
        in_maps.append(m)
    res = run_bass_kernel_spmd(nc, in_maps, core_ids=list(range(B)))
    return np.stack([r["out"] for r in res.results], axis=0)


# version 13
# speedup vs baseline: 1.0415x; 1.0339x over previous
import numpy as np
from contextlib import ExitStack

import concourse.bass as bass
import concourse.mybir as mybir
from concourse.bass_utils import run_bass_kernel_spmd
from concourse import library_config as LC

F32 = mybir.dt.float32
BF16 = mybir.dt.bfloat16
I16 = mybir.dt.int16
I32 = mybir.dt.int32
AF = mybir.ActivationFunctionType
AL = mybir.AluOpType
AX = mybir.AxisListType

D = 1024
NC8 = 8
H = 16
DH = 64
FF0 = 2816
DS = 128
E = 16
FEXP = 2048
FSH = 2048
NEG = -1.0e30

SWAP_MASK = [(i ^ 1) for i in range(32)]


class P:

    def __init__(self, nc, stack):
        self.nc = nc
        self.stack = stack
        self.q = {k: [] for k in ("t", "v", "s", "g", "y")}
        self.sem = {}
        self.cnt = {}
        self.gd_rr = 0
        self.yd_rr = 0
        self.prev_cx = False

    def mksem(self, name):
        h = self.stack.enter_context(self.nc.semaphore(name))
        self.sem[name] = h
        self.cnt[name] = 0
        return h

    SELFWAIT = "v"

    def emit(self, eng, fn, waits=(), inc=None, by=1, cx=False):
        h = None
        swl = []
        if eng == "v" and inc == eng and self.cnt[eng] > 0 and (
                cx or self.prev_cx):
            swl = [(self.sem[eng], self.cnt[eng])]
        if eng == "v":
            self.prev_cx = cx
        if eng in P.SELFWAIT and inc == eng and self.cnt[eng] > 0:
            swl = [(self.sem[eng], self.cnt[eng])]
        if inc is not None:
            self.cnt[inc] += by
            h = self.sem[inc]
        wl = swl + [(self.sem[n], v) for (n, v) in waits if v > 0]

        def op(e, fn=fn, wl=wl, h=h, by=by):
            for (s, v) in wl:
                e.wait_ge(s, v)
            inst = fn(e)
            if h is not None:
                inst.then_inc(h, by)
        self.q[eng].append(op)
        return self.cnt[inc] if inc is not None else None

    def t(self, fn, waits=()):
        return self.emit("t", fn, waits, inc="t")

    def v(self, fn, waits=(), cx=False):
        return self.emit("v", fn, waits, inc="v", cx=cx)

    def s(self, fn, waits=()):
        return self.emit("s", fn, waits, inc="s")

    def g(self, fn, waits=()):
        return self.emit("g", fn, waits, inc="g")

    def gdma(self, fn, waits=()):
        k = self.gd_rr
        self.gd_rr = (k + 1) % 8
        name = "gd%d" % k
        prior = self.cnt[name]
        w = list(waits) + ([(name, prior)] if prior else [])
        self.emit("g", fn, w, inc=name, by=16)
        return (name, self.cnt[name])

    def ydma(self, fn, waits=()):
        k = self.yd_rr
        self.yd_rr = (k + 1) % 8
        name = "yd%d" % k
        prior = self.cnt[name]
        w = list(waits) + ([(name, prior)] if prior else [])
        self.emit("y", fn, w, inc=name, by=16)
        return (name, self.cnt[name])

    def wait(self, eng, waits):
        self.emit(eng, lambda e: None, waits, inc=None)

    def barrier(self):
        snap = [(n, self.cnt[n]) for n in self.cnt]
        for eng in ("t", "v", "s", "g", "y"):
            self.wait(eng, [(n, v) for (n, v) in snap if v > 0])

    def c(self, name):
        return self.cnt[name]


def _ap(t, offset, dims):
    return bass.AP(t, offset, dims)


def build_program(T=2048, blocks=(0, 1, 2, 3)):
    nc = bass.Bass("TRN2", target_bir_lowering=False)
    stack = ExitStack()
    NT = T // 512
    NTT = T // 128

    xT = nc.dram_tensor("xT", [D, T], F32, kind="ExternalInput").ap()
    out = nc.dram_tensor("out", [T, D], F32, kind="ExternalOutput").ap()
    wqkv = nc.dram_tensor("wqkv", [D, 3 * D], F32, kind="ExternalInput").ap()
    wout_d = nc.dram_tensor("wout", [D, D], F32, kind="ExternalInput").ap()
    trig_d = nc.dram_tensor("trig", [128, 2 * T], BF16, kind="ExternalInput").ap()
    cbf_d = nc.dram_tensor("cbf", [128, 640], BF16, kind="ExternalInput").ap()
    cf32_d = nc.dram_tensor("cf32", [128, 512], F32, kind="ExternalInput").ap()
    if 3 in blocks:
        gwT3_d = nc.dram_tensor("gatewT", [D, E], F32, kind="ExternalInput").ap()
        WgT_d = nc.dram_tensor("WgT", [E, D, FEXP], F32, kind="ExternalInput").ap()
        WuT_d = nc.dram_tensor("WuT", [E, D, FEXP], F32, kind="ExternalInput").ap()
        WdT_d = nc.dram_tensor("WdT", [E, FEXP, D], F32, kind="ExternalInput").ap()
        WsgT_d = nc.dram_tensor("WsgT", [D, FSH], F32, kind="ExternalInput").ap()
        WsuT_d = nc.dram_tensor("WsuT", [D, FSH], F32, kind="ExternalInput").ap()
        WsdT_d = nc.dram_tensor("WsdT", [FSH, D], F32, kind="ExternalInput").ap()
        xln_h = nc.dram_tensor("xln_h", [T, D], BF16).ap()
    if 2 in blocks:
        At_d = nc.dram_tensor("At", [DS, DS], F32, kind="ExternalInput").ap()
        BwT_d = nc.dram_tensor("BwT", [D, DS], F32, kind="ExternalInput").ap()
        CwT_d = nc.dram_tensor("CwT", [DS, D], F32, kind="ExternalInput").ap()
        DwT_d = nc.dram_tensor("DwT", [D, D], F32, kind="ExternalInput").ap()
        sw1T_d = nc.dram_tensor("sw1T", [D, 512], F32, kind="ExternalInput").ap()
        sw2T_d = nc.dram_tensor("sw2T", [512, DS], F32, kind="ExternalInput").ap()
        gwT_d = nc.dram_tensor("gwT", [D, D], F32, kind="ExternalInput").ap()
        bias2_d = nc.dram_tensor("bias2", [128, 32], F32, kind="ExternalInput").ap()
    if 1 in blocks:
        wgate_d = nc.dram_tensor("wgate", [FF0, D], F32, kind="ExternalInput").ap()
        wup_d = nc.dram_tensor("wup", [FF0, D], F32, kind="ExternalInput").ap()
        wdown_d = nc.dram_tensor("wdown", [D, FF0], F32, kind="ExternalInput").ap()
        wgq_h = nc.dram_tensor("wgq_h", [D, FF0], BF16).ap()
        wuq_h = nc.dram_tensor("wuq_h", [D, FF0], BF16).ap()
        wdq_h = nc.dram_tensor("wdq_h", [FF0, D], BF16).ap()

    BASE = (nc.sbuf_base + 31) // 32 * 32
    TOP = nc.sbuf_top
    cur = [BASE]
    offs = {}

    def at(name, shape, dt, offset=None):
        nbytes = int(np.prod(shape[1:])) * {F32: 4, BF16: 2, I32: 4, I16: 2}[dt]
        if offset is None:
            offset = cur[0]
            cur[0] = (offset + nbytes + 31) // 32 * 32
        assert offset + nbytes <= TOP, (name, offset + nbytes, TOP)
        offs[name] = offset
        return nc.alloc_sbuf_tensor_at(name, shape, dt, offset=offset)

    xres = at("xres", [128, NC8, T], F32)
    bufA = at("bufA", [128, NC8, T], BF16)
    bufB = at("bufB", [128, NC8, T], BF16)
    scrA = at("scrA", [128, 512], F32)
    scrB = at("scrB", [128, 512], F32)
    scrC = at("scrC", [128, 512], F32)
    cb = at("cb", [128, 640], BF16)
    cf = at("cf", [128, 512], F32)
    rows = at("rows", [1, 1536], F32)
    lbuf = at("lbuf", [128, 8], F32)
    rbuf = at("rbuf", [128, 8], F32)
    sqs = at("sqs", [128, 512], BF16)
    PH = cur[0]

    trg = at("trg", [128, 2, T], BF16, PH)
    qb = at("qb", [128, 2, T], BF16, PH + 8192)
    kb = at("kb", [128, 2, T], BF16, PH + 16384)
    vtm = at("vtm", [128, 2, NTT, 130], BF16, PH + 24576)
    pts = at("pts", [128, 2, 512], BF16, PH + 32960)
    oacc = at("oacc", [128, 512], F32, PH + 35008)
    wpair = at("wpair", [128, 2, 3, NC8, 128], BF16, PH + 37056)
    wobuf = at("wobuf", [128, NC8, 512], BF16, PH)
    tout = at("tout", [128, NC8, 128], F32, offs["bufB"])

    if 1 in blocks:
        qwin = at("qwin", [128, 2, 512], F32, PH + 49344)
        qsA = at("qsA", [128, 512], F32, PH + 53440)
        qsB = at("qsB", [128, 512], F32, PH + 55488)
        qsg = at("qsg", [128, 512], BF16, PH + 57536)
        qbm = at("qbm", [128, 5, 32], F32, PH + 58560)
        qiA = at("qiA", [128, 512], BF16, PH + 59200)
        qiB = at("qiB", [128, 512], BF16, PH + 60224)
        qow = at("qow", [128, 2, 512], BF16, PH + 61248)
        qstage = at("qstage", [128, 2, 512], BF16, (TOP - 2048) // 32 * 32)
        ysb = at("ysb", [128, 512], F32, PH)
        qyb = at("qyb", [128, 512], BF16, PH + 2048)
        sgbuf = at("sgbuf", [128, 512], F32, PH + 3072)
        hstage = at("hstage", [128, 512], F32, PH + 5120)
        wgbuf = at("wgbuf", [128, 2, NC8, 512], BF16, PH + 7168)
        wubuf = at("wubuf", [128, 2, NC8, 512], BF16, PH + 23552)
        wdbuf = at("wdbuf", [128, 2, 22, 128], BF16, PH + 39936)
        hfm_off = offs["bufA"]
        hfm = at("hfm", [128, 22, 512], BF16, hfm_off)
    if 2 in blocks:
        NCH = T // 64
        a1b = at("a1b", [128, 4, T], BF16, PH)
        ssb = at("ssb", [128, T], F32, PH + 8 * T)
        usb = at("usb", [128, T], F32, PH + 12 * T)
        hsf = at("hsf", [128, T], BF16, PH + 16 * T)
        sw1b = at("sw1b", [128, NC8, 512], BF16, PH + 18 * T)
        sw2b = at("sw2b", [128, 4, 128], BF16, PH + 18 * T + 8192)
        bwb = at("bwb", [128, NC8, 128], BF16, PH + 18 * T + 9216)
        cwb = at("cwb", [128, NC8, 128], BF16, PH + 18 * T + 11264)
        atb = at("atb", [128, 128], BF16, PH + 18 * T + 13312)
        a64b = at("a64b", [128, 128], BF16, PH + 18 * T + 13568)
        apw = at("apw", [128, 128], BF16, PH + 18 * T + 13824)
        b2b = at("b2b", [128, 32], F32, PH + 18 * T + 14080)
        cdb = at("cdb", [128, 8], F32, PH + 18 * T + 14208)
        w2s = at("w2s", [128, 2, 2, NC8, 128], BF16, PH + 18 * T + 14336)
        hstart = at("hstart", [128, NCH + 1], BF16, PH + 18 * T + 22528)
        sc2A = at("sc2A", [128, 512], F32, PH + 18 * T + 22656)
        sc2B = at("sc2B", [128, 512], F32, PH + 18 * T + 24704)
    if 3 in blocks:
        CAP = T // 8
        SG = T // 8
        NST3 = max(CAP // 128, 1)
        o3 = [PH]

        def nxt(nbytes):
            r = o3[0]
            o3[0] = (r + nbytes + 31) // 32 * 32
            return r
        wsgb = at("wsgb", [128, 2, NC8, 256], BF16, nxt(8192))
        wsub = at("wsub", [128, 2, NC8, 256], BF16, nxt(8192))
        wsdb = at("wsdb", [128, 2, 16, 128], BF16, nxt(8192))
        hshb = at("hshb", [128, 16, 512], BF16, nxt(16384))
        gwT3b = at("gwT3b", [128, NC8, 16], BF16, nxt(256))
        sctm = at("sctm", [128, T // 128, 16], F32, nxt((T // 128) * 64))
        slottm = at("slottm", [128, T // 128, 16], F32, nxt((T // 128) * 64))
        iotab = at("iotab", [128, 256], F32, nxt(1024))
        ttrow = at("ttrow", [128, T // 128], F32, nxt((T // 128) * 4))
        pcol = at("pcol", [128, 1], F32, nxt(32))
        rhs3 = at("rhs3", [128, T // 128, E, 3], BF16, nxt((T // 128) * 96))
        pbt = at("pbt", [128, 256], BF16, nxt(512))
        semb = at("semb", [16, T], F32, offs["bufA"])
        junkb = at("junkb", [16, T], F32, offs["bufA"] + 4 * T)
        mskb = at("mskb", [16, T], F32, offs["bufA"] + 8 * T)
        iu32 = at("iu32", [128, E, 2], I32, (TOP - 2048) // 32 * 32)
        idw = at("idw", [128, E, 2, 3], F32, (TOP - 4096) // 32 * 32)
        idc = at("idc", [128, E, 2], F32, (TOP - 5120) // 32 * 32)
        smls = at("smls", [16, 16], F32, (TOP - 5632) // 32 * 32)
        ob = [PH]

        def nxb(nbytes):
            r = ob[0]
            ob[0] = (r + nbytes + 31) // 32 * 32
            return r
        xtokt = at("xtokt", [128, 2, NST3, D], BF16, nxb(2 * NST3 * D * 2))
        xtokf = at("xtokf", [128, 2, NC8, CAP], BF16, nxb(2 * NC8 * CAP * 2))
        heb = at("heb", [128, 2, FEXP // 128, CAP], BF16, offs["bufA"])
        wdeb = at("wdeb", [128, 2, FEXP // 128, 256], BF16, nxb(2 * 16 * 256 * 2))
        eob = at("eob", [128, NST3, D], F32, nxb(NST3 * D * 4))
        wgeb = at("wgeb", [128, 2, NC8, 512], BF16, offs["bufB"])
        wueb = at("wueb", [128, 2, NC8, 512], BF16, offs["bufB"] + 16384)
    IDENT = cb[:, 0:128]
    MDIAG = cb[:, 128:256]
    MFULL = cb[:, 256:384]
    MDIAGT = cb[:, 512:640]
    ONESB = cb[:, 384:385]
    ONEROW = cf[0:1, 0:128]
    IDENTF = cf[:, 128:256]
    EPSAP = cf[0:1, 300:301]
    MBIAS = cf[:, 301:302]
    r_mm = rows[0:1, 0:512]
    r_v = rows[0:1, 512:1024]
    r_t1 = rows[0:1, 1024:1536]

    ps_s = stack.enter_context(nc.psum_tensor("ps_s", [128, 2048], F32))
    ps_a = stack.enter_context(nc.psum_tensor("ps_a", [128, 512], F32))
    ps_b = stack.enter_context(nc.psum_tensor("ps_b", [128, 512], F32))
    ps_c = stack.enter_context(nc.psum_tensor("ps_c", [128, 512], F32))
    ps_t = stack.enter_context(nc.psum_tensor("ps_t", [128, 1024], BF16))

    p = P(nc, stack)
    for name in ("t", "v", "s", "g"):
        p.mksem(name)
    for k in range(8):
        p.mksem("gd%d" % k)
        p.mksem("yd%d" % k)

    xsrc = _ap(xT.tensor, 0, [[T, 128], [128 * T, NC8], [1, T]])
    l1 = p.ydma(lambda e: e.dma_start(xres[:], xsrc))
    l2 = p.ydma(lambda e: e.dma_start(cb[:], cbf_d))
    l3 = p.ydma(lambda e: e.dma_start(cf[:], cf32_d))
    for eng in ("t", "v", "s", "g"):
        p.wait(eng, [l1, l2, l3])

    def layer_norm(dst, quant=False):
        for tc in range(NT):
            t0 = tc * 512
            for c in range(NC8):
                vv = p.v(lambda e, c=c, t0=t0: e.tensor_copy(
                    bufA[:, c, t0:t0 + 512], xres[:, c, t0:t0 + 512]))
                sv = p.s(lambda e, c=c, t0=t0: e.activation(
                    sqs[:], xres[:, c, t0:t0 + 512], AF.Square))
                p.t(lambda e, c=c, t0=t0: e.matmul(
                    ps_c[0:1, :], ONESB, bufA[:, c, t0:t0 + 512],
                    start=(c == 0), stop=(c == NC8 - 1)), waits=[("v", vv)])
                sv2 = p.t(lambda e, c=c: e.matmul(
                    ps_b[0:1, :], ONESB, sqs[:],
                    start=(c == 0), stop=(c == NC8 - 1)), waits=[("s", sv)])
                p.wait("s", [("t", sv2)])
            mmv = p.c("t")
            p.v(lambda e: e.tensor_scalar_mul(r_mm, ps_c[0:1, :], 1.0 / D),
                waits=[("t", mmv)])
            p.v(lambda e: e.tensor_scalar_mul(r_v, ps_b[0:1, :], 1.0 / D))
            p.v(lambda e: e.tensor_tensor(r_t1, r_mm, r_mm, AL.mult))
            vv2 = p.v(lambda e: e.tensor_tensor(r_v, r_v, r_t1, AL.subtract))
            sv2 = p.s(lambda e: e.activation(r_v, r_v, AF.Sqrt, bias=EPSAP),
                      waits=[("v", vv2)])
            p.v(lambda e: e.reciprocal(r_t1, r_v), waits=[("s", sv2)])
            vv3 = p.v(lambda e: e.scalar_tensor_tensor(
                r_v, r_mm, -1.0, r_t1, AL.mult, AL.mult))
            p.t(lambda e: e.matmul(ps_a[:], ONEROW, r_t1, start=True, stop=True),
                waits=[("v", vv3)])
            tb2 = p.t(lambda e: e.matmul(ps_b[:], ONEROW, r_v,
                                         start=True, stop=True))
            lastv = 0
            for c in range(NC8):
                p.v(lambda e, c=c, t0=t0: e.tensor_tensor(
                    scrA[:], xres[:, c, t0:t0 + 512], ps_a[:], AL.mult),
                    waits=[("t", tb2)])
                if not quant:
                    lastv = p.v(lambda e, c=c, t0=t0: e.tensor_tensor(
                        dst[:, c, t0:t0 + 512], scrA[:], ps_b[:], AL.add))
                else:
                    lastv = p.v(lambda e, c=c, t0=t0: e.tensor_tensor(
                        scrB[:], scrA[:], ps_b[:], AL.add))
                    fp4q_y(scrB[:], dst[:, c, t0:t0 + 512])
                    lastv = p.c("v")
            p.wait("t", [("v", lastv)])
        return p.c("v")

    def fp4q(src, dst, W, waits0=(), waitsd=()):
        NB = W // 16
        bm = qbm[:, 0, 0:NB]
        sc = qbm[:, 1, 0:NB]
        rr = qbm[:, 2, 0:NB]
        r2 = qbm[:, 3, 0:NB]
        hs = qbm[:, 4, 0:NB]
        qt_ = qbm.ap().tensor
        s3 = _ap(src.tensor, src.offset, [list(src.ap[0]), [16, NB], [1, 16]])
        p.v(lambda e: e.tensor_reduce(bm, s3, AX.X, AL.max,
                                      apply_absolute_value=True),
            waits=list(waits0), cx=True)
        p.v(lambda e: e.tensor_scalar(sc, bm, 1.0 / 6.0, 1e-12,
                                      AL.mult, AL.max))
        p.v(lambda e: e.reciprocal(rr, sc))
        p.v(lambda e: e.tensor_scalar(r2, rr, 2.0, None, AL.mult))
        r2v = p.v(lambda e: e.tensor_scalar(hs, sc, 0.5, None, AL.mult))
        r2b = _ap(qt_, 3 * 32, [[160, 128], [1, NB], [0, 16]])
        hsb = _ap(qt_, 4 * 32, [[160, 128], [1, NB], [0, 16]])
        qsA_t = qsA.ap().tensor
        xnv = p.v(lambda e: e.tensor_tensor(
            _ap(qsA_t, 0, [[512, 128], [16, NB], [1, 16]]), s3, r2b, AL.mult),
            waits=[("s", p.c("s"))])
        sgv = p.s(lambda e: e.activation(qsg[:, 0:W], qsA[:, 0:W], AF.Sign),
                  waits=[("v", xnv)])
        av = p.v(lambda e: e.scalar_tensor_tensor(
            qsB[:, 0:W], qsA[:, 0:W], -1.0, qsA[:, 0:W], AL.mult, AL.max))
        p.v(lambda e: e.tensor_scalar(qiB[:, 0:W], qsB[:, 0:W], 5.0, None,
                                      AL.is_gt))
        for th in (7.0, 10.0, 10.0):
            p.v(lambda e, th=th: e.scalar_tensor_tensor(
                qiB[:, 0:W], qsB[:, 0:W], th, qiB[:, 0:W], AL.is_gt, AL.add))
        p.v(lambda e: e.tensor_scalar(qiA[:, 0:W], qsB[:, 0:W], 0.5, None,
                                      AL.is_gt))
        for th in (1.5, 2.5, 3.5):
            p.v(lambda e, th=th: e.scalar_tensor_tensor(
                qiA[:, 0:W], qsB[:, 0:W], th, qiA[:, 0:W], AL.is_gt, AL.add))
        cmb = p.v(lambda e: e.scalar_tensor_tensor(
            qsA[:, 0:W], qiB[:, 0:W], 2.0, qiA[:, 0:W], AL.mult, AL.add),
            waits=[("s", sgv)])
        p.v(lambda e: e.tensor_tensor(qsB[:, 0:W], qsA[:, 0:W],
                                      qsg[:, 0:W], AL.mult))
        d3 = _ap(dst.tensor, dst.offset, [list(dst.ap[0]), [16, NB], [1, 16]])
        return p.v(lambda e: e.tensor_tensor(
            d3, _ap(qsB.ap().tensor, 0, [[512, 128], [16, NB], [1, 16]]),
            hsb, AL.mult), waits=list(waitsd))

    def fp4q_y(chunk_src, chunk_dst):
        p.v(lambda e: e.transpose(ysb[:], chunk_src), cx=True)
        NB2 = 16
        bm = qbm[:, 0, 0:32]
        sc = qbm[:, 1, 0:32]
        rr = qbm[:, 2, 0:32]
        r2 = qbm[:, 3, 0:32]
        hs = qbm[:, 4, 0:32]
        qt_ = qbm.ap().tensor
        ys_t = ysb.ap().tensor
        y4 = _ap(ys_t, 0, [[512, 128], [32, NB2], [16, 2], [1, 16]])
        p.v(lambda e: e.tensor_reduce(
            _ap(qt_, 0, [[160, 128], [2, NB2], [1, 2]]), y4, AX.X, AL.max,
            apply_absolute_value=True), cx=True)
        p.v(lambda e: e.tensor_scalar(sc, bm, 1.0 / 6.0, 1e-12, AL.mult, AL.max))
        p.v(lambda e: e.reciprocal(rr, sc))
        p.v(lambda e: e.tensor_scalar(r2, rr, 2.0, None, AL.mult))
        p.v(lambda e: e.tensor_scalar(hs, sc, 0.5, None, AL.mult))
        r2b = _ap(qt_, 3 * 32, [[160, 128], [2, NB2], [1, 2], [0, 16]])
        hsb = _ap(qt_, 4 * 32, [[160, 128], [2, NB2], [1, 2], [0, 16]])
        W = 512
        qsA_t = qsA.ap().tensor
        xnv = p.v(lambda e: e.tensor_tensor(
            qsA[:, 0:W], ysb[:], r2b, AL.mult), waits=[("s", p.c("s"))])
        sgv = p.s(lambda e: e.activation(qsg[:, 0:W], qsA[:, 0:W], AF.Sign),
                  waits=[("v", xnv)])
        av = p.v(lambda e: e.scalar_tensor_tensor(
            qsB[:, 0:W], qsA[:, 0:W], -1.0, qsA[:, 0:W], AL.mult, AL.max))
        p.v(lambda e: e.tensor_scalar(qiB[:, 0:W], qsB[:, 0:W], 5.0, None,
                                      AL.is_gt))
        for th in (7.0, 10.0, 10.0):
            p.v(lambda e, th=th: e.scalar_tensor_tensor(
                qiB[:, 0:W], qsB[:, 0:W], th, qiB[:, 0:W], AL.is_gt, AL.add))
        p.v(lambda e: e.tensor_scalar(qiA[:, 0:W], qsB[:, 0:W], 0.5, None,
                                      AL.is_gt))
        for th in (1.5, 2.5, 3.5):
            p.v(lambda e, th=th: e.scalar_tensor_tensor(
                qiA[:, 0:W], qsB[:, 0:W], th, qiA[:, 0:W], AL.is_gt, AL.add))
        cmb = p.v(lambda e: e.scalar_tensor_tensor(
            qsA[:, 0:W], qiB[:, 0:W], 2.0, qiA[:, 0:W], AL.mult, AL.add),
            waits=[("s", sgv)])
        p.v(lambda e: e.tensor_tensor(qsB[:, 0:W], qsA[:, 0:W],
                                      qsg[:, 0:W], AL.mult))
        p.v(lambda e: e.tensor_tensor(qyb[:], qsB[:, 0:W], hsb, AL.mult))
        return p.v(lambda e: e.transpose(chunk_dst, qyb[:]), cx=True)


    def qkv_dma(hp, buf):
        wl = []
        tw = [("t", p.c("t"))]
        for j, base in enumerate((128 * hp, D + 128 * hp, 2 * D + 128 * hp)):
            src = _ap(wqkv.tensor, base,
                      [[3 * D, 128], [128 * 3 * D, NC8], [1, 128]])
            wl.append(p.gdma(lambda e, j=j, src=src, buf=buf: e.dma_start(
                wpair[:, buf, j, :, :], src), waits=tw))
        return wl

    def block0_attn():
        lnv = layer_norm(bufB)
        p.wait("g", [("v", lnv)])
        NG = T // 512
        vtm_t = vtm.ap().tensor
        psc_t = ps_c.ap().tensor
        tgv = p.ydma(lambda e: e.dma_start(
            trg[:].rearrange("p a t -> p (a t)"),
            _ap(trig_d.tensor, 0, [[2 * T, 128], [T, 2], [1, T]])))
        dv = qkv_dma(0, 0)
        rope_v = {-1: lnv}
        sq_end = {}
        sc_t_end = {}
        exp_s = {}
        sc_par = 0
        fin_prev = 0
        bc_prev = 0
        for hp in range(8):
            par = hp % 2
            buf = hp % 2
            guard_t = sc_t_end.get(hp - 2, 0)
            qrv = 0
            krv = 0
            vtmv_prev = rope_v[hp - 1]
            first = True
            for tc in range(NT):
                t0 = tc * 512
                w0 = [("v", vtmv_prev)]
                if first:
                    w0 = list(dv) + w0
                    first = False
                for c in range(NC8):
                    p.t(lambda e, c=c, buf=buf, t0=t0: e.matmul(
                        ps_a[:], wpair[:, buf, 0, c, :],
                        bufB[:, c, t0:t0 + 512],
                        start=(c == 0), stop=(c == NC8 - 1)),
                        waits=(w0 if c == 0 else []))
                qmv = p.c("t")
                for c in range(NC8):
                    p.t(lambda e, c=c, buf=buf, t0=t0: e.matmul(
                        ps_b[:], wpair[:, buf, 1, c, :],
                        bufB[:, c, t0:t0 + 512],
                        start=(c == 0), stop=(c == NC8 - 1)))
                kmv = p.c("t")
                for tt2 in range(4):
                    tt = 4 * tc + tt2
                    for c in range(NC8):
                        p.t(lambda e, c=c, buf=buf, tt=tt, tt2=tt2: e.matmul(
                            ps_c[:, tt2 * 128:tt2 * 128 + 128],
                            bufB[:, c, tt * 128:tt * 128 + 128],
                            wpair[:, buf, 2, c, :],
                            start=(c == 0), stop=(c == NC8 - 1)))
                vmv = p.c("t")
                for (pspk, dstb, wv_) in ((ps_a, qb, qmv), (ps_b, kb, kmv)):
                    shw = [("t", wv_)]
                    if tc == 0 and pspk is ps_a:
                        shw += [("t", guard_t), tgv]
                    shv = p.v(lambda e, pspk=pspk: e.stream_shuffle(
                        scrC[:], pspk[:], SWAP_MASK), waits=shw, cx=True)
                    p.v(lambda e, t0=t0: e.tensor_tensor(
                        scrB[:], scrC[:], trg[:, 1, t0:t0 + 512], AL.mult))
                    p.v(lambda e, pspk=pspk, t0=t0: e.tensor_tensor(
                        scrA[:], pspk[:], trg[:, 0, t0:t0 + 512], AL.mult))
                    p.v(lambda e, dstb=dstb, t0=t0, par=par: e.tensor_tensor(
                        dstb[:, par, t0:t0 + 512], scrA[:], scrB[:], AL.add))
                p.v(lambda e, tc=tc, par=par: e.tensor_copy(
                    _ap(vtm_t, par * NTT * 130 + 4 * tc * 130,
                        [[2 * NTT * 130, 128], [130, 4], [65, 2], [1, 64]]),
                    _ap(psc_t, 0, [[512, 128], [128, 4], [64, 2], [1, 64]])),
                    waits=[("t", vmv)])
                vtmv_prev = p.c("v")
            p.v(lambda e, par=par: e.memset(
                _ap(vtm_t, par * NTT * 130 + 64,
                    [[2 * NTT * 130, 128], [130, NTT], [65, 2], [1, 1]]), 1.0))
            rope_v[hp] = p.c("v")
            pump(4)
            if hp < 7:
                dv = qkv_dma(hp + 1, (hp + 1) % 2)
            first_sc = True
            for h01 in range(2):
                base = 64 * h01
                for qg in range(NG):
                    ntile = 4 * qg + 4
                    pend = []
                    for t in range(ntile):
                        pr = sc_par % 2
                        psc = ps_s[:, pr * 512:pr * 512 + 512]
                        w0 = []
                        if first_sc:
                            w0.append(("v", rope_v[hp]))
                            first_sc = False
                        if sc_par - 2 in exp_s:
                            w0.append(("s", exp_s[sc_par - 2]))
                        if t == 0:
                            w0.append(("v", fin_prev))
                        band = t - 4 * qg
                        p.t(lambda e, t=t, qg=qg, base=base, par=par, psc=psc:
                            e.matmul(
                                psc, kb[base:base + 64, par,
                                        t * 128:t * 128 + 128],
                                qb[base:base + 64, par,
                                   qg * 512:qg * 512 + 512],
                                start=True, stop=True), waits=w0)
                        if band >= 0:
                            for jj in range(band):
                                p.t(lambda e, jj=jj, psc=psc: e.matmul(
                                    psc[:, jj * 128:jj * 128 + 128], IDENT,
                                    MFULL, start=False, stop=True,
                                    skip_group_check=True))
                            p.t(lambda e, band=band, psc=psc: e.matmul(
                                psc[:, band * 128:band * 128 + 128], IDENT,
                                MDIAGT, start=False, stop=True,
                                skip_group_check=True))
                        ex = p.s(lambda e, pr=pr, psc=psc:
                                 e.activation(
                                     pts[:, pr, :], psc, AF.Exp, scale=0.125,
                                     bias=MBIAS),
                                 waits=[("t", p.c("t"))])
                        exp_s[sc_par] = ex
                        pend.append((t, pr, ex))
                        if len(pend) > 1:
                            t_, pr_, ex_ = pend.pop(0)
                            p.t(lambda e, t=t_, par=par, h01=h01, pr=pr_:
                                e.matmul(
                                    ps_s[0:65, 1024:1536],
                                    vtm[:, par, t, 65 * h01:65 * h01 + 65],
                                    pts[:, pr, :],
                                    start=(t == 0), stop=False),
                                waits=[("s", ex_)])
                        sc_par += 1
                    t_, pr_, ex_ = pend.pop(0)
                    p.t(lambda e, t=t_, par=par, h01=h01, pr=pr_, nt=ntile:
                        e.matmul(
                            ps_s[0:65, 1024:1536],
                            vtm[:, par, t, 65 * h01:65 * h01 + 65],
                            pts[:, pr, :],
                            start=(t == 0), stop=True),
                        waits=[("s", ex_)])
                    pvend = p.c("t")
                    rv0 = p.s(lambda e: e.activation(
                        r_v, ps_s[64:65, 1024:1536], AF.Ln),
                        waits=[("t", pvend)])
                    rv = p.s(lambda e: e.activation(
                        r_t1, r_v, AF.Exp, scale=-1.0))
                    oev = p.v(lambda e: e.tensor_copy(
                        oacc[0:64, :], ps_s[0:64, 1024:1536]),
                        waits=[("t", pvend)])
                    bc_prev = p.t(lambda e: e.matmul(
                        ps_s[0:64, 1536:2048], cf[0:1, 0:64], r_t1,
                        start=True, stop=True), waits=[("s", rv)])
                    fin_prev = p.v(lambda e, h01=h01, hp=hp, qg=qg:
                        e.tensor_tensor(
                            bufA[64 * h01:64 * h01 + 64, hp,
                                 qg * 512:qg * 512 + 512],
                            oacc[0:64, :], ps_s[0:64, 1536:2048],
                            AL.mult), waits=[("t", bc_prev)])
                    pump(4)
            sc_t_end[hp] = p.c("t")
        p.barrier()
        for mh in range(2):
            src = _ap(wout_d.tensor, mh * 512,
                      [[D, 128], [128 * D, NC8], [1, 512]])
            wv = p.gdma(lambda e, src=src: e.dma_start(wobuf[:], src),
                        waits=[("t", p.c("t"))])
            p.wait("t", [wv])
            for mt in range(4):
                m0c = (mh * 512 + mt * 128) // 128
                for tc in range(NT):
                    t0 = tc * 512
                    for c in range(NC8):
                        p.t(lambda e, c=c, mt=mt, t0=t0: e.matmul(
                            ps_a[:], wobuf[:, c, mt * 128:mt * 128 + 128],
                            bufA[:, c, t0:t0 + 512],
                            start=(c == 0), stop=(c == NC8 - 1)))
                    omv = p.c("t")
                    rv = p.v(lambda e, m0c=m0c, t0=t0: e.tensor_tensor(
                        xres[:, m0c, t0:t0 + 512],
                        xres[:, m0c, t0:t0 + 512], ps_a[:], AL.add),
                        waits=[("t", omv)])
                    p.wait("t", [("v", rv)])

    def quant_gen():
        units = []
        for (wsrc, wdst, RS, LD) in ((wgate_d, wgq_h, D, FF0),
                                     (wup_d, wuq_h, D, FF0)):
            for rc in range(FF0 // 128):
                for scc in range(D // 512):
                    units.append((wsrc, wdst, RS, LD, rc, scc, 512))
        for rc in range(NC8):
            for scc in range((FF0 + 511) // 512):
                kw = min(512, FF0 - scc * 512)
                units.append((wdown_d, wdq_h, FF0, D, rc, scc, kw))
        N = len(units)
        ldtok = {}
        qvdone = {}
        trt = {}
        evq = {}
        sttok = {}

        def load(i):
            if i >= N:
                return
            wsrc, wdst, RS, LD, rc, scc, kw = units[i]
            par = i % 2
            ldtok[i] = p.ydma(lambda e, wsrc=wsrc, RS=RS, rc=rc, scc=scc,
                              kw=kw, par=par: e.dma_start(
                qwin[:, par, 0:kw],
                _ap(wsrc.tensor, rc * 128 * RS + scc * 512,
                    [[RS, 128], [1, kw]])),
                waits=[("v", qvdone.get(i - 2, 0))])
        load(0)
        load(1)
        for i in range(N):
            wsrc, wdst, RS, LD, rc, scc, kw = units[i]
            par = i % 2
            qvdone[i] = fp4q(
                qwin[:, par, 0:kw],
                _ap(qow.ap().tensor, par * 512, [[1024, 128], [1, kw]]),
                kw, waits0=[ldtok[i]],
                waitsd=[("t", trt.get(i - 2, 0))])
            load(i + 1)
            yield
            for j in range(kw // 128):
                w0 = []
                if j == 0:
                    w0 = [("v", qvdone[i]), ("v", evq.get(i - 1, 0))]
                p.t(lambda e, j=j, par=par: e.transpose(
                    ps_t[:, j * 128:j * 128 + 128],
                    qow[:, par, j * 128:j * 128 + 128], IDENT), waits=w0)
            trt[i] = p.c("t")
            evq[i] = p.v(lambda e, par=par, kw=kw: e.tensor_copy(
                qstage[:, par, 0:kw], ps_t[:, 0:kw]),
                waits=[("t", trt[i])] + ([sttok[i - 2]] if i - 2 in sttok
                                         else []))
            sttok[i] = p.ydma(lambda e, wdst=wdst, LD=LD, rc=rc, scc=scc,
                              kw=kw, par=par: e.dma_start(
                _ap(wdst.tensor, scc * 512 * LD + rc * 128,
                    [[LD, 128], [128 * LD, kw // 128], [1, 128]]),
                _ap(qstage.ap().tensor, par * 512,
                    [[1024, 128], [128, kw // 128], [1, 128]])),
                waits=[("v", evq[i])])
            yield

    qg_state = {"g": None}

    def pump(n=1):
        g = qg_state["g"]
        if g is None:
            return
        for _ in range(n):
            try:
                next(g)
            except StopIteration:
                qg_state["g"] = None
                return

    def block1_ffn():
        pump(10 ** 9)
        p.barrier()
        layer_norm(bufB, quant=True)
        p.barrier()
        FCW = [512] * 5 + [256]
        FOFF = [0, 512, 1024, 1536, 2048, 2560]
        b1_dtok = {}
        b1_dcnt = {}
        b1_rv = {}
        b1tok = {}
        b1cnt = {}

        def gload(gq):
            if gq >= NT * 6:
                return
            fc = gq % 6
            fw = FCW[fc]
            f0 = FOFF[fc]
            j = gq % 2
            wt = [("t", b1cnt.get(gq - 2, 0))]
            d1 = p.gdma(lambda e, f0=f0, fw=fw, j=j: e.dma_start(
                wgbuf[:, j, :, 0:fw],
                _ap(wgq_h.tensor, f0,
                    [[FF0, 128], [128 * FF0, NC8], [1, fw]])), waits=wt)
            d2 = p.gdma(lambda e, f0=f0, fw=fw, j=j: e.dma_start(
                wubuf[:, j, :, 0:fw],
                _ap(wuq_h.tensor, f0,
                    [[FF0, 128], [128 * FF0, NC8], [1, fw]])), waits=wt)
            b1tok[gq] = [d1, d2]
        gload(0)
        gload(1)
        for tc in range(NT):
            t0 = tc * 512
            for fc, fw in enumerate(FCW):
                gq = tc * 6 + fc
                jq = gq % 2
                for tt2 in range(4):
                    tt = 4 * tc + tt2
                    for c in range(NC8):
                        w0 = b1tok[gq] if (tt2 == 0 and c == 0) else []
                        p.t(lambda e, c=c, tt=tt, fw=fw, j=jq: e.matmul(
                            ps_a[:, 0:fw], bufB[:, c, tt * 128:tt * 128 + 128],
                            wgbuf[:, j, c, 0:fw],
                            start=(c == 0), stop=(c == NC8 - 1)), waits=w0)
                    gv2 = p.c("t")
                    for c in range(NC8):
                        p.t(lambda e, c=c, tt=tt, fw=fw, j=jq: e.matmul(
                            ps_b[:, 0:fw], bufB[:, c, tt * 128:tt * 128 + 128],
                            wubuf[:, j, c, 0:fw],
                            start=(c == 0), stop=(c == NC8 - 1)))
                    uv2 = p.c("t")
                    sv = p.s(lambda e, fw=fw: e.activation(
                        sgbuf[:, 0:fw], ps_a[:, 0:fw], AF.Silu),
                        waits=[("t", gv2)])
                    hv = p.v(lambda e, fw=fw: e.tensor_tensor(
                        hstage[:, 0:fw], sgbuf[:, 0:fw], ps_b[:, 0:fw],
                        AL.mult), waits=[("s", sv), ("t", uv2)])
                    p.wait("t", [("v", hv)])
                    fp4q(hstage[:, 0:fw], qow[:, 0, 0:fw], fw)
                    qv = p.c("v")
                    p.wait("t", [("v", qv)])
                    for j in range(fw // 128):
                        p.t(lambda e, j=j: e.transpose(
                            ps_t[:, j * 128:j * 128 + 128],
                            qow[:, 0, j * 128:j * 128 + 128], IDENT))
                    trv = p.c("t")
                    ev = p.v(lambda e, fc=fc, fw=fw, tt2=tt2: e.tensor_copy(
                        _ap(hfm.ap().tensor, (4 * fc) * 512 + tt2 * 128,
                            [[22 * 512, 128], [512, fw // 128], [1, 128]]),
                        ps_t[:, 0:fw]), waits=[("t", trv)])
                    p.wait("t", [("v", ev)])
                b1cnt[gq] = p.c("t")
                gload(gq + 2)
            def dload(dt, tc=tc):
                if dt >= NC8:
                    return
                gd = tc * NC8 + dt
                b1_dtok[gd] = p.gdma(lambda e, dt=dt, gd=gd: e.dma_start(
                    wdbuf[:, gd % 2, :, :],
                    _ap(wdq_h.tensor, dt * 128,
                        [[D, 128], [128 * D, 22], [1, 128]])),
                    waits=[("t", b1_dcnt.get(gd - 2, 0))])
            dload(0)
            dload(1)
            for dt in range(NC8):
                gd = tc * NC8 + dt
                pbk = ps_a[:] if gd % 2 == 0 else ps_b[:]
                w0 = [b1_dtok[gd]]
                if gd - 2 in b1_rv:
                    w0.append(("v", b1_rv[gd - 2]))
                for fc2 in range(22):
                    p.t(lambda e, fc2=fc2, gd=gd, pbk=pbk: e.matmul(
                        pbk, wdbuf[:, gd % 2, fc2, :], hfm[:, fc2, :],
                        start=(fc2 == 0), stop=(fc2 == 21)),
                        waits=(w0 if fc2 == 0 else []))
                b1_dcnt[gd] = omv = p.c("t")
                dload(dt + 2)
                b1_rv[gd] = p.v(lambda e, dt=dt, t0=t0, pbk=pbk: e.tensor_tensor(
                    xres[:, dt, t0:t0 + 512],
                    xres[:, dt, t0:t0 + 512], pbk, AL.add),
                    waits=[("t", omv)])
            p.wait("t", [("v", b1_rv[tc * NC8 + NC8 - 1]),
                         ("v", b1_rv[tc * NC8 + NC8 - 2])])

    def block2_ssm():
        p.barrier()
        NCH = T // 64
        wl = []
        wl.append(p.gdma(lambda e: e.dma_start(
            sw1b[:], _ap(sw1T_d.tensor, 0,
                         [[512, 128], [128 * 512, NC8], [1, 512]]))))
        wl.append(p.gdma(lambda e: e.dma_start(
            sw2b[:], _ap(sw2T_d.tensor, 0,
                         [[DS, 128], [128 * DS, 4], [1, 128]]))))
        wl.append(p.gdma(lambda e: e.dma_start(
            bwb[:], _ap(BwT_d.tensor, 0,
                        [[DS, 128], [128 * DS, NC8], [1, 128]]))))
        wl.append(p.gdma(lambda e: e.dma_start(
            cwb[:], _ap(CwT_d.tensor, 0,
                        [[D, 128], [128, NC8], [1, 128]]))))
        wl.append(p.gdma(lambda e: e.dma_start(atb[:], At_d)))
        wl.append(p.ydma(lambda e: e.dma_start(b2b[:], bias2_d)))
        lnv = layer_norm(bufB)
        p.wait("t", wl + [("v", lnv)])
        p.wait("v", wl[-1:])
        cdv = p.v(lambda e: e.tensor_tensor(
            cdb[:], b2b[:, 6:14], b2b[:, 14:22], AL.add))
        for tc in range(NT):
            t0 = tc * 512
            for mt in range(4):
                for c in range(NC8):
                    p.t(lambda e, c=c, mt=mt, t0=t0: e.matmul(
                        ps_a[:], sw1b[:, c, mt * 128:mt * 128 + 128],
                        bufB[:, c, t0:t0 + 512],
                        start=(c == 0), stop=(c == NC8 - 1)))
                mv = p.c("t")
                sv = p.s(lambda e, mt=mt, t0=t0: e.activation(
                    a1b[:, mt, t0:t0 + 512], ps_a[:], AF.Silu,
                    bias=b2b[:, 2 + mt:3 + mt]), waits=[("t", mv)])
                p.wait("t", [("s", sv)])
            for c4 in range(4):
                p.t(lambda e, c4=c4, t0=t0: e.matmul(
                    ps_b[:], sw2b[:, c4, :], a1b[:, c4, t0:t0 + 512],
                    start=(c4 == 0), stop=(c4 == 3)))
            mv = p.c("t")
            sv = p.s(lambda e, t0=t0: e.activation(
                ssb[:, t0:t0 + 512], ps_b[:], AF.Sigmoid,
                bias=b2b[:, 1:2]), waits=[("t", mv)])
            p.wait("t", [("s", sv)])
            for c in range(NC8):
                p.t(lambda e, c=c, t0=t0: e.matmul(
                    ps_c[:], bwb[:, c, :], bufB[:, c, t0:t0 + 512],
                    start=(c == 0), stop=(c == NC8 - 1)))
            mv = p.c("t")
            uv = p.v(lambda e, t0=t0: e.scalar_tensor_tensor(
                usb[:, t0:t0 + 512], ps_c[:], b2b[:, 0:1],
                ssb[:, t0:t0 + 512], AL.add, AL.mult),
                waits=[("t", mv), ("s", sv)])
            p.wait("t", [("v", uv)])
        p.v(lambda e: e.tensor_copy(a64b[:], atb[:]))
        for it in range(6):
            tv = p.t(lambda e: e.transpose(ps_t[:, 0:128], a64b[:], IDENT),
                     waits=[("v", p.c("v"))])
            ev = p.v(lambda e: e.tensor_copy(apw[:], ps_t[:, 0:128]),
                     waits=[("t", tv)])
            mv = p.t(lambda e: e.matmul(ps_c[:, 0:128], apw[:], a64b[:],
                                        start=True, stop=True),
                     waits=[("v", ev)])
            ev2 = p.v(lambda e: e.tensor_copy(a64b[:], ps_c[:, 0:128]),
                      waits=[("t", mv)])
            p.wait("t", [("v", ev2)])
        zv = p.v(lambda e: e.memset(hstart[:], 0.0))
        p.wait("t", [("v", zv)])
        for tl in range(64):
            mv = p.t(lambda e: e.matmul(ps_c[:, 0:NCH], atb[:],
                                        hstart[:, 0:NCH],
                                        start=True, stop=True))
            uap = _ap(usb.ap().tensor, tl, [[T, 128], [64, NCH]])
            av = p.v(lambda e, uap=uap: e.tensor_tensor(
                hstart[:, 0:NCH], ps_c[:, 0:NCH], uap, AL.add),
                waits=[("t", mv)])
            p.wait("t", [("v", av)])
        bcv = p.v(lambda e: e.tensor_copy(a1b[:, 0, 0:NCH], hstart[:, 0:NCH]))
        zv = p.v(lambda e: e.memset(hstart[:], 0.0))
        p.wait("t", [("v", p.c("v"))])
        for c in range(NCH - 1):
            mv = p.t(lambda e, c=c: e.matmul(
                ps_c[:, 0:1], a64b[:], hstart[:, c:c + 1],
                start=True, stop=True))
            av = p.v(lambda e, c=c: e.tensor_tensor(
                hstart[:, c + 1:c + 2], ps_c[:, 0:1], a1b[:, 0, c:c + 1],
                AL.add), waits=[("t", mv)])
            p.wait("t", [("v", av)])
        for tl in range(64):
            if tl == 0:
                rhs = hstart[:, 0:NCH]
            else:
                rhs = _ap(hsf.ap().tensor, tl - 1, [[T, 128], [64, NCH]])
            mv = p.t(lambda e, rhs=rhs: e.matmul(
                ps_c[:, 0:NCH], atb[:], rhs, start=True, stop=True))
            uap = _ap(usb.ap().tensor, tl, [[T, 128], [64, NCH]])
            oap = _ap(hsf.ap().tensor, tl, [[T, 128], [64, NCH]])
            av = p.v(lambda e, uap=uap, oap=oap: e.tensor_tensor(
                oap, ps_c[:, 0:NCH], uap, AL.add), waits=[("t", mv)])
            p.wait("t", [("v", av)])
        scan_av = p.c("v")
        s2tok = {}
        s2cnt = {}
        y_rv = {}
        y_dv2 = {}

        def w2load(dt):
            if dt >= NC8:
                return
            j = dt % 2
            wt = [("t", s2cnt.get(dt - 2, 0))]
            wv1 = p.gdma(lambda e, dt=dt, j=j: e.dma_start(
                w2s[:, j, 0, :, :],
                _ap(DwT_d.tensor, dt * 128,
                    [[D, 128], [128 * D, NC8], [1, 128]])), waits=wt)
            wv2 = p.gdma(lambda e, dt=dt, j=j: e.dma_start(
                w2s[:, j, 1, :, :],
                _ap(gwT_d.tensor, dt * 128,
                    [[D, 128], [128 * D, NC8], [1, 128]])), waits=wt)
            s2tok[dt] = [wv1, wv2]
        w2load(0)
        w2load(1)
        YPB = [(ps_a[:, 0:512], ps_b[:, 0:512], scrA[:], scrB[:]),
               (ps_c[:, 0:512], ps_s[:, 0:512], sc2A[:], sc2B[:])]
        it = 0
        for dt in range(NC8):
            j = dt % 2
            for tc in range(NT):
                t0 = tc * 512
                pa, pb, sA, sB = YPB[it % 2]
                w0 = []
                if tc == 0:
                    w0 += s2tok[dt]
                if it < 2:
                    w0.append(("v", scan_av))
                if it - 2 in y_rv:
                    w0.append(("v", y_rv[it - 2]))
                p.t(lambda e, dt=dt, t0=t0, pa=pa: e.matmul(
                    pa, cwb[:, dt, :], hsf[:, t0:t0 + 512],
                    start=True, stop=False), waits=w0)
                for c in range(NC8):
                    p.t(lambda e, c=c, t0=t0, j=j, pa=pa: e.matmul(
                        pa, w2s[:, j, 0, c, :], bufB[:, c, t0:t0 + 512],
                        start=False, stop=(c == NC8 - 1)))
                for c in range(NC8):
                    p.t(lambda e, c=c, t0=t0, j=j, pb=pb: e.matmul(
                        pb, w2s[:, j, 1, c, :], bufB[:, c, t0:t0 + 512],
                        start=(c == 0), stop=(c == NC8 - 1)))
                mv = p.c("t")
                sw = [("t", mv)]
                if it - 2 in y_dv2:
                    sw.append(("v", y_dv2[it - 2]))
                sgv = p.s(lambda e, dt=dt, pb=pb, sB=sB: e.activation(
                    sB, pb, AF.Sigmoid,
                    bias=b2b[:, 22 + dt:23 + dt]), waits=sw)
                y_dv2[it] = p.v(lambda e, dt=dt, pa=pa, sA=sA, sB=sB:
                    e.scalar_tensor_tensor(
                        sA, pa, cdb[:, dt:dt + 1], sB,
                        AL.add, AL.mult), waits=[("s", sgv), ("v", cdv)])
                y_rv[it] = p.v(lambda e, dt=dt, t0=t0, sA=sA: e.tensor_tensor(
                    xres[:, dt, t0:t0 + 512], xres[:, dt, t0:t0 + 512],
                    sA, AL.add))
                it += 1
            s2cnt[dt] = p.c("t")
            w2load(dt + 2)
        p.wait("t", [("v", y_rv[it - 1]), ("v", y_rv[it - 2])])


    def block3_moe_pre():
        p.barrier()
        CAP = T // 8
        SG = T // 8
        lnv = layer_norm(bufB)
        gw = p.gdma(lambda e: e.dma_start(
            gwT3b[:], _ap(gwT3_d.tensor, 0, [[E, 128], [128 * E, NC8], [1, E]])))
        xst = hshb[:, 0:2, :].rearrange("p a b -> p (a b)")
        for tt in range(NTT):
            p.wait("t", [("v", p.c("v"))])
            for c in range(NC8):
                p.t(lambda e, tt=tt, c=c: e.transpose(
                    ps_t[:, c * 128:c * 128 + 128],
                    bufB[:, c, tt * 128:tt * 128 + 128], IDENT))
            trv = p.c("t")
            ev = p.v(lambda e: e.tensor_copy(xst, ps_t[:]), waits=[("t", trv)])
            dv2 = p.ydma(lambda e, tt=tt: e.dma_start(
                _ap(xln_h.tensor, tt * 128 * D, [[D, 128], [1, D]]), xst),
                waits=[("v", ev)])
            p.wait("v", [dv2])
        p.wait("t", [gw])
        for tt in range(NTT):
            for c in range(NC8):
                p.t(lambda e, tt=tt, c=c: e.matmul(
                    ps_c[:, 0:E], bufB[:, c, tt * 128:tt * 128 + 128],
                    gwT3b[:, c, :], start=(c == 0), stop=(c == NC8 - 1)))
            mv = p.c("t")
            p.v(lambda e: e.reduce_max(rbuf[:, 0:1], ps_c[:, 0:E], AX.X,
                                       negate=True), waits=[("t", mv)],
                cx=True)
            vv = p.c("v")
            sv = p.s(lambda e, tt=tt: e.activation(
                sctm[:, tt, :], ps_c[:, 0:E], AF.Exp, bias=rbuf[:, 0:1],
                accum_out=lbuf[:, 0:1]), waits=[("v", vv), ("t", mv)])
            p.v(lambda e: e.reciprocal(rbuf[:, 1:2], lbuf[:, 0:1]),
                waits=[("s", sv)])
            scv = p.v(lambda e, tt=tt: e.tensor_scalar_mul(
                sctm[:, tt, :], sctm[:, tt, :], rbuf[:, 1:2]))
            p.wait("t", [("v", scv)])
        for tg in range(NTT // 4):
            for j in range(4):
                tt = tg * 4 + j
                p.t(lambda e, tt=tt, j=j: e.transpose(
                    ps_b[0:16, j * 128:j * 128 + 128],
                    sctm[:, tt, :], IDENTF))
            trv = p.c("t")
            ev = p.v(lambda e, tg=tg: e.tensor_copy(
                semb[:, tg * 512:tg * 512 + 512], ps_b[0:16, :]),
                waits=[("t", trv)])
            p.wait("t", [("v", ev)])
        sh_tok = {}
        sh_cnt = {}
        sd_tok = {}
        sd_cnt = {}
        chain = {}
        NQ = FSH // 256

        def shload(gq):
            if gq >= NT * NQ:
                return
            qq = gq % NQ
            j = gq % 2
            wt = [("t", sh_cnt.get(gq - 2, 0))]
            w1 = p.gdma(lambda e, qq=qq, j=j: e.dma_start(
                wsgb[:, j, :, :], _ap(WsgT_d.tensor, qq * 256,
                    [[FSH, 128], [128 * FSH, NC8], [1, 256]])), waits=wt)
            w2 = p.gdma(lambda e, qq=qq, j=j: e.dma_start(
                wsub[:, j, :, :], _ap(WsuT_d.tensor, qq * 256,
                    [[FSH, 128], [128 * FSH, NC8], [1, 256]])), waits=wt)
            sh_tok[gq] = [w1, w2]

        def sdload(gd):
            if gd >= NT * NC8:
                return
            dt = gd % NC8
            j = gd % 2
            sd_tok[gd] = [p.gdma(lambda e, dt=dt, j=j: e.dma_start(
                wsdb[:, j, :, :], _ap(WsdT_d.tensor, dt * 128,
                    [[D, 128], [128 * D, 16], [1, 128]])),
                waits=[("t", sd_cnt.get(gd - 2, 0))])]
        shload(0)
        shload(1)
        sdload(0)
        sdload(1)
        SPB = [(ps_a[:, 0:512], ps_b[:, 0:512], scrB[:]),
               (ps_c[:, 0:512], ps_s[:, 0:512], scrC[:])]
        gch = 0
        for tc in range(NT):
            t0 = tc * 512
            last_hv = 0
            for qq in range(NQ):
                gq = tc * NQ + qq
                j = gq % 2
                for ft in range(2):
                    pg, pu, sB = SPB[gch % 2]
                    w0 = []
                    if ft == 0:
                        w0 += sh_tok[gq]
                    if gch - 2 in chain:
                        w0.append(("v", chain[gch - 2]))
                    for c in range(NC8):
                        p.t(lambda e, c=c, ft=ft, j=j, t0=t0, pg=pg: e.matmul(
                            pg, wsgb[:, j, c, ft * 128:ft * 128 + 128],
                            bufB[:, c, t0:t0 + 512],
                            start=(c == 0), stop=(c == NC8 - 1)),
                            waits=(w0 if c == 0 else []))
                    gv2 = p.c("t")
                    for c in range(NC8):
                        p.t(lambda e, c=c, ft=ft, j=j, t0=t0, pu=pu: e.matmul(
                            pu, wsub[:, j, c, ft * 128:ft * 128 + 128],
                            bufB[:, c, t0:t0 + 512],
                            start=(c == 0), stop=(c == NC8 - 1)))
                    uv2 = p.c("t")
                    sw = [("t", gv2)]
                    if gch - 2 in chain:
                        sw.append(("v", chain[gch - 2]))
                    sv = p.s(lambda e, pg=pg, sB=sB: e.activation(
                        sB, pg, AF.Gelu), waits=sw)
                    hv = p.v(lambda e, qq=qq, ft=ft, sB=sB, pu=pu:
                        e.tensor_tensor(
                            hshb[:, qq * 2 + ft, :], sB, pu, AL.mult),
                        waits=[("s", sv), ("t", uv2)])
                    chain[gch] = hv
                    last_hv = hv
                    gch += 1
                sh_cnt[gq] = p.c("t")
                shload(gq + 2)
            for dt in range(NC8):
                gd = tc * NC8 + dt
                pg, pu, sB = SPB[gch % 2]
                w0 = list(sd_tok[gd])
                if dt == 0:
                    w0.append(("v", last_hv))
                if gch - 2 in chain:
                    w0.append(("v", chain[gch - 2]))
                for fc in range(16):
                    p.t(lambda e, fc=fc, gd=gd, pg=pg: e.matmul(
                        pg, wsdb[:, gd % 2, fc, :], hshb[:, fc, :],
                        start=(fc == 0), stop=(fc == 15)),
                        waits=(w0 if fc == 0 else []))
                mv = p.c("t")
                rv = p.v(lambda e, dt=dt, t0=t0, pg=pg: e.tensor_tensor(
                    xres[:, dt, t0:t0 + 512], xres[:, dt, t0:t0 + 512],
                    pg, AL.add), waits=[("t", mv)])
                chain[gch] = rv
                gch += 1
                sd_cnt[gd] = p.c("t")
                sdload(gd + 2)
        p.wait("t", [("v", chain[gch - 1]), ("v", chain[gch - 2])])
        zv = p.v(lambda e: e.memset(smls[:], 0.0))
        lo_col = 0
        hk = 0.5
        for it in range(44):
            lo = smls[0:16, lo_col:lo_col + 1]
            lo2 = smls[0:16, 4 - lo_col:5 - lo_col]
            p.v(lambda e, lo=lo, hk=hk: e.tensor_scalar(
                smls[0:16, 3:4], lo, hk, None, AL.add))
            p.v(lambda e: e.tensor_scalar(
                junkb[:], semb[:], smls[0:16, 3:4], None, AL.is_ge, AL.add,
                accum_out=smls[0:16, 1:2]), cx=True)
            p.v(lambda e: e.tensor_scalar(
                smls[0:16, 2:3], smls[0:16, 1:2], CAP - 0.5, None, AL.is_ge))
            p.v(lambda e, lo=lo, lo2=lo2, hk=hk: e.scalar_tensor_tensor(
                lo2, smls[0:16, 2:3], hk, lo, AL.mult, AL.add))
            lo_col = 4 - lo_col
            hk *= 0.5
        LO = smls[0:16, lo_col:lo_col + 1]
        p.v(lambda e: e.tensor_scalar(mskb[:], semb[:], LO, None, AL.is_ge))
        zv = p.v(lambda e: e.memset(junkb[:], 0.0))
        p.v(lambda e: e.tensor_tensor_scan(
            semb[:], junkb[:], mskb[:], 0.0, AL.max, AL.add), cx=True)
        p.v(lambda e: e.tensor_tensor(mskb[:], mskb[:], semb[:], AL.mult))
        p.v(lambda e: e.tensor_scalar(
            junkb[:], mskb[:], CAP + 0.5, -30000.0, AL.is_gt, AL.mult))
        sl2 = p.v(lambda e: e.scalar_tensor_tensor(
            mskb[:], mskb[:], -1.0, junkb[:], AL.add, AL.add))
        p.wait("t", [("v", sl2)])
        for tg in range(NTT // 4):
            for j in range(4):
                tt = tg * 4 + j
                p.t(lambda e, tt=tt, j=j: e.transpose(
                    ps_c[:, j * 16:j * 16 + 16],
                    mskb[:, tt * 128:tt * 128 + 128], IDENTF[0:16, 0:16]))
            trv = p.c("t")
            ev = p.v(lambda e, tg=tg: e.tensor_copy(
                _ap(slottm.ap().tensor, tg * 4 * 16,
                    [[(T // 128) * 16, 128], [1, 64]]),
                ps_c[:, 0:64]), waits=[("t", trv)])
            p.wait("t", [("v", ev)])
        g1 = p.g(lambda e: e.iota(iotab[:], [[1, 256]], base=0,
                                  channel_multiplier=0,
                                  allow_small_or_imprecise_dtypes=True))
        g2 = p.g(lambda e: e.iota(ttrow[:], [[1, T // 128]], base=0,
                                  channel_multiplier=0,
                                  allow_small_or_imprecise_dtypes=True))
        g3 = p.g(lambda e: e.iota(pcol[:], [[0, 1]], base=0,
                                  channel_multiplier=1,
                                  allow_small_or_imprecise_dtypes=True))
        NTTl = T // 128
        r3t = rhs3.ap().tensor
        p.v(lambda e: e.tensor_copy(
            _ap(r3t, 0, [[NTTl * E * 3, 128], [E * 3, NTTl], [3, E]]),
            _ap(pcol.ap().tensor, 0, [[1, 128], [0, NTTl], [0, E]])),
            waits=[("g", g3)])
        p.v(lambda e: e.tensor_copy(
            _ap(r3t, 1, [[NTTl * E * 3, 128], [E * 3, NTTl], [3, E]]),
            _ap(ttrow.ap().tensor, 0, [[NTTl, 128], [1, NTTl], [0, E]])),
            waits=[("g", g2)])
        r3v = p.v(lambda e: e.tensor_copy(
            _ap(r3t, 2, [[NTTl * E * 3, 128], [E * 3, NTTl], [3, E]]),
            sctm[:].rearrange("p a b -> p (a b)")))
        NH = max(CAP // 128, 1)
        p.v(lambda e: e.memset(idw[:], 0.0))
        p.v(lambda e: e.memset(idc[:], 0.0))
        for ex in range(E):
            for tt in range(NTTl):
                pv2 = p.v(lambda e, ex=ex, tt=tt: e.tensor_scalar(
                    pbt[:, 0:CAP], iotab[:, 0:CAP],
                    slottm[:, tt, ex:ex + 1], None, AL.is_equal),
                    waits=[("g", g1)])
                p.wait("t", [("v", pv2)])
                for hf in range(NH):
                    tgt = ps_c if hf == 0 else ps_b
                    p.t(lambda e, ex=ex, tt=tt, hf=hf, tgt=tgt: e.matmul(
                        tgt[0:min(CAP, 128), 0:3],
                        pbt[:, hf * 128:hf * 128 + min(CAP, 128)],
                        rhs3[:, tt, ex, :],
                        start=(tt == 0), stop=(tt == NTTl - 1)))
                mmv = p.c("t")
                if tt < NTTl - 1:
                    p.wait("v", [("t", mmv)])
            mmv = p.c("t")
            for hf in range(NH):
                tgt = ps_c if hf == 0 else ps_b
                p.v(lambda e, ex=ex, hf=hf, tgt=tgt: e.tensor_copy(
                    idw[0:min(CAP, 128), ex, hf, :], tgt[0:min(CAP, 128), 0:3]),
                    waits=[("t", mmv)])
            p.wait("t", [("v", p.c("v"))])
        idwt = idw.ap().tensor
        idcv = p.v(lambda e: e.scalar_tensor_tensor(
            idc[:].rearrange("p a b -> p (a b)"),
            _ap(idwt, 1, [[E * 6, 128], [3, E * 2]]), 128.0,
            _ap(idwt, 0, [[E * 6, 128], [3, E * 2]]), AL.mult, AL.add))
        uv32 = p.v(lambda e: e.tensor_copy(
            iu32[:].rearrange("p a b -> p (a b)"),
            idc[:].rearrange("p a b -> p (a b)")), waits=[("v", idcv)])
        if CAP < 128:
            uv32 = p.v(lambda e: e.tensor_copy(
                iu32[64:128, :, :].rearrange("p a b -> p (a b)"),
                _ap(pcol.ap().tensor, 64 * 1, [[1, 64], [0, E * 2]])))

    def block3_moe_experts():
        p.barrier()
        CAP = T // 8
        NST = CAP // 128
        assert CAP == 256 and NST == 2
        xlnap = _ap(xln_h.tensor, 0, [[D, T], [1, D]])
        outap = _ap(out.tensor, 0, [[D, T], [1, D]])
        IU32V = p.c("v")
        gath_toks = {}
        tfree = {}
        mm_tcnt = {}
        dn_tcnt = {}
        wl_toks = {}
        wd_toks = {}
        gbank_ev = {}
        dbank_ev = {}
        sc_toks = []
        last_hv = {}
        last_ev = {}

        def gath(ex):
            if ex >= E:
                return
            toks = []
            b = ex % 2
            for st in range(NST):
                toks.append(p.gdma(lambda e, ex=ex, st=st, b=b:
                    e.indirect_dma_start(
                        xtokt[:, b, st, :], None, xlnap,
                        bass.IndirectOffsetOnAxis(iu32[0:128, ex, st:st + 1], 0)),
                    waits=[("v", IU32V), ("t", tfree.get(ex - 2, 0))]))
            gath_toks[ex] = toks

        def wl(ex, i):
            if ex >= E:
                return
            q = 4 * ex + i
            j = q % 2
            wt = [("t", mm_tcnt.get(q - 2, 0))]
            t1 = p.gdma(lambda e, ex=ex, i=i, j=j: e.dma_start(
                wgeb[:, j, :, :],
                _ap(WgT_d.tensor, ex * D * FEXP + i * 512,
                    [[FEXP, 128], [128 * FEXP, NC8], [1, 512]])), waits=wt)
            t2 = p.gdma(lambda e, ex=ex, i=i, j=j: e.dma_start(
                wueb[:, j, :, :],
                _ap(WuT_d.tensor, ex * D * FEXP + i * 512,
                    [[FEXP, 128], [128 * FEXP, NC8], [1, 512]])), waits=wt)
            wl_toks[q] = [t1, t2]

        def wd(ex, j):
            if ex >= E:
                return
            r = 4 * ex + j
            jj = r % 2
            wd_toks[r] = [p.gdma(lambda e, ex=ex, j=j, jj=jj: e.dma_start(
                wdeb[:, jj, :, :],
                _ap(WdT_d.tensor, ex * FEXP * D + j * 256,
                    [[D, 128], [128 * D, 16], [1, 256]])),
                waits=[("t", dn_tcnt.get(r - 2, 0))])]

        def tstage(ex):
            b = ex % 2
            ev = 0
            for st in range(NST):
                if st == 0:
                    w = list(gath_toks[ex]) + [("v", last_ev.get(ex - 1, 0))]
                else:
                    w = [("v", ev)]
                for c in range(NC8):
                    p.t(lambda e, st=st, c=c, b=b: e.transpose(
                        ps_t[:, c * 128:c * 128 + 128],
                        xtokt[:, b, st, c * 128:c * 128 + 128], IDENT),
                        waits=(w if c == 0 else []))
                trv = p.c("t")
                ev = p.v(lambda e, st=st, b=b: e.tensor_copy(
                    _ap(xtokf.ap().tensor, b * (NC8 * CAP) + st * 128,
                        [[2 * NC8 * CAP, 128], [CAP, NC8], [1, 128]]),
                    _ap(ps_t.ap().tensor, 0,
                        [[1024, 128], [128, NC8], [1, 128]])),
                    waits=[("t", trv)])
            tfree[ex] = p.c("t")
            last_ev[ex] = ev

        GB = [(ps_a[:, 0:CAP], ps_b[:, 0:CAP]),
              (ps_s[:, 0:CAP], ps_s[:, 512:512 + CAP])]

        def mm(ex, i):
            b = ex % 2
            q = 4 * ex + i
            j = q % 2
            for ft in range(4):
                g = 4 * i + ft
                gg = 16 * ex + g
                pg, pu = GB[gg % 2]
                w0 = []
                if ft == 0:
                    w0 = wl_toks[q] + [("v", last_ev[ex])]
                if gg - 2 in gbank_ev:
                    w0 = w0 + [("v", gbank_ev[gg - 2])]
                for c in range(NC8):
                    p.t(lambda e, c=c, ft=ft, j=j, b=b, pg=pg: e.matmul(
                        pg, wgeb[:, j, c, ft * 128:ft * 128 + 128],
                        xtokf[:, b, c, :], start=(c == 0),
                        stop=(c == NC8 - 1)), waits=(w0 if c == 0 else []))
                gv2 = p.c("t")
                for c in range(NC8):
                    p.t(lambda e, c=c, ft=ft, j=j, b=b, pu=pu: e.matmul(
                        pu, wueb[:, j, c, ft * 128:ft * 128 + 128],
                        xtokf[:, b, c, :], start=(c == 0),
                        stop=(c == NC8 - 1)))
                uv2 = p.c("t")
                sv = p.s(lambda e, pg=pg: e.activation(
                    scrB[:, 0:CAP], pg, AF.Gelu),
                    waits=[("t", gv2)])
                hv = p.v(lambda e, g=g, b=b, pu=pu: e.tensor_tensor(
                    heb[:, b, g, :], scrB[:, 0:CAP],
                    pu, AL.mult), waits=[("s", sv), ("t", uv2)])
                gbank_ev[gg] = hv
                last_hv[ex] = hv
            mm_tcnt[q] = p.c("t")

        DB = [ps_c[:, 0:256], ps_s[:, 1024:1280]]

        def dn(ex, j):
            b = ex % 2
            r = 4 * ex + j
            jj = r % 2
            for st in range(NST):
                gd = 2 * r + st
                pd = DB[gd % 2]
                w0 = []
                if st == 0:
                    w0 = wd_toks[r] + [("v", last_hv[ex])]
                if gd - 2 in dbank_ev:
                    w0 = w0 + [("v", dbank_ev[gd - 2])]
                for fc in range(16):
                    p.t(lambda e, fc=fc, st=st, jj=jj, b=b, pd=pd: e.matmul(
                        pd, heb[:, b, fc, st * 128:st * 128 + 128],
                        wdeb[:, jj, fc, :], start=(fc == 0), stop=(fc == 15)),
                        waits=(w0 if fc == 0 else []))
                mv = p.c("t")
                wv = [("t", mv)]
                if j == 0 and st == 0 and sc_toks:
                    wv = wv + sc_toks[-2:]
                evv = p.v(lambda e, st=st, j=j, ex=ex, pd=pd: e.tensor_scalar_mul(
                    eob[0:128, st, j * 256:j * 256 + 256], pd,
                    idw[0:128, ex, st, 2:3]), waits=wv)
                dbank_ev[gd] = evv
            dn_tcnt[r] = p.c("t")

        def sc(ex):
            vlast = p.c("v")
            for st in range(NST):
                wv = [("v", vlast)] + (sc_toks[-1:] if sc_toks else [])
                sc_toks.append(p.gdma(lambda e, ex=ex, st=st:
                    e.indirect_dma_start(
                        outap,
                        bass.IndirectOffsetOnAxis(iu32[0:128, ex, st:st + 1], 0),
                        eob[0:128, st, :], None, compute_op=AL.add),
                    waits=wv))

        gath(0)
        wl(0, 0)
        wl(0, 1)
        for ex in range(E):
            tstage(ex)
            mm(ex, 0)
            wl(ex, 2)
            mm(ex, 1)
            wl(ex, 3)
            wd(ex, 0)
            wd(ex, 1)
            mm(ex, 2)
            mm(ex, 3)
            wl(ex + 1, 0)
            wl(ex + 1, 1)
            dn(ex, 0)
            wd(ex, 2)
            dn(ex, 1)
            wd(ex, 3)
            dn(ex, 2)
            gath(ex + 1)
            dn(ex, 3)
            sc(ex)

    if 1 in blocks:
        qg_state["g"] = quant_gen()
    if 0 in blocks:
        block0_attn()
    if 1 in blocks:
        block1_ffn()
    if 2 in blocks:
        block2_ssm()
    if 3 in blocks:
        block3_moe_pre()

    def store_output():
        p.barrier()
        for tt in range(NTT):
            vv = 0
            for c in range(NC8):
                cc = c % 4
                trv = p.t(lambda e, tt=tt, c=c, cc=cc: e.transpose(
                    ps_s[:, cc * 512:cc * 512 + 128],
                    xres[:, c, tt * 128:tt * 128 + 128], IDENTF))
                vv = p.v(lambda e, c=c, cc=cc: e.tensor_copy(
                    tout[:, c, :], ps_s[:, cc * 512:cc * 512 + 128]),
                    waits=[("t", trv)])
                p.wait("t", [("v", vv)])
            ov = p.ydma(lambda e, tt=tt: e.dma_start(
                _ap(out.tensor, tt * 128 * D, [[D, 128], [1, D]]),
                tout[:].rearrange("p c d -> p (c d)")),
                waits=[("v", vv)])
            p.wait("v", [ov])
        p.wait("y", [("yd%d" % k, p.c("yd%d" % k)) for k in range(8)])

    store_output()
    if 3 in blocks:
        block3_moe_experts()
        snap = [(n, p.cnt[n]) for n in p.cnt if p.cnt[n] > 0]
        p.wait("y", snap)

    with nc.Block() as block:
        @block.tensor
        def _(e):
            for f in p.q["t"]:
                f(e)

        @block.vector
        def _(e):
            for f in p.q["v"]:
                f(e)

        @block.scalar
        def _(e):
            for f in p.q["s"]:
                f(e)

        @block.gpsimd
        def _(e):
            for f in p.q["g"]:
                f(e)

        @block.sync
        def _(e):
            for f in p.q["y"]:
                f(e)

    stack.close()
    return nc


def make_consts(T):
    import ml_dtypes
    ident = np.eye(128, dtype=np.float32)
    cbf = np.zeros((128, 640), np.float32)
    cbf[:, 0:128] = ident
    mdiag = np.zeros((128, 128), np.float32)
    iu = np.triu_indices(128, 1)
    mdiag[iu] = NEG
    cbf[:, 128:256] = mdiag
    cbf[:, 256:384] = NEG
    cbf[:, 384] = 1.0
    cbf[:, 512:640] = mdiag.T
    cf32 = np.zeros((128, 512), np.float32)
    cf32[0, 0:128] = 1.0
    cf32[:, 128:256] = ident
    cf32[:, 300] = 1e-5
    cf32[:, 301] = -16.0
    inv = 1.0 / (10000.0 ** (np.arange(0, DH, 2, dtype=np.float64) / DH))
    tt = np.arange(T, dtype=np.float64)
    fr = np.outer(inv, tt)
    cos = np.cos(fr)
    sin = np.sin(fr)
    trig = np.zeros((128, 2 * T), np.float32)
    for pp in range(128):
        i = (pp % 64) // 2
        trig[pp, 0:T] = cos[i]
        trig[pp, T:2 * T] = (-sin[i]) if pp % 2 == 0 else sin[i]
    return cbf.astype(ml_dtypes.bfloat16), cf32, trig.astype(ml_dtypes.bfloat16)


def kernel(**inputs):
    T = 2048
    B = 8
    x = np.asarray(inputs["L0_attn_x"])
    cbf, cf32, trig = make_consts(T)
    SG = T // 8
    cf32[:, 302] = SG * (np.arange(128) // 16)
    for p_ in range(128):
        cf32[p_, 304 + (p_ % 16)] = 1.0
    bias2 = np.zeros((128, 32), np.float32)
    bias2[:, 0] = np.asarray(inputs["L1_attn_B_b"])
    bias2[:, 1] = np.asarray(inputs["L1_attn_s_b2"])
    bias2[:, 2:6] = np.asarray(inputs["L1_attn_s_b1"]).reshape(4, 128).T
    bias2[:, 6:14] = np.asarray(inputs["L1_attn_C_b"]).reshape(8, 128).T
    bias2[:, 14:22] = np.asarray(inputs["L1_attn_D_b"]).reshape(8, 128).T
    bias2[:, 22:30] = np.asarray(inputs["L1_attn_g_b"]).reshape(8, 128).T
    nc = build_program(T=T, blocks=(0, 1, 2, 3))
    shared = {
        "wqkv": np.ascontiguousarray(inputs["L0_attn_w_qkv"]),
        "wout": np.ascontiguousarray(inputs["L0_attn_w_out"]),
        "wgate": np.ascontiguousarray(inputs["L0_ffn_w_gate"]),
        "wup": np.ascontiguousarray(inputs["L0_ffn_w_up"]),
        "wdown": np.ascontiguousarray(inputs["L0_ffn_w_down"]),
        "At": np.ascontiguousarray(np.asarray(inputs["L1_attn_A"]).T),
        "BwT": np.ascontiguousarray(np.asarray(inputs["L1_attn_B_w"]).T),
        "CwT": np.ascontiguousarray(np.asarray(inputs["L1_attn_C_w"]).T),
        "DwT": np.ascontiguousarray(np.asarray(inputs["L1_attn_D_w"]).T),
        "sw1T": np.ascontiguousarray(np.asarray(inputs["L1_attn_s_w1"]).T),
        "sw2T": np.ascontiguousarray(np.asarray(inputs["L1_attn_s_w2"]).T),
        "gwT": np.ascontiguousarray(np.asarray(inputs["L1_attn_g_w"]).T),
        "bias2": bias2,
        "gatewT": np.ascontiguousarray(np.asarray(inputs["L1_ffn_gate_w"]).T),
        "WgT": np.ascontiguousarray(np.asarray(inputs["L1_ffn_Wg"]).transpose(0, 2, 1)),
        "WuT": np.ascontiguousarray(np.asarray(inputs["L1_ffn_Wu"]).transpose(0, 2, 1)),
        "WdT": np.ascontiguousarray(np.asarray(inputs["L1_ffn_Wd"]).transpose(0, 2, 1)),
        "WsgT": np.ascontiguousarray(np.asarray(inputs["L1_ffn_Ws_g"]).T),
        "WsuT": np.ascontiguousarray(np.asarray(inputs["L1_ffn_Ws_u"]).T),
        "WsdT": np.ascontiguousarray(np.asarray(inputs["L1_ffn_Ws_d"]).T),
        "trig": trig, "cbf": cbf, "cf32": cf32,
    }
    in_maps = []
    for b in range(B):
        m = dict(shared)
        m["xT"] = np.ascontiguousarray(x[b].T)
        in_maps.append(m)
    res = run_bass_kernel_spmd(nc, in_maps, core_ids=list(range(B)))
    return np.stack([r["out"] for r in res.results], axis=0)


# version 14 (final)
# speedup vs baseline: 1.0758x; 1.0330x over previous
import numpy as np
from contextlib import ExitStack

import concourse.bass as bass
import concourse.mybir as mybir
from concourse.bass_utils import run_bass_kernel_spmd
from concourse import library_config as LC

F32 = mybir.dt.float32
BF16 = mybir.dt.bfloat16
I16 = mybir.dt.int16
I32 = mybir.dt.int32
AF = mybir.ActivationFunctionType
AL = mybir.AluOpType
AX = mybir.AxisListType

D = 1024
NC8 = 8
H = 16
DH = 64
FF0 = 2816
DS = 128
E = 16
FEXP = 2048
FSH = 2048
NEG = -1.0e30

SWAP_MASK = [(i ^ 1) for i in range(32)]


class P:

    def __init__(self, nc, stack):
        self.nc = nc
        self.stack = stack
        self.q = {k: [] for k in ("t", "v", "s", "g", "y")}
        self.sem = {}
        self.cnt = {}
        self.gd_rr = 0
        self.yd_rr = 0
        self.prev_cx = False

    def mksem(self, name):
        h = self.stack.enter_context(self.nc.semaphore(name))
        self.sem[name] = h
        self.cnt[name] = 0
        return h

    SELFWAIT = "v"

    def emit(self, eng, fn, waits=(), inc=None, by=1, cx=False):
        h = None
        swl = []
        if eng == "v" and inc == eng and self.cnt[eng] > 0 and (
                cx or self.prev_cx):
            swl = [(self.sem[eng], self.cnt[eng])]
        if eng == "v":
            self.prev_cx = cx
        if eng in P.SELFWAIT and inc == eng and self.cnt[eng] > 0:
            swl = [(self.sem[eng], self.cnt[eng])]
        if inc is not None:
            self.cnt[inc] += by
            h = self.sem[inc]
        wl = swl + [(self.sem[n], v) for (n, v) in waits if v > 0]

        def op(e, fn=fn, wl=wl, h=h, by=by):
            for (s, v) in wl:
                e.wait_ge(s, v)
            inst = fn(e)
            if h is not None:
                inst.then_inc(h, by)
        self.q[eng].append(op)
        return self.cnt[inc] if inc is not None else None

    def t(self, fn, waits=()):
        return self.emit("t", fn, waits, inc="t")

    def v(self, fn, waits=(), cx=False):
        return self.emit("v", fn, waits, inc="v", cx=cx)

    def s(self, fn, waits=()):
        return self.emit("s", fn, waits, inc="s")

    def g(self, fn, waits=()):
        return self.emit("g", fn, waits, inc="g")

    def gdma(self, fn, waits=()):
        k = self.gd_rr
        self.gd_rr = (k + 1) % 8
        name = "gd%d" % k
        prior = self.cnt[name]
        w = list(waits) + ([(name, prior)] if prior else [])
        self.emit("g", fn, w, inc=name, by=16)
        return (name, self.cnt[name])

    def ydma(self, fn, waits=()):
        k = self.yd_rr
        self.yd_rr = (k + 1) % 8
        name = "yd%d" % k
        prior = self.cnt[name]
        w = list(waits) + ([(name, prior)] if prior else [])
        self.emit("y", fn, w, inc=name, by=16)
        return (name, self.cnt[name])

    def wait(self, eng, waits):
        self.emit(eng, lambda e: None, waits, inc=None)

    def barrier(self):
        snap = [(n, self.cnt[n]) for n in self.cnt]
        for eng in ("t", "v", "s", "g", "y"):
            self.wait(eng, [(n, v) for (n, v) in snap if v > 0])

    def c(self, name):
        return self.cnt[name]


def _ap(t, offset, dims):
    return bass.AP(t, offset, dims)


def build_program(T=2048, blocks=(0, 1, 2, 3)):
    nc = bass.Bass("TRN2", target_bir_lowering=False)
    stack = ExitStack()
    NT = T // 512
    NTT = T // 128

    xT = nc.dram_tensor("xT", [D, T], F32, kind="ExternalInput").ap()
    out = nc.dram_tensor("out", [T, D], F32, kind="ExternalOutput").ap()
    wqkv = nc.dram_tensor("wqkv", [D, 3 * D], F32, kind="ExternalInput").ap()
    wout_d = nc.dram_tensor("wout", [D, D], F32, kind="ExternalInput").ap()
    trig_d = nc.dram_tensor("trig", [128, 2 * T], BF16, kind="ExternalInput").ap()
    cbf_d = nc.dram_tensor("cbf", [128, 640], BF16, kind="ExternalInput").ap()
    cf32_d = nc.dram_tensor("cf32", [128, 512], F32, kind="ExternalInput").ap()
    if 3 in blocks:
        gwT3_d = nc.dram_tensor("gatewT", [D, E], F32, kind="ExternalInput").ap()
        WgT_d = nc.dram_tensor("WgT", [E, D, FEXP], F32, kind="ExternalInput").ap()
        WuT_d = nc.dram_tensor("WuT", [E, D, FEXP], F32, kind="ExternalInput").ap()
        WdT_d = nc.dram_tensor("WdT", [E, FEXP, D], F32, kind="ExternalInput").ap()
        WsgT_d = nc.dram_tensor("WsgT", [D, FSH], F32, kind="ExternalInput").ap()
        WsuT_d = nc.dram_tensor("WsuT", [D, FSH], F32, kind="ExternalInput").ap()
        WsdT_d = nc.dram_tensor("WsdT", [FSH, D], F32, kind="ExternalInput").ap()
        xln_h = nc.dram_tensor("xln_h", [T, D], BF16).ap()
    if 2 in blocks:
        At_d = nc.dram_tensor("At", [DS, DS], F32, kind="ExternalInput").ap()
        BwT_d = nc.dram_tensor("BwT", [D, DS], F32, kind="ExternalInput").ap()
        CwT_d = nc.dram_tensor("CwT", [DS, D], F32, kind="ExternalInput").ap()
        DwT_d = nc.dram_tensor("DwT", [D, D], F32, kind="ExternalInput").ap()
        sw1T_d = nc.dram_tensor("sw1T", [D, 512], F32, kind="ExternalInput").ap()
        sw2T_d = nc.dram_tensor("sw2T", [512, DS], F32, kind="ExternalInput").ap()
        gwT_d = nc.dram_tensor("gwT", [D, D], F32, kind="ExternalInput").ap()
        bias2_d = nc.dram_tensor("bias2", [128, 32], F32, kind="ExternalInput").ap()
    if 1 in blocks:
        wgate_d = nc.dram_tensor("wgate", [FF0, D], F32, kind="ExternalInput").ap()
        wup_d = nc.dram_tensor("wup", [FF0, D], F32, kind="ExternalInput").ap()
        wdown_d = nc.dram_tensor("wdown", [D, FF0], F32, kind="ExternalInput").ap()
        wgq_h = nc.dram_tensor("wgq_h", [D, FF0], BF16).ap()
        wuq_h = nc.dram_tensor("wuq_h", [D, FF0], BF16).ap()
        wdq_h = nc.dram_tensor("wdq_h", [FF0, D], BF16).ap()

    BASE = (nc.sbuf_base + 31) // 32 * 32
    TOP = nc.sbuf_top
    cur = [BASE]
    offs = {}

    def at(name, shape, dt, offset=None):
        nbytes = int(np.prod(shape[1:])) * {F32: 4, BF16: 2, I32: 4, I16: 2}[dt]
        if offset is None:
            offset = cur[0]
            cur[0] = (offset + nbytes + 31) // 32 * 32
        assert offset + nbytes <= TOP, (name, offset + nbytes, TOP)
        offs[name] = offset
        return nc.alloc_sbuf_tensor_at(name, shape, dt, offset=offset)

    xres = at("xres", [128, NC8, T], F32)
    bufA = at("bufA", [128, NC8, T], BF16)
    bufB = at("bufB", [128, NC8, T], BF16)
    scrA = at("scrA", [128, 512], F32)
    scrB = at("scrB", [128, 512], F32)
    scrC = at("scrC", [128, 512], F32)
    cb = at("cb", [128, 640], BF16)
    cf = at("cf", [128, 512], F32)
    rows = at("rows", [1, 1536], F32)
    lbuf = at("lbuf", [128, 8], F32)
    rbuf = at("rbuf", [128, 8], F32)
    sqs = at("sqs", [128, 512], BF16)
    PH = cur[0]

    trg = at("trg", [128, 2, T], BF16, PH)
    qb = at("qb", [128, 2, T], BF16, PH + 8192)
    kb = at("kb", [128, 2, T], BF16, PH + 16384)
    vtm = at("vtm", [128, 2, NTT, 130], BF16, PH + 24576)
    pts = at("pts", [128, 2, 512], BF16, PH + 32960)
    oacc = at("oacc", [128, 512], F32, PH + 35008)
    wpair = at("wpair", [128, 2, 3, NC8, 128], BF16, PH + 37056)
    wobuf = at("wobuf", [128, NC8, 512], BF16, PH)
    tout = at("tout", [128, NC8, 128], F32, offs["bufB"])

    if 1 in blocks:
        qwin = at("qwin", [128, 2, 512], F32, PH + 49344)
        qsA = at("qsA", [128, 512], F32, PH + 53440)
        qsB = at("qsB", [128, 512], F32, PH + 55488)
        qsg = at("qsg", [128, 512], BF16, PH + 57536)
        qbm = at("qbm", [128, 5, 32], F32, PH + 58560)
        qiA = at("qiA", [128, 512], BF16, PH + 59200)
        qiB = at("qiB", [128, 512], BF16, PH + 60224)
        qow = at("qow", [128, 2, 512], BF16, PH + 61248)
        qstage = at("qstage", [128, 2, 512], BF16, (TOP - 2048) // 32 * 32)
        ysb = at("ysb", [128, 512], F32, PH)
        qyb = at("qyb", [128, 512], BF16, PH + 2048)
        sgbuf = at("sgbuf", [128, 512], F32, PH + 3072)
        hstage = at("hstage", [128, 512], F32, PH + 5120)
        wgbuf = at("wgbuf", [128, 2, NC8, 512], BF16, PH + 7168)
        wubuf = at("wubuf", [128, 2, NC8, 512], BF16, PH + 23552)
        wdbuf = at("wdbuf", [128, 2, 22, 128], BF16, PH + 39936)
        sgb2 = at("sgb2", [128, 512], F32, PH + 51200)
        hst2 = at("hst2", [128, 512], F32, (TOP - 4096) // 32 * 32)
        hfm_off = offs["bufA"]
        hfm = at("hfm", [128, 22, 512], BF16, hfm_off)
    if 2 in blocks:
        NCH = T // 64
        a1b = at("a1b", [128, 4, T], BF16, PH)
        ssb = at("ssb", [128, T], F32, PH + 8 * T)
        usb = at("usb", [128, T], F32, PH + 12 * T)
        hsf = at("hsf", [128, T], BF16, PH + 16 * T)
        sw1b = at("sw1b", [128, NC8, 512], BF16, PH + 18 * T)
        sw2b = at("sw2b", [128, 4, 128], BF16, PH + 18 * T + 8192)
        bwb = at("bwb", [128, NC8, 128], BF16, PH + 18 * T + 9216)
        cwb = at("cwb", [128, NC8, 128], BF16, PH + 18 * T + 11264)
        atb = at("atb", [128, 128], BF16, PH + 18 * T + 13312)
        a64b = at("a64b", [128, 128], BF16, PH + 18 * T + 13568)
        apw = at("apw", [128, 128], BF16, PH + 18 * T + 13824)
        b2b = at("b2b", [128, 32], F32, PH + 18 * T + 14080)
        cdb = at("cdb", [128, 8], F32, PH + 18 * T + 14208)
        w2s = at("w2s", [128, 2, 2, NC8, 128], BF16, PH + 18 * T + 14336)
        hstart = at("hstart", [128, NCH + 1], BF16, PH + 18 * T + 22528)
        sc2A = at("sc2A", [128, 512], F32, PH + 18 * T + 22656)
        sc2B = at("sc2B", [128, 512], F32, PH + 18 * T + 24704)
    if 3 in blocks:
        CAP = T // 8
        SG = T // 8
        NST3 = max(CAP // 128, 1)
        o3 = [PH]

        def nxt(nbytes):
            r = o3[0]
            o3[0] = (r + nbytes + 31) // 32 * 32
            return r
        wsgb = at("wsgb", [128, 2, NC8, 256], BF16, nxt(8192))
        wsub = at("wsub", [128, 2, NC8, 256], BF16, nxt(8192))
        wsdb = at("wsdb", [128, 2, 16, 128], BF16, nxt(8192))
        hshb = at("hshb", [128, 16, 512], BF16, nxt(16384))
        gwT3b = at("gwT3b", [128, NC8, 16], BF16, nxt(256))
        sctm = at("sctm", [128, T // 128, 16], F32, nxt((T // 128) * 64))
        slottm = at("slottm", [128, T // 128, 16], F32, nxt((T // 128) * 64))
        iotab = at("iotab", [128, 256], F32, nxt(1024))
        ttrow = at("ttrow", [128, T // 128], F32, nxt((T // 128) * 4))
        pcol = at("pcol", [128, 1], F32, nxt(32))
        rhs3 = at("rhs3", [128, T // 128, E, 3], BF16, nxt((T // 128) * 96))
        pbt = at("pbt", [128, 256], BF16, nxt(512))
        semb = at("semb", [16, T], F32, offs["bufA"])
        junkb = at("junkb", [16, T], F32, offs["bufA"] + 4 * T)
        mskb = at("mskb", [16, T], F32, offs["bufA"] + 8 * T)
        iu32 = at("iu32", [128, E, 2], I32, (TOP - 2048) // 32 * 32)
        idw = at("idw", [128, E, 2, 3], F32, (TOP - 4096) // 32 * 32)
        idc = at("idc", [128, E, 2], F32, (TOP - 5120) // 32 * 32)
        smls = at("smls", [16, 16], F32, (TOP - 5632) // 32 * 32)
        ob = [PH]

        def nxb(nbytes):
            r = ob[0]
            ob[0] = (r + nbytes + 31) // 32 * 32
            return r
        xtokt = at("xtokt", [128, 2, NST3, D], BF16, nxb(2 * NST3 * D * 2))
        xtokf = at("xtokf", [128, 2, NC8, CAP], BF16, nxb(2 * NC8 * CAP * 2))
        heb = at("heb", [128, 2, FEXP // 128, CAP], BF16, offs["bufA"])
        wdeb = at("wdeb", [128, 2, FEXP // 128, 256], BF16, nxb(2 * 16 * 256 * 2))
        eob = at("eob", [128, NST3, D], F32, nxb(NST3 * D * 4))
        wgeb = at("wgeb", [128, 2, NC8, 512], BF16, offs["bufB"])
        wueb = at("wueb", [128, 2, NC8, 512], BF16, offs["bufB"] + 16384)
    IDENT = cb[:, 0:128]
    MDIAG = cb[:, 128:256]
    MFULL = cb[:, 256:384]
    MDIAGT = cb[:, 512:640]
    ONESB = cb[:, 384:385]
    ONEROW = cf[0:1, 0:128]
    IDENTF = cf[:, 128:256]
    EPSAP = cf[0:1, 300:301]
    MBIAS = cf[:, 301:302]
    r_mm = rows[0:1, 0:512]
    r_v = rows[0:1, 512:1024]
    r_t1 = rows[0:1, 1024:1536]

    ps_s = stack.enter_context(nc.psum_tensor("ps_s", [128, 2048], F32))
    ps_a = stack.enter_context(nc.psum_tensor("ps_a", [128, 512], F32))
    ps_b = stack.enter_context(nc.psum_tensor("ps_b", [128, 512], F32))
    ps_c = stack.enter_context(nc.psum_tensor("ps_c", [128, 512], F32))
    ps_t = stack.enter_context(nc.psum_tensor("ps_t", [128, 1024], BF16))

    p = P(nc, stack)
    for name in ("t", "v", "s", "g"):
        p.mksem(name)
    for k in range(8):
        p.mksem("gd%d" % k)
        p.mksem("yd%d" % k)

    xsrc = _ap(xT.tensor, 0, [[T, 128], [128 * T, NC8], [1, T]])
    l1 = p.ydma(lambda e: e.dma_start(xres[:], xsrc))
    l2 = p.ydma(lambda e: e.dma_start(cb[:], cbf_d))
    l3 = p.ydma(lambda e: e.dma_start(cf[:], cf32_d))
    for eng in ("t", "v", "s", "g"):
        p.wait(eng, [l1, l2, l3])

    def layer_norm(dst, quant=False):
        for tc in range(NT):
            t0 = tc * 512
            for c in range(NC8):
                vv = p.v(lambda e, c=c, t0=t0: e.tensor_copy(
                    bufA[:, c, t0:t0 + 512], xres[:, c, t0:t0 + 512]))
                sv = p.s(lambda e, c=c, t0=t0: e.activation(
                    sqs[:], xres[:, c, t0:t0 + 512], AF.Square))
                p.t(lambda e, c=c, t0=t0: e.matmul(
                    ps_c[0:1, :], ONESB, bufA[:, c, t0:t0 + 512],
                    start=(c == 0), stop=(c == NC8 - 1)), waits=[("v", vv)])
                sv2 = p.t(lambda e, c=c: e.matmul(
                    ps_b[0:1, :], ONESB, sqs[:],
                    start=(c == 0), stop=(c == NC8 - 1)), waits=[("s", sv)])
                p.wait("s", [("t", sv2)])
            mmv = p.c("t")
            p.v(lambda e: e.tensor_scalar_mul(r_mm, ps_c[0:1, :], 1.0 / D),
                waits=[("t", mmv)])
            p.v(lambda e: e.tensor_scalar_mul(r_v, ps_b[0:1, :], 1.0 / D))
            p.v(lambda e: e.tensor_tensor(r_t1, r_mm, r_mm, AL.mult))
            vv2 = p.v(lambda e: e.tensor_tensor(r_v, r_v, r_t1, AL.subtract))
            sv2 = p.s(lambda e: e.activation(r_v, r_v, AF.Sqrt, bias=EPSAP),
                      waits=[("v", vv2)])
            p.v(lambda e: e.reciprocal(r_t1, r_v), waits=[("s", sv2)])
            vv3 = p.v(lambda e: e.scalar_tensor_tensor(
                r_v, r_mm, -1.0, r_t1, AL.mult, AL.mult))
            p.t(lambda e: e.matmul(ps_a[:], ONEROW, r_t1, start=True, stop=True),
                waits=[("v", vv3)])
            tb2 = p.t(lambda e: e.matmul(ps_b[:], ONEROW, r_v,
                                         start=True, stop=True))
            lastv = 0
            for c in range(NC8):
                p.v(lambda e, c=c, t0=t0: e.tensor_tensor(
                    scrA[:], xres[:, c, t0:t0 + 512], ps_a[:], AL.mult),
                    waits=[("t", tb2)])
                if not quant:
                    lastv = p.v(lambda e, c=c, t0=t0: e.tensor_tensor(
                        dst[:, c, t0:t0 + 512], scrA[:], ps_b[:], AL.add))
                else:
                    lastv = p.v(lambda e, c=c, t0=t0: e.tensor_tensor(
                        scrB[:], scrA[:], ps_b[:], AL.add))
                    fp4q_y(scrB[:], dst[:, c, t0:t0 + 512])
                    lastv = p.c("v")
            p.wait("t", [("v", lastv)])
        return p.c("v")

    def fp4q(src, dst, W, waits0=(), waitsd=()):
        NB = W // 16
        bm = qbm[:, 0, 0:NB]
        sc = qbm[:, 1, 0:NB]
        rr = qbm[:, 2, 0:NB]
        r2 = qbm[:, 3, 0:NB]
        hs = qbm[:, 4, 0:NB]
        qt_ = qbm.ap().tensor
        s3 = _ap(src.tensor, src.offset, [list(src.ap[0]), [16, NB], [1, 16]])
        p.v(lambda e: e.tensor_reduce(bm, s3, AX.X, AL.max,
                                      apply_absolute_value=True),
            waits=list(waits0), cx=True)
        p.v(lambda e: e.tensor_scalar(sc, bm, 1.0 / 6.0, 1e-12,
                                      AL.mult, AL.max))
        p.v(lambda e: e.reciprocal(rr, sc))
        p.v(lambda e: e.tensor_scalar(r2, rr, 2.0, None, AL.mult))
        r2v = p.v(lambda e: e.tensor_scalar(hs, sc, 0.5, None, AL.mult))
        r2b = _ap(qt_, 3 * 32, [[160, 128], [1, NB], [0, 16]])
        hsb = _ap(qt_, 4 * 32, [[160, 128], [1, NB], [0, 16]])
        qsA_t = qsA.ap().tensor
        xnv = p.v(lambda e: e.tensor_tensor(
            _ap(qsA_t, 0, [[512, 128], [16, NB], [1, 16]]), s3, r2b, AL.mult),
            waits=[("s", p.c("s"))])
        sgv = p.s(lambda e: e.activation(qsg[:, 0:W], qsA[:, 0:W], AF.Sign),
                  waits=[("v", xnv)])
        av = p.v(lambda e: e.scalar_tensor_tensor(
            qsB[:, 0:W], qsA[:, 0:W], -1.0, qsA[:, 0:W], AL.mult, AL.max))
        p.v(lambda e: e.tensor_scalar(qiB[:, 0:W], qsB[:, 0:W], 5.0, None,
                                      AL.is_gt))
        for th in (7.0, 10.0, 10.0):
            p.v(lambda e, th=th: e.scalar_tensor_tensor(
                qiB[:, 0:W], qsB[:, 0:W], th, qiB[:, 0:W], AL.is_gt, AL.add))
        p.v(lambda e: e.tensor_scalar(qiA[:, 0:W], qsB[:, 0:W], 0.5, None,
                                      AL.is_gt))
        for th in (1.5, 2.5, 3.5):
            p.v(lambda e, th=th: e.scalar_tensor_tensor(
                qiA[:, 0:W], qsB[:, 0:W], th, qiA[:, 0:W], AL.is_gt, AL.add))
        cmb = p.v(lambda e: e.scalar_tensor_tensor(
            qsA[:, 0:W], qiB[:, 0:W], 2.0, qiA[:, 0:W], AL.mult, AL.add),
            waits=[("s", sgv)])
        p.v(lambda e: e.tensor_tensor(qsB[:, 0:W], qsA[:, 0:W],
                                      qsg[:, 0:W], AL.mult))
        d3 = _ap(dst.tensor, dst.offset, [list(dst.ap[0]), [16, NB], [1, 16]])
        return p.v(lambda e: e.tensor_tensor(
            d3, _ap(qsB.ap().tensor, 0, [[512, 128], [16, NB], [1, 16]]),
            hsb, AL.mult), waits=list(waitsd))

    def fp4q_y(chunk_src, chunk_dst):
        p.v(lambda e: e.transpose(ysb[:], chunk_src), cx=True)
        NB2 = 16
        bm = qbm[:, 0, 0:32]
        sc = qbm[:, 1, 0:32]
        rr = qbm[:, 2, 0:32]
        r2 = qbm[:, 3, 0:32]
        hs = qbm[:, 4, 0:32]
        qt_ = qbm.ap().tensor
        ys_t = ysb.ap().tensor
        y4 = _ap(ys_t, 0, [[512, 128], [32, NB2], [16, 2], [1, 16]])
        p.v(lambda e: e.tensor_reduce(
            _ap(qt_, 0, [[160, 128], [2, NB2], [1, 2]]), y4, AX.X, AL.max,
            apply_absolute_value=True), cx=True)
        p.v(lambda e: e.tensor_scalar(sc, bm, 1.0 / 6.0, 1e-12, AL.mult, AL.max))
        p.v(lambda e: e.reciprocal(rr, sc))
        p.v(lambda e: e.tensor_scalar(r2, rr, 2.0, None, AL.mult))
        p.v(lambda e: e.tensor_scalar(hs, sc, 0.5, None, AL.mult))
        r2b = _ap(qt_, 3 * 32, [[160, 128], [2, NB2], [1, 2], [0, 16]])
        hsb = _ap(qt_, 4 * 32, [[160, 128], [2, NB2], [1, 2], [0, 16]])
        W = 512
        qsA_t = qsA.ap().tensor
        xnv = p.v(lambda e: e.tensor_tensor(
            qsA[:, 0:W], ysb[:], r2b, AL.mult), waits=[("s", p.c("s"))])
        sgv = p.s(lambda e: e.activation(qsg[:, 0:W], qsA[:, 0:W], AF.Sign),
                  waits=[("v", xnv)])
        av = p.v(lambda e: e.scalar_tensor_tensor(
            qsB[:, 0:W], qsA[:, 0:W], -1.0, qsA[:, 0:W], AL.mult, AL.max))
        p.v(lambda e: e.tensor_scalar(qiB[:, 0:W], qsB[:, 0:W], 5.0, None,
                                      AL.is_gt))
        for th in (7.0, 10.0, 10.0):
            p.v(lambda e, th=th: e.scalar_tensor_tensor(
                qiB[:, 0:W], qsB[:, 0:W], th, qiB[:, 0:W], AL.is_gt, AL.add))
        p.v(lambda e: e.tensor_scalar(qiA[:, 0:W], qsB[:, 0:W], 0.5, None,
                                      AL.is_gt))
        for th in (1.5, 2.5, 3.5):
            p.v(lambda e, th=th: e.scalar_tensor_tensor(
                qiA[:, 0:W], qsB[:, 0:W], th, qiA[:, 0:W], AL.is_gt, AL.add))
        cmb = p.v(lambda e: e.scalar_tensor_tensor(
            qsA[:, 0:W], qiB[:, 0:W], 2.0, qiA[:, 0:W], AL.mult, AL.add),
            waits=[("s", sgv)])
        p.v(lambda e: e.tensor_tensor(qsB[:, 0:W], qsA[:, 0:W],
                                      qsg[:, 0:W], AL.mult))
        p.v(lambda e: e.tensor_tensor(qyb[:], qsB[:, 0:W], hsb, AL.mult))
        return p.v(lambda e: e.transpose(chunk_dst, qyb[:]), cx=True)


    def qkv_dma(hp, buf):
        wl = []
        tw = [("t", p.c("t"))]
        for j, base in enumerate((128 * hp, D + 128 * hp, 2 * D + 128 * hp)):
            src = _ap(wqkv.tensor, base,
                      [[3 * D, 128], [128 * 3 * D, NC8], [1, 128]])
            wl.append(p.gdma(lambda e, j=j, src=src, buf=buf: e.dma_start(
                wpair[:, buf, j, :, :], src), waits=tw))
        return wl

    def block0_attn():
        lnv = layer_norm(bufB)
        p.wait("g", [("v", lnv)])
        NG = T // 512
        vtm_t = vtm.ap().tensor
        psc_t = ps_c.ap().tensor
        tgv = p.ydma(lambda e: e.dma_start(
            trg[:].rearrange("p a t -> p (a t)"),
            _ap(trig_d.tensor, 0, [[2 * T, 128], [T, 2], [1, T]])))
        dv = qkv_dma(0, 0)
        rope_v = {-1: lnv}
        sq_end = {}
        sc_t_end = {}
        exp_s = {}
        sc_par = 0
        fin_prev = 0
        bc_prev = 0
        for hp in range(8):
            par = hp % 2
            buf = hp % 2
            guard_t = sc_t_end.get(hp - 2, 0)
            qrv = 0
            krv = 0
            vtmv_prev = rope_v[hp - 1]
            first = True
            for tc in range(NT):
                t0 = tc * 512
                w0 = [("v", vtmv_prev)]
                if first:
                    w0 = list(dv) + w0
                    first = False
                for c in range(NC8):
                    p.t(lambda e, c=c, buf=buf, t0=t0: e.matmul(
                        ps_a[:], wpair[:, buf, 0, c, :],
                        bufB[:, c, t0:t0 + 512],
                        start=(c == 0), stop=(c == NC8 - 1)),
                        waits=(w0 if c == 0 else []))
                qmv = p.c("t")
                for c in range(NC8):
                    p.t(lambda e, c=c, buf=buf, t0=t0: e.matmul(
                        ps_b[:], wpair[:, buf, 1, c, :],
                        bufB[:, c, t0:t0 + 512],
                        start=(c == 0), stop=(c == NC8 - 1)))
                kmv = p.c("t")
                for tt2 in range(4):
                    tt = 4 * tc + tt2
                    for c in range(NC8):
                        p.t(lambda e, c=c, buf=buf, tt=tt, tt2=tt2: e.matmul(
                            ps_c[:, tt2 * 128:tt2 * 128 + 128],
                            bufB[:, c, tt * 128:tt * 128 + 128],
                            wpair[:, buf, 2, c, :],
                            start=(c == 0), stop=(c == NC8 - 1)))
                vmv = p.c("t")
                for (pspk, dstb, wv_) in ((ps_a, qb, qmv), (ps_b, kb, kmv)):
                    shw = [("t", wv_)]
                    if tc == 0 and pspk is ps_a:
                        shw += [("t", guard_t), tgv]
                    shv = p.v(lambda e, pspk=pspk: e.stream_shuffle(
                        scrC[:], pspk[:], SWAP_MASK), waits=shw, cx=True)
                    p.v(lambda e, t0=t0: e.tensor_tensor(
                        scrB[:], scrC[:], trg[:, 1, t0:t0 + 512], AL.mult))
                    p.v(lambda e, pspk=pspk, t0=t0: e.tensor_tensor(
                        scrA[:], pspk[:], trg[:, 0, t0:t0 + 512], AL.mult))
                    p.v(lambda e, dstb=dstb, t0=t0, par=par: e.tensor_tensor(
                        dstb[:, par, t0:t0 + 512], scrA[:], scrB[:], AL.add))
                p.v(lambda e, tc=tc, par=par: e.tensor_copy(
                    _ap(vtm_t, par * NTT * 130 + 4 * tc * 130,
                        [[2 * NTT * 130, 128], [130, 4], [65, 2], [1, 64]]),
                    _ap(psc_t, 0, [[512, 128], [128, 4], [64, 2], [1, 64]])),
                    waits=[("t", vmv)])
                vtmv_prev = p.c("v")
            p.v(lambda e, par=par: e.memset(
                _ap(vtm_t, par * NTT * 130 + 64,
                    [[2 * NTT * 130, 128], [130, NTT], [65, 2], [1, 1]]), 1.0))
            rope_v[hp] = p.c("v")
            pump(4)
            if hp < 7:
                dv = qkv_dma(hp + 1, (hp + 1) % 2)
            first_sc = True
            for h01 in range(2):
                base = 64 * h01
                for qg in range(NG):
                    ntile = 4 * qg + 4
                    pend = []
                    for t in range(ntile):
                        pr = sc_par % 2
                        psc = ps_s[:, pr * 512:pr * 512 + 512]
                        w0 = []
                        if first_sc:
                            w0.append(("v", rope_v[hp]))
                            first_sc = False
                        if sc_par - 2 in exp_s:
                            w0.append(("s", exp_s[sc_par - 2]))
                        if t == 0:
                            w0.append(("v", fin_prev))
                        band = t - 4 * qg
                        p.t(lambda e, t=t, qg=qg, base=base, par=par, psc=psc:
                            e.matmul(
                                psc, kb[base:base + 64, par,
                                        t * 128:t * 128 + 128],
                                qb[base:base + 64, par,
                                   qg * 512:qg * 512 + 512],
                                start=True, stop=True), waits=w0)
                        if band >= 0:
                            for jj in range(band):
                                p.t(lambda e, jj=jj, psc=psc: e.matmul(
                                    psc[:, jj * 128:jj * 128 + 128], IDENT,
                                    MFULL, start=False, stop=True,
                                    skip_group_check=True))
                            p.t(lambda e, band=band, psc=psc: e.matmul(
                                psc[:, band * 128:band * 128 + 128], IDENT,
                                MDIAGT, start=False, stop=True,
                                skip_group_check=True))
                        ex = p.s(lambda e, pr=pr, psc=psc:
                                 e.activation(
                                     pts[:, pr, :], psc, AF.Exp, scale=0.125,
                                     bias=MBIAS),
                                 waits=[("t", p.c("t"))])
                        exp_s[sc_par] = ex
                        pend.append((t, pr, ex))
                        if len(pend) > 1:
                            t_, pr_, ex_ = pend.pop(0)
                            p.t(lambda e, t=t_, par=par, h01=h01, pr=pr_:
                                e.matmul(
                                    ps_s[0:65, 1024:1536],
                                    vtm[:, par, t, 65 * h01:65 * h01 + 65],
                                    pts[:, pr, :],
                                    start=(t == 0), stop=False),
                                waits=[("s", ex_)])
                        sc_par += 1
                    t_, pr_, ex_ = pend.pop(0)
                    p.t(lambda e, t=t_, par=par, h01=h01, pr=pr_, nt=ntile:
                        e.matmul(
                            ps_s[0:65, 1024:1536],
                            vtm[:, par, t, 65 * h01:65 * h01 + 65],
                            pts[:, pr, :],
                            start=(t == 0), stop=True),
                        waits=[("s", ex_)])
                    pvend = p.c("t")
                    rv0 = p.s(lambda e: e.activation(
                        r_v, ps_s[64:65, 1024:1536], AF.Ln),
                        waits=[("t", pvend)])
                    rv = p.s(lambda e: e.activation(
                        r_t1, r_v, AF.Exp, scale=-1.0))
                    oev = p.v(lambda e: e.tensor_copy(
                        oacc[0:64, :], ps_s[0:64, 1024:1536]),
                        waits=[("t", pvend)])
                    bc_prev = p.t(lambda e: e.matmul(
                        ps_s[0:64, 1536:2048], cf[0:1, 0:64], r_t1,
                        start=True, stop=True), waits=[("s", rv)])
                    fin_prev = p.v(lambda e, h01=h01, hp=hp, qg=qg:
                        e.tensor_tensor(
                            bufA[64 * h01:64 * h01 + 64, hp,
                                 qg * 512:qg * 512 + 512],
                            oacc[0:64, :], ps_s[0:64, 1536:2048],
                            AL.mult), waits=[("t", bc_prev)])
                    pump(4)
            sc_t_end[hp] = p.c("t")
        p.barrier()
        for mh in range(2):
            src = _ap(wout_d.tensor, mh * 512,
                      [[D, 128], [128 * D, NC8], [1, 512]])
            wv = p.gdma(lambda e, src=src: e.dma_start(wobuf[:], src),
                        waits=[("t", p.c("t"))])
            p.wait("t", [wv])
            for mt in range(4):
                m0c = (mh * 512 + mt * 128) // 128
                for tc in range(NT):
                    t0 = tc * 512
                    for c in range(NC8):
                        p.t(lambda e, c=c, mt=mt, t0=t0: e.matmul(
                            ps_a[:], wobuf[:, c, mt * 128:mt * 128 + 128],
                            bufA[:, c, t0:t0 + 512],
                            start=(c == 0), stop=(c == NC8 - 1)))
                    omv = p.c("t")
                    rv = p.v(lambda e, m0c=m0c, t0=t0: e.tensor_tensor(
                        xres[:, m0c, t0:t0 + 512],
                        xres[:, m0c, t0:t0 + 512], ps_a[:], AL.add),
                        waits=[("t", omv)])
                    p.wait("t", [("v", rv)])

    def quant_gen():
        units = []
        for (wsrc, wdst, RS, LD) in ((wgate_d, wgq_h, D, FF0),
                                     (wup_d, wuq_h, D, FF0)):
            for rc in range(FF0 // 128):
                for scc in range(D // 512):
                    units.append((wsrc, wdst, RS, LD, rc, scc, 512))
        for rc in range(NC8):
            for scc in range((FF0 + 511) // 512):
                kw = min(512, FF0 - scc * 512)
                units.append((wdown_d, wdq_h, FF0, D, rc, scc, kw))
        N = len(units)
        ldtok = {}
        qvdone = {}
        trt = {}
        evq = {}
        sttok = {}

        def load(i):
            if i >= N:
                return
            wsrc, wdst, RS, LD, rc, scc, kw = units[i]
            par = i % 2
            ldtok[i] = p.ydma(lambda e, wsrc=wsrc, RS=RS, rc=rc, scc=scc,
                              kw=kw, par=par: e.dma_start(
                qwin[:, par, 0:kw],
                _ap(wsrc.tensor, rc * 128 * RS + scc * 512,
                    [[RS, 128], [1, kw]])),
                waits=[("v", qvdone.get(i - 2, 0))])
        load(0)
        load(1)
        for i in range(N):
            wsrc, wdst, RS, LD, rc, scc, kw = units[i]
            par = i % 2
            qvdone[i] = fp4q(
                qwin[:, par, 0:kw],
                _ap(qow.ap().tensor, par * 512, [[1024, 128], [1, kw]]),
                kw, waits0=[ldtok[i]],
                waitsd=[("t", trt.get(i - 2, 0))])
            load(i + 1)
            yield
            for j in range(kw // 128):
                w0 = []
                if j == 0:
                    w0 = [("v", qvdone[i]), ("v", evq.get(i - 1, 0))]
                p.t(lambda e, j=j, par=par: e.transpose(
                    ps_t[:, j * 128:j * 128 + 128],
                    qow[:, par, j * 128:j * 128 + 128], IDENT), waits=w0)
            trt[i] = p.c("t")
            evq[i] = p.v(lambda e, par=par, kw=kw: e.tensor_copy(
                qstage[:, par, 0:kw], ps_t[:, 0:kw]),
                waits=[("t", trt[i])] + ([sttok[i - 2]] if i - 2 in sttok
                                         else []))
            sttok[i] = p.ydma(lambda e, wdst=wdst, LD=LD, rc=rc, scc=scc,
                              kw=kw, par=par: e.dma_start(
                _ap(wdst.tensor, scc * 512 * LD + rc * 128,
                    [[LD, 128], [128 * LD, kw // 128], [1, 128]]),
                _ap(qstage.ap().tensor, par * 512,
                    [[1024, 128], [128, kw // 128], [1, 128]])),
                waits=[("v", evq[i])])
            yield

    qg_state = {"g": None}

    def pump(n=1):
        g = qg_state["g"]
        if g is None:
            return
        for _ in range(n):
            try:
                next(g)
            except StopIteration:
                qg_state["g"] = None
                return

    def block1_ffn():
        pump(10 ** 9)
        p.barrier()
        layer_norm(bufB, quant=True)
        p.barrier()
        FCW = [512] * 5 + [256]
        FOFF = [0, 512, 1024, 1536, 2048, 2560]
        B1P = [(ps_a, ps_b, sgbuf, hstage), (ps_c, ps_s, sgb2, hst2)]
        b1ch = {}
        b1tr = {}
        b1ev = {}
        b1pend = []

        def emit_htr():
            par_, fc_, fw_, tt2_, qv_ = b1pend.pop(0)
            it_ = None
            for j in range(fw_ // 128):
                w0 = []
                if j == 0:
                    w0 = [("v", qv_), ("v", b1ev.get("last", 0))]
                p.t(lambda e, j=j, par_=par_: e.transpose(
                    ps_t[:, j * 128:j * 128 + 128],
                    qow[:, par_, j * 128:j * 128 + 128], IDENT), waits=w0)
            trv = p.c("t")
            b1tr[len(b1tr)] = trv
            b1ev["last"] = p.v(lambda e, fc=fc_, fw=fw_, tt2=tt2_:
                e.tensor_copy(
                    _ap(hfm.ap().tensor, (4 * fc) * 512 + tt2 * 128,
                        [[22 * 512, 128], [512, fw // 128], [1, 128]]),
                    ps_t[:, 0:fw]), waits=[("t", trv)])
        b1_dtok = {}
        b1_dcnt = {}
        b1_rv = {}
        b1tok = {}
        b1cnt = {}

        def gload(gq):
            if gq >= NT * 6:
                return
            fc = gq % 6
            fw = FCW[fc]
            f0 = FOFF[fc]
            j = gq % 2
            wt = [("t", b1cnt.get(gq - 2, 0))]
            d1 = p.gdma(lambda e, f0=f0, fw=fw, j=j: e.dma_start(
                wgbuf[:, j, :, 0:fw],
                _ap(wgq_h.tensor, f0,
                    [[FF0, 128], [128 * FF0, NC8], [1, fw]])), waits=wt)
            d2 = p.gdma(lambda e, f0=f0, fw=fw, j=j: e.dma_start(
                wubuf[:, j, :, 0:fw],
                _ap(wuq_h.tensor, f0,
                    [[FF0, 128], [128 * FF0, NC8], [1, fw]])), waits=wt)
            b1tok[gq] = [d1, d2]
        gload(0)
        gload(1)
        for tc in range(NT):
            t0 = tc * 512
            for fc, fw in enumerate(FCW):
                gq = tc * 6 + fc
                jq = gq % 2
                for tt2 in range(4):
                    tt = 4 * tc + tt2
                    it = gq * 4 + tt2
                    pg, pu, sB, hS = B1P[it % 2]
                    w0 = []
                    if tt2 == 0:
                        w0 += b1tok[gq]
                    if it - 2 in b1ch:
                        w0.append(("v", b1ch[it - 2]))
                    for c in range(NC8):
                        p.t(lambda e, c=c, tt=tt, fw=fw, j=jq, pg=pg: e.matmul(
                            pg[:, 0:fw], bufB[:, c, tt * 128:tt * 128 + 128],
                            wgbuf[:, j, c, 0:fw],
                            start=(c == 0), stop=(c == NC8 - 1)),
                            waits=(w0 if c == 0 else []))
                    gv2 = p.c("t")
                    for c in range(NC8):
                        p.t(lambda e, c=c, tt=tt, fw=fw, j=jq, pu=pu: e.matmul(
                            pu[:, 0:fw], bufB[:, c, tt * 128:tt * 128 + 128],
                            wubuf[:, j, c, 0:fw],
                            start=(c == 0), stop=(c == NC8 - 1)))
                    uv2 = p.c("t")
                    sw = [("t", gv2)]
                    if it - 2 in b1ch:
                        sw.append(("v", b1ch[it - 2]))
                    sv = p.s(lambda e, fw=fw, pg=pg, sB=sB: e.activation(
                        sB[:, 0:fw], pg[:, 0:fw], AF.Silu), waits=sw)
                    hv = p.v(lambda e, fw=fw, pu=pu, sB=sB, hS=hS:
                        e.tensor_tensor(
                            hS[:, 0:fw], sB[:, 0:fw], pu[:, 0:fw],
                            AL.mult), waits=[("s", sv), ("t", uv2)])
                    b1ch[it] = hv
                    qv = fp4q(hS[:, 0:fw], qow[:, it % 2, 0:fw], fw,
                              waitsd=[("t", b1tr.get(it - 2, 0))])
                    b1pend.append((it % 2, fc, fw, tt2, qv))
                    if len(b1pend) > 1:
                        emit_htr()
                b1cnt[gq] = p.c("t")
                gload(gq + 2)
            while b1pend:
                emit_htr()

            def dload(dt, tc=tc):
                if dt >= NC8:
                    return
                gd = tc * NC8 + dt
                b1_dtok[gd] = p.gdma(lambda e, dt=dt, gd=gd: e.dma_start(
                    wdbuf[:, gd % 2, :, :],
                    _ap(wdq_h.tensor, dt * 128,
                        [[D, 128], [128 * D, 22], [1, 128]])),
                    waits=[("t", b1_dcnt.get(gd - 2, 0))])
            dload(0)
            dload(1)
            for dt in range(NC8):
                gd = tc * NC8 + dt
                pbk = ps_a[:] if gd % 2 == 0 else ps_b[:]
                w0 = [b1_dtok[gd]]
                if dt == 0:
                    w0.append(("v", b1ev["last"]))
                if gd - 2 in b1_rv:
                    w0.append(("v", b1_rv[gd - 2]))
                for fc2 in range(22):
                    p.t(lambda e, fc2=fc2, gd=gd, pbk=pbk: e.matmul(
                        pbk, wdbuf[:, gd % 2, fc2, :], hfm[:, fc2, :],
                        start=(fc2 == 0), stop=(fc2 == 21)),
                        waits=(w0 if fc2 == 0 else []))
                b1_dcnt[gd] = omv = p.c("t")
                dload(dt + 2)
                b1_rv[gd] = p.v(lambda e, dt=dt, t0=t0, pbk=pbk: e.tensor_tensor(
                    xres[:, dt, t0:t0 + 512],
                    xres[:, dt, t0:t0 + 512], pbk, AL.add),
                    waits=[("t", omv)])
            p.wait("t", [("v", b1_rv[tc * NC8 + NC8 - 1]),
                         ("v", b1_rv[tc * NC8 + NC8 - 2])])

    def block2_ssm():
        p.barrier()
        NCH = T // 64
        wl = []
        wl.append(p.gdma(lambda e: e.dma_start(
            sw1b[:], _ap(sw1T_d.tensor, 0,
                         [[512, 128], [128 * 512, NC8], [1, 512]]))))
        wl.append(p.gdma(lambda e: e.dma_start(
            sw2b[:], _ap(sw2T_d.tensor, 0,
                         [[DS, 128], [128 * DS, 4], [1, 128]]))))
        wl.append(p.gdma(lambda e: e.dma_start(
            bwb[:], _ap(BwT_d.tensor, 0,
                        [[DS, 128], [128 * DS, NC8], [1, 128]]))))
        wl.append(p.gdma(lambda e: e.dma_start(
            cwb[:], _ap(CwT_d.tensor, 0,
                        [[D, 128], [128, NC8], [1, 128]]))))
        wl.append(p.gdma(lambda e: e.dma_start(atb[:], At_d)))
        wl.append(p.ydma(lambda e: e.dma_start(b2b[:], bias2_d)))
        lnv = layer_norm(bufB)
        p.wait("t", wl + [("v", lnv)])
        p.wait("v", wl[-1:])
        cdv = p.v(lambda e: e.tensor_tensor(
            cdb[:], b2b[:, 6:14], b2b[:, 14:22], AL.add))
        for tc in range(NT):
            t0 = tc * 512
            for mt in range(4):
                for c in range(NC8):
                    p.t(lambda e, c=c, mt=mt, t0=t0: e.matmul(
                        ps_a[:], sw1b[:, c, mt * 128:mt * 128 + 128],
                        bufB[:, c, t0:t0 + 512],
                        start=(c == 0), stop=(c == NC8 - 1)))
                mv = p.c("t")
                sv = p.s(lambda e, mt=mt, t0=t0: e.activation(
                    a1b[:, mt, t0:t0 + 512], ps_a[:], AF.Silu,
                    bias=b2b[:, 2 + mt:3 + mt]), waits=[("t", mv)])
                p.wait("t", [("s", sv)])
            for c4 in range(4):
                p.t(lambda e, c4=c4, t0=t0: e.matmul(
                    ps_b[:], sw2b[:, c4, :], a1b[:, c4, t0:t0 + 512],
                    start=(c4 == 0), stop=(c4 == 3)))
            mv = p.c("t")
            sv = p.s(lambda e, t0=t0: e.activation(
                ssb[:, t0:t0 + 512], ps_b[:], AF.Sigmoid,
                bias=b2b[:, 1:2]), waits=[("t", mv)])
            p.wait("t", [("s", sv)])
            for c in range(NC8):
                p.t(lambda e, c=c, t0=t0: e.matmul(
                    ps_c[:], bwb[:, c, :], bufB[:, c, t0:t0 + 512],
                    start=(c == 0), stop=(c == NC8 - 1)))
            mv = p.c("t")
            uv = p.v(lambda e, t0=t0: e.scalar_tensor_tensor(
                usb[:, t0:t0 + 512], ps_c[:], b2b[:, 0:1],
                ssb[:, t0:t0 + 512], AL.add, AL.mult),
                waits=[("t", mv), ("s", sv)])
            p.wait("t", [("v", uv)])
        p.v(lambda e: e.tensor_copy(a64b[:], atb[:]))
        for it in range(6):
            tv = p.t(lambda e: e.transpose(ps_t[:, 0:128], a64b[:], IDENT),
                     waits=[("v", p.c("v"))])
            ev = p.v(lambda e: e.tensor_copy(apw[:], ps_t[:, 0:128]),
                     waits=[("t", tv)])
            mv = p.t(lambda e: e.matmul(ps_c[:, 0:128], apw[:], a64b[:],
                                        start=True, stop=True),
                     waits=[("v", ev)])
            ev2 = p.v(lambda e: e.tensor_copy(a64b[:], ps_c[:, 0:128]),
                      waits=[("t", mv)])
            p.wait("t", [("v", ev2)])
        zv = p.v(lambda e: e.memset(hstart[:], 0.0))
        p.wait("t", [("v", zv)])
        for tl in range(64):
            mv = p.t(lambda e: e.matmul(ps_c[:, 0:NCH], atb[:],
                                        hstart[:, 0:NCH],
                                        start=True, stop=True))
            uap = _ap(usb.ap().tensor, tl, [[T, 128], [64, NCH]])
            av = p.v(lambda e, uap=uap: e.tensor_tensor(
                hstart[:, 0:NCH], ps_c[:, 0:NCH], uap, AL.add),
                waits=[("t", mv)])
            p.wait("t", [("v", av)])
        bcv = p.v(lambda e: e.tensor_copy(a1b[:, 0, 0:NCH], hstart[:, 0:NCH]))
        zv = p.v(lambda e: e.memset(hstart[:], 0.0))
        p.wait("t", [("v", p.c("v"))])
        for c in range(NCH - 1):
            mv = p.t(lambda e, c=c: e.matmul(
                ps_c[:, 0:1], a64b[:], hstart[:, c:c + 1],
                start=True, stop=True))
            av = p.v(lambda e, c=c: e.tensor_tensor(
                hstart[:, c + 1:c + 2], ps_c[:, 0:1], a1b[:, 0, c:c + 1],
                AL.add), waits=[("t", mv)])
            p.wait("t", [("v", av)])
        for tl in range(64):
            if tl == 0:
                rhs = hstart[:, 0:NCH]
            else:
                rhs = _ap(hsf.ap().tensor, tl - 1, [[T, 128], [64, NCH]])
            mv = p.t(lambda e, rhs=rhs: e.matmul(
                ps_c[:, 0:NCH], atb[:], rhs, start=True, stop=True))
            uap = _ap(usb.ap().tensor, tl, [[T, 128], [64, NCH]])
            oap = _ap(hsf.ap().tensor, tl, [[T, 128], [64, NCH]])
            av = p.v(lambda e, uap=uap, oap=oap: e.tensor_tensor(
                oap, ps_c[:, 0:NCH], uap, AL.add), waits=[("t", mv)])
            p.wait("t", [("v", av)])
        scan_av = p.c("v")
        s2tok = {}
        s2cnt = {}
        y_rv = {}
        y_dv2 = {}

        def w2load(dt):
            if dt >= NC8:
                return
            j = dt % 2
            wt = [("t", s2cnt.get(dt - 2, 0))]
            wv1 = p.gdma(lambda e, dt=dt, j=j: e.dma_start(
                w2s[:, j, 0, :, :],
                _ap(DwT_d.tensor, dt * 128,
                    [[D, 128], [128 * D, NC8], [1, 128]])), waits=wt)
            wv2 = p.gdma(lambda e, dt=dt, j=j: e.dma_start(
                w2s[:, j, 1, :, :],
                _ap(gwT_d.tensor, dt * 128,
                    [[D, 128], [128 * D, NC8], [1, 128]])), waits=wt)
            s2tok[dt] = [wv1, wv2]
        w2load(0)
        w2load(1)
        YPB = [(ps_a[:, 0:512], ps_b[:, 0:512], scrA[:], scrB[:]),
               (ps_c[:, 0:512], ps_s[:, 0:512], sc2A[:], sc2B[:])]
        it = 0
        for dt in range(NC8):
            j = dt % 2
            for tc in range(NT):
                t0 = tc * 512
                pa, pb, sA, sB = YPB[it % 2]
                w0 = []
                if tc == 0:
                    w0 += s2tok[dt]
                if it < 2:
                    w0.append(("v", scan_av))
                if it - 2 in y_rv:
                    w0.append(("v", y_rv[it - 2]))
                p.t(lambda e, dt=dt, t0=t0, pa=pa: e.matmul(
                    pa, cwb[:, dt, :], hsf[:, t0:t0 + 512],
                    start=True, stop=False), waits=w0)
                for c in range(NC8):
                    p.t(lambda e, c=c, t0=t0, j=j, pa=pa: e.matmul(
                        pa, w2s[:, j, 0, c, :], bufB[:, c, t0:t0 + 512],
                        start=False, stop=(c == NC8 - 1)))
                for c in range(NC8):
                    p.t(lambda e, c=c, t0=t0, j=j, pb=pb: e.matmul(
                        pb, w2s[:, j, 1, c, :], bufB[:, c, t0:t0 + 512],
                        start=(c == 0), stop=(c == NC8 - 1)))
                mv = p.c("t")
                sw = [("t", mv)]
                if it - 2 in y_dv2:
                    sw.append(("v", y_dv2[it - 2]))
                sgv = p.s(lambda e, dt=dt, pb=pb, sB=sB: e.activation(
                    sB, pb, AF.Sigmoid,
                    bias=b2b[:, 22 + dt:23 + dt]), waits=sw)
                y_dv2[it] = p.v(lambda e, dt=dt, pa=pa, sA=sA, sB=sB:
                    e.scalar_tensor_tensor(
                        sA, pa, cdb[:, dt:dt + 1], sB,
                        AL.add, AL.mult), waits=[("s", sgv), ("v", cdv)])
                y_rv[it] = p.v(lambda e, dt=dt, t0=t0, sA=sA: e.tensor_tensor(
                    xres[:, dt, t0:t0 + 512], xres[:, dt, t0:t0 + 512],
                    sA, AL.add))
                it += 1
            s2cnt[dt] = p.c("t")
            w2load(dt + 2)
        p.wait("t", [("v", y_rv[it - 1]), ("v", y_rv[it - 2])])


    def block3_moe_pre():
        p.barrier()
        CAP = T // 8
        SG = T // 8
        lnv = layer_norm(bufB)
        gw = p.gdma(lambda e: e.dma_start(
            gwT3b[:], _ap(gwT3_d.tensor, 0, [[E, 128], [128 * E, NC8], [1, E]])))
        xst = hshb[:, 0:2, :].rearrange("p a b -> p (a b)")
        for tt in range(NTT):
            p.wait("t", [("v", p.c("v"))])
            for c in range(NC8):
                p.t(lambda e, tt=tt, c=c: e.transpose(
                    ps_t[:, c * 128:c * 128 + 128],
                    bufB[:, c, tt * 128:tt * 128 + 128], IDENT))
            trv = p.c("t")
            ev = p.v(lambda e: e.tensor_copy(xst, ps_t[:]), waits=[("t", trv)])
            dv2 = p.ydma(lambda e, tt=tt: e.dma_start(
                _ap(xln_h.tensor, tt * 128 * D, [[D, 128], [1, D]]), xst),
                waits=[("v", ev)])
            p.wait("v", [dv2])
        p.wait("t", [gw])
        for tt in range(NTT):
            for c in range(NC8):
                p.t(lambda e, tt=tt, c=c: e.matmul(
                    ps_c[:, 0:E], bufB[:, c, tt * 128:tt * 128 + 128],
                    gwT3b[:, c, :], start=(c == 0), stop=(c == NC8 - 1)))
            mv = p.c("t")
            p.v(lambda e: e.reduce_max(rbuf[:, 0:1], ps_c[:, 0:E], AX.X,
                                       negate=True), waits=[("t", mv)],
                cx=True)
            vv = p.c("v")
            sv = p.s(lambda e, tt=tt: e.activation(
                sctm[:, tt, :], ps_c[:, 0:E], AF.Exp, bias=rbuf[:, 0:1],
                accum_out=lbuf[:, 0:1]), waits=[("v", vv), ("t", mv)])
            p.v(lambda e: e.reciprocal(rbuf[:, 1:2], lbuf[:, 0:1]),
                waits=[("s", sv)])
            scv = p.v(lambda e, tt=tt: e.tensor_scalar_mul(
                sctm[:, tt, :], sctm[:, tt, :], rbuf[:, 1:2]))
            p.wait("t", [("v", scv)])
        for tg in range(NTT // 4):
            for j in range(4):
                tt = tg * 4 + j
                p.t(lambda e, tt=tt, j=j: e.transpose(
                    ps_b[0:16, j * 128:j * 128 + 128],
                    sctm[:, tt, :], IDENTF))
            trv = p.c("t")
            ev = p.v(lambda e, tg=tg: e.tensor_copy(
                semb[:, tg * 512:tg * 512 + 512], ps_b[0:16, :]),
                waits=[("t", trv)])
            p.wait("t", [("v", ev)])
        sh_tok = {}
        sh_cnt = {}
        sd_tok = {}
        sd_cnt = {}
        chain = {}
        NQ = FSH // 256

        def shload(gq):
            if gq >= NT * NQ:
                return
            qq = gq % NQ
            j = gq % 2
            wt = [("t", sh_cnt.get(gq - 2, 0))]
            w1 = p.gdma(lambda e, qq=qq, j=j: e.dma_start(
                wsgb[:, j, :, :], _ap(WsgT_d.tensor, qq * 256,
                    [[FSH, 128], [128 * FSH, NC8], [1, 256]])), waits=wt)
            w2 = p.gdma(lambda e, qq=qq, j=j: e.dma_start(
                wsub[:, j, :, :], _ap(WsuT_d.tensor, qq * 256,
                    [[FSH, 128], [128 * FSH, NC8], [1, 256]])), waits=wt)
            sh_tok[gq] = [w1, w2]

        def sdload(gd):
            if gd >= NT * NC8:
                return
            dt = gd % NC8
            j = gd % 2
            sd_tok[gd] = [p.gdma(lambda e, dt=dt, j=j: e.dma_start(
                wsdb[:, j, :, :], _ap(WsdT_d.tensor, dt * 128,
                    [[D, 128], [128 * D, 16], [1, 128]])),
                waits=[("t", sd_cnt.get(gd - 2, 0))])]
        shload(0)
        shload(1)
        sdload(0)
        sdload(1)
        SPB = [(ps_a[:, 0:512], ps_b[:, 0:512], scrB[:]),
               (ps_c[:, 0:512], ps_s[:, 0:512], scrC[:])]
        gch = 0
        for tc in range(NT):
            t0 = tc * 512
            last_hv = 0
            for qq in range(NQ):
                gq = tc * NQ + qq
                j = gq % 2
                for ft in range(2):
                    pg, pu, sB = SPB[gch % 2]
                    w0 = []
                    if ft == 0:
                        w0 += sh_tok[gq]
                    if gch - 2 in chain:
                        w0.append(("v", chain[gch - 2]))
                    for c in range(NC8):
                        p.t(lambda e, c=c, ft=ft, j=j, t0=t0, pg=pg: e.matmul(
                            pg, wsgb[:, j, c, ft * 128:ft * 128 + 128],
                            bufB[:, c, t0:t0 + 512],
                            start=(c == 0), stop=(c == NC8 - 1)),
                            waits=(w0 if c == 0 else []))
                    gv2 = p.c("t")
                    for c in range(NC8):
                        p.t(lambda e, c=c, ft=ft, j=j, t0=t0, pu=pu: e.matmul(
                            pu, wsub[:, j, c, ft * 128:ft * 128 + 128],
                            bufB[:, c, t0:t0 + 512],
                            start=(c == 0), stop=(c == NC8 - 1)))
                    uv2 = p.c("t")
                    sw = [("t", gv2)]
                    if gch - 2 in chain:
                        sw.append(("v", chain[gch - 2]))
                    sv = p.s(lambda e, pg=pg, sB=sB: e.activation(
                        sB, pg, AF.Gelu), waits=sw)
                    hv = p.v(lambda e, qq=qq, ft=ft, sB=sB, pu=pu:
                        e.tensor_tensor(
                            hshb[:, qq * 2 + ft, :], sB, pu, AL.mult),
                        waits=[("s", sv), ("t", uv2)])
                    chain[gch] = hv
                    last_hv = hv
                    gch += 1
                sh_cnt[gq] = p.c("t")
                shload(gq + 2)
            for dt in range(NC8):
                gd = tc * NC8 + dt
                pg, pu, sB = SPB[gch % 2]
                w0 = list(sd_tok[gd])
                if dt == 0:
                    w0.append(("v", last_hv))
                if gch - 2 in chain:
                    w0.append(("v", chain[gch - 2]))
                for fc in range(16):
                    p.t(lambda e, fc=fc, gd=gd, pg=pg: e.matmul(
                        pg, wsdb[:, gd % 2, fc, :], hshb[:, fc, :],
                        start=(fc == 0), stop=(fc == 15)),
                        waits=(w0 if fc == 0 else []))
                mv = p.c("t")
                rv = p.v(lambda e, dt=dt, t0=t0, pg=pg: e.tensor_tensor(
                    xres[:, dt, t0:t0 + 512], xres[:, dt, t0:t0 + 512],
                    pg, AL.add), waits=[("t", mv)])
                chain[gch] = rv
                gch += 1
                sd_cnt[gd] = p.c("t")
                sdload(gd + 2)
        p.wait("t", [("v", chain[gch - 1]), ("v", chain[gch - 2])])
        zv = p.v(lambda e: e.memset(smls[:], 0.0))
        lo_col = 0
        hk = 0.5
        for it in range(44):
            lo = smls[0:16, lo_col:lo_col + 1]
            lo2 = smls[0:16, 4 - lo_col:5 - lo_col]
            p.v(lambda e, lo=lo, hk=hk: e.tensor_scalar(
                smls[0:16, 3:4], lo, hk, None, AL.add))
            p.v(lambda e: e.tensor_scalar(
                junkb[:], semb[:], smls[0:16, 3:4], None, AL.is_ge, AL.add,
                accum_out=smls[0:16, 1:2]), cx=True)
            p.v(lambda e: e.tensor_scalar(
                smls[0:16, 2:3], smls[0:16, 1:2], CAP - 0.5, None, AL.is_ge))
            p.v(lambda e, lo=lo, lo2=lo2, hk=hk: e.scalar_tensor_tensor(
                lo2, smls[0:16, 2:3], hk, lo, AL.mult, AL.add))
            lo_col = 4 - lo_col
            hk *= 0.5
        LO = smls[0:16, lo_col:lo_col + 1]
        p.v(lambda e: e.tensor_scalar(mskb[:], semb[:], LO, None, AL.is_ge))
        zv = p.v(lambda e: e.memset(junkb[:], 0.0))
        p.v(lambda e: e.tensor_tensor_scan(
            semb[:], junkb[:], mskb[:], 0.0, AL.max, AL.add), cx=True)
        p.v(lambda e: e.tensor_tensor(mskb[:], mskb[:], semb[:], AL.mult))
        p.v(lambda e: e.tensor_scalar(
            junkb[:], mskb[:], CAP + 0.5, -30000.0, AL.is_gt, AL.mult))
        sl2 = p.v(lambda e: e.scalar_tensor_tensor(
            mskb[:], mskb[:], -1.0, junkb[:], AL.add, AL.add))
        p.wait("t", [("v", sl2)])
        for tg in range(NTT // 4):
            for j in range(4):
                tt = tg * 4 + j
                p.t(lambda e, tt=tt, j=j: e.transpose(
                    ps_c[:, j * 16:j * 16 + 16],
                    mskb[:, tt * 128:tt * 128 + 128], IDENTF[0:16, 0:16]))
            trv = p.c("t")
            ev = p.v(lambda e, tg=tg: e.tensor_copy(
                _ap(slottm.ap().tensor, tg * 4 * 16,
                    [[(T // 128) * 16, 128], [1, 64]]),
                ps_c[:, 0:64]), waits=[("t", trv)])
            p.wait("t", [("v", ev)])
        g1 = p.g(lambda e: e.iota(iotab[:], [[1, 256]], base=0,
                                  channel_multiplier=0,
                                  allow_small_or_imprecise_dtypes=True))
        g2 = p.g(lambda e: e.iota(ttrow[:], [[1, T // 128]], base=0,
                                  channel_multiplier=0,
                                  allow_small_or_imprecise_dtypes=True))
        g3 = p.g(lambda e: e.iota(pcol[:], [[0, 1]], base=0,
                                  channel_multiplier=1,
                                  allow_small_or_imprecise_dtypes=True))
        NTTl = T // 128
        r3t = rhs3.ap().tensor
        p.v(lambda e: e.tensor_copy(
            _ap(r3t, 0, [[NTTl * E * 3, 128], [E * 3, NTTl], [3, E]]),
            _ap(pcol.ap().tensor, 0, [[1, 128], [0, NTTl], [0, E]])),
            waits=[("g", g3)])
        p.v(lambda e: e.tensor_copy(
            _ap(r3t, 1, [[NTTl * E * 3, 128], [E * 3, NTTl], [3, E]]),
            _ap(ttrow.ap().tensor, 0, [[NTTl, 128], [1, NTTl], [0, E]])),
            waits=[("g", g2)])
        r3v = p.v(lambda e: e.tensor_copy(
            _ap(r3t, 2, [[NTTl * E * 3, 128], [E * 3, NTTl], [3, E]]),
            sctm[:].rearrange("p a b -> p (a b)")))
        NH = max(CAP // 128, 1)
        p.v(lambda e: e.memset(idw[:], 0.0))
        p.v(lambda e: e.memset(idc[:], 0.0))
        for ex in range(E):
            for tt in range(NTTl):
                pv2 = p.v(lambda e, ex=ex, tt=tt: e.tensor_scalar(
                    pbt[:, 0:CAP], iotab[:, 0:CAP],
                    slottm[:, tt, ex:ex + 1], None, AL.is_equal),
                    waits=[("g", g1)])
                p.wait("t", [("v", pv2)])
                for hf in range(NH):
                    tgt = ps_c if hf == 0 else ps_b
                    p.t(lambda e, ex=ex, tt=tt, hf=hf, tgt=tgt: e.matmul(
                        tgt[0:min(CAP, 128), 0:3],
                        pbt[:, hf * 128:hf * 128 + min(CAP, 128)],
                        rhs3[:, tt, ex, :],
                        start=(tt == 0), stop=(tt == NTTl - 1)))
                mmv = p.c("t")
                if tt < NTTl - 1:
                    p.wait("v", [("t", mmv)])
            mmv = p.c("t")
            for hf in range(NH):
                tgt = ps_c if hf == 0 else ps_b
                p.v(lambda e, ex=ex, hf=hf, tgt=tgt: e.tensor_copy(
                    idw[0:min(CAP, 128), ex, hf, :], tgt[0:min(CAP, 128), 0:3]),
                    waits=[("t", mmv)])
            p.wait("t", [("v", p.c("v"))])
        idwt = idw.ap().tensor
        idcv = p.v(lambda e: e.scalar_tensor_tensor(
            idc[:].rearrange("p a b -> p (a b)"),
            _ap(idwt, 1, [[E * 6, 128], [3, E * 2]]), 128.0,
            _ap(idwt, 0, [[E * 6, 128], [3, E * 2]]), AL.mult, AL.add))
        uv32 = p.v(lambda e: e.tensor_copy(
            iu32[:].rearrange("p a b -> p (a b)"),
            idc[:].rearrange("p a b -> p (a b)")), waits=[("v", idcv)])
        if CAP < 128:
            uv32 = p.v(lambda e: e.tensor_copy(
                iu32[64:128, :, :].rearrange("p a b -> p (a b)"),
                _ap(pcol.ap().tensor, 64 * 1, [[1, 64], [0, E * 2]])))

    def block3_moe_experts():
        p.barrier()
        CAP = T // 8
        NST = CAP // 128
        assert CAP == 256 and NST == 2
        xlnap = _ap(xln_h.tensor, 0, [[D, T], [1, D]])
        outap = _ap(out.tensor, 0, [[D, T], [1, D]])
        IU32V = p.c("v")
        gath_toks = {}
        tfree = {}
        mm_tcnt = {}
        dn_tcnt = {}
        wl_toks = {}
        wd_toks = {}
        gbank_ev = {}
        dbank_ev = {}
        sc_toks = []
        last_hv = {}
        last_ev = {}

        def gath(ex):
            if ex >= E:
                return
            toks = []
            b = ex % 2
            for st in range(NST):
                toks.append(p.gdma(lambda e, ex=ex, st=st, b=b:
                    e.indirect_dma_start(
                        xtokt[:, b, st, :], None, xlnap,
                        bass.IndirectOffsetOnAxis(iu32[0:128, ex, st:st + 1], 0)),
                    waits=[("v", IU32V), ("t", tfree.get(ex - 2, 0))]))
            gath_toks[ex] = toks

        def wl(ex, i):
            if ex >= E:
                return
            q = 4 * ex + i
            j = q % 2
            wt = [("t", mm_tcnt.get(q - 2, 0))]
            t1 = p.gdma(lambda e, ex=ex, i=i, j=j: e.dma_start(
                wgeb[:, j, :, :],
                _ap(WgT_d.tensor, ex * D * FEXP + i * 512,
                    [[FEXP, 128], [128 * FEXP, NC8], [1, 512]])), waits=wt)
            t2 = p.gdma(lambda e, ex=ex, i=i, j=j: e.dma_start(
                wueb[:, j, :, :],
                _ap(WuT_d.tensor, ex * D * FEXP + i * 512,
                    [[FEXP, 128], [128 * FEXP, NC8], [1, 512]])), waits=wt)
            wl_toks[q] = [t1, t2]

        def wd(ex, j):
            if ex >= E:
                return
            r = 4 * ex + j
            jj = r % 2
            wd_toks[r] = [p.gdma(lambda e, ex=ex, j=j, jj=jj: e.dma_start(
                wdeb[:, jj, :, :],
                _ap(WdT_d.tensor, ex * FEXP * D + j * 256,
                    [[D, 128], [128 * D, 16], [1, 256]])),
                waits=[("t", dn_tcnt.get(r - 2, 0))])]

        def tstage(ex):
            b = ex % 2
            ev = 0
            for st in range(NST):
                if st == 0:
                    w = list(gath_toks[ex]) + [("v", last_ev.get(ex - 1, 0))]
                else:
                    w = [("v", ev)]
                for c in range(NC8):
                    p.t(lambda e, st=st, c=c, b=b: e.transpose(
                        ps_t[:, c * 128:c * 128 + 128],
                        xtokt[:, b, st, c * 128:c * 128 + 128], IDENT),
                        waits=(w if c == 0 else []))
                trv = p.c("t")
                ev = p.v(lambda e, st=st, b=b: e.tensor_copy(
                    _ap(xtokf.ap().tensor, b * (NC8 * CAP) + st * 128,
                        [[2 * NC8 * CAP, 128], [CAP, NC8], [1, 128]]),
                    _ap(ps_t.ap().tensor, 0,
                        [[1024, 128], [128, NC8], [1, 128]])),
                    waits=[("t", trv)])
            tfree[ex] = p.c("t")
            last_ev[ex] = ev

        GB = [(ps_a[:, 0:CAP], ps_b[:, 0:CAP]),
              (ps_s[:, 0:CAP], ps_s[:, 512:512 + CAP])]

        def mm(ex, i):
            b = ex % 2
            q = 4 * ex + i
            j = q % 2
            for ft in range(4):
                g = 4 * i + ft
                gg = 16 * ex + g
                pg, pu = GB[gg % 2]
                w0 = []
                if ft == 0:
                    w0 = wl_toks[q] + [("v", last_ev[ex])]
                if gg - 2 in gbank_ev:
                    w0 = w0 + [("v", gbank_ev[gg - 2])]
                for c in range(NC8):
                    p.t(lambda e, c=c, ft=ft, j=j, b=b, pg=pg: e.matmul(
                        pg, wgeb[:, j, c, ft * 128:ft * 128 + 128],
                        xtokf[:, b, c, :], start=(c == 0),
                        stop=(c == NC8 - 1)), waits=(w0 if c == 0 else []))
                gv2 = p.c("t")
                for c in range(NC8):
                    p.t(lambda e, c=c, ft=ft, j=j, b=b, pu=pu: e.matmul(
                        pu, wueb[:, j, c, ft * 128:ft * 128 + 128],
                        xtokf[:, b, c, :], start=(c == 0),
                        stop=(c == NC8 - 1)))
                uv2 = p.c("t")
                sv = p.s(lambda e, pg=pg: e.activation(
                    scrB[:, 0:CAP], pg, AF.Gelu),
                    waits=[("t", gv2)])
                hv = p.v(lambda e, g=g, b=b, pu=pu: e.tensor_tensor(
                    heb[:, b, g, :], scrB[:, 0:CAP],
                    pu, AL.mult), waits=[("s", sv), ("t", uv2)])
                gbank_ev[gg] = hv
                last_hv[ex] = hv
            mm_tcnt[q] = p.c("t")

        DB = [ps_c[:, 0:256], ps_s[:, 1024:1280]]

        def dn(ex, j):
            b = ex % 2
            r = 4 * ex + j
            jj = r % 2
            for st in range(NST):
                gd = 2 * r + st
                pd = DB[gd % 2]
                w0 = []
                if st == 0:
                    w0 = wd_toks[r] + [("v", last_hv[ex])]
                if gd - 2 in dbank_ev:
                    w0 = w0 + [("v", dbank_ev[gd - 2])]
                for fc in range(16):
                    p.t(lambda e, fc=fc, st=st, jj=jj, b=b, pd=pd: e.matmul(
                        pd, heb[:, b, fc, st * 128:st * 128 + 128],
                        wdeb[:, jj, fc, :], start=(fc == 0), stop=(fc == 15)),
                        waits=(w0 if fc == 0 else []))
                mv = p.c("t")
                wv = [("t", mv)]
                if j == 0 and st == 0 and sc_toks:
                    wv = wv + sc_toks[-2:]
                evv = p.v(lambda e, st=st, j=j, ex=ex, pd=pd: e.tensor_scalar_mul(
                    eob[0:128, st, j * 256:j * 256 + 256], pd,
                    idw[0:128, ex, st, 2:3]), waits=wv)
                dbank_ev[gd] = evv
            dn_tcnt[r] = p.c("t")

        def sc(ex):
            vlast = p.c("v")
            for st in range(NST):
                wv = [("v", vlast)] + (sc_toks[-1:] if sc_toks else [])
                sc_toks.append(p.gdma(lambda e, ex=ex, st=st:
                    e.indirect_dma_start(
                        outap,
                        bass.IndirectOffsetOnAxis(iu32[0:128, ex, st:st + 1], 0),
                        eob[0:128, st, :], None, compute_op=AL.add),
                    waits=wv))

        gath(0)
        wl(0, 0)
        wl(0, 1)
        for ex in range(E):
            tstage(ex)
            mm(ex, 0)
            wl(ex, 2)
            mm(ex, 1)
            wl(ex, 3)
            wd(ex, 0)
            wd(ex, 1)
            mm(ex, 2)
            mm(ex, 3)
            wl(ex + 1, 0)
            wl(ex + 1, 1)
            dn(ex, 0)
            wd(ex, 2)
            dn(ex, 1)
            wd(ex, 3)
            dn(ex, 2)
            gath(ex + 1)
            dn(ex, 3)
            sc(ex)

    if 1 in blocks:
        qg_state["g"] = quant_gen()
    if 0 in blocks:
        block0_attn()
    if 1 in blocks:
        block1_ffn()
    if 2 in blocks:
        block2_ssm()
    if 3 in blocks:
        block3_moe_pre()

    def store_output():
        p.barrier()
        for tt in range(NTT):
            vv = 0
            for c in range(NC8):
                cc = c % 4
                trv = p.t(lambda e, tt=tt, c=c, cc=cc: e.transpose(
                    ps_s[:, cc * 512:cc * 512 + 128],
                    xres[:, c, tt * 128:tt * 128 + 128], IDENTF))
                vv = p.v(lambda e, c=c, cc=cc: e.tensor_copy(
                    tout[:, c, :], ps_s[:, cc * 512:cc * 512 + 128]),
                    waits=[("t", trv)])
                p.wait("t", [("v", vv)])
            ov = p.ydma(lambda e, tt=tt: e.dma_start(
                _ap(out.tensor, tt * 128 * D, [[D, 128], [1, D]]),
                tout[:].rearrange("p c d -> p (c d)")),
                waits=[("v", vv)])
            p.wait("v", [ov])
        p.wait("y", [("yd%d" % k, p.c("yd%d" % k)) for k in range(8)])

    store_output()
    if 3 in blocks:
        block3_moe_experts()
        snap = [(n, p.cnt[n]) for n in p.cnt if p.cnt[n] > 0]
        p.wait("y", snap)

    with nc.Block() as block:
        @block.tensor
        def _(e):
            for f in p.q["t"]:
                f(e)

        @block.vector
        def _(e):
            for f in p.q["v"]:
                f(e)

        @block.scalar
        def _(e):
            for f in p.q["s"]:
                f(e)

        @block.gpsimd
        def _(e):
            for f in p.q["g"]:
                f(e)

        @block.sync
        def _(e):
            for f in p.q["y"]:
                f(e)

    stack.close()
    return nc


def make_consts(T):
    import ml_dtypes
    ident = np.eye(128, dtype=np.float32)
    cbf = np.zeros((128, 640), np.float32)
    cbf[:, 0:128] = ident
    mdiag = np.zeros((128, 128), np.float32)
    iu = np.triu_indices(128, 1)
    mdiag[iu] = NEG
    cbf[:, 128:256] = mdiag
    cbf[:, 256:384] = NEG
    cbf[:, 384] = 1.0
    cbf[:, 512:640] = mdiag.T
    cf32 = np.zeros((128, 512), np.float32)
    cf32[0, 0:128] = 1.0
    cf32[:, 128:256] = ident
    cf32[:, 300] = 1e-5
    cf32[:, 301] = -16.0
    inv = 1.0 / (10000.0 ** (np.arange(0, DH, 2, dtype=np.float64) / DH))
    tt = np.arange(T, dtype=np.float64)
    fr = np.outer(inv, tt)
    cos = np.cos(fr)
    sin = np.sin(fr)
    trig = np.zeros((128, 2 * T), np.float32)
    for pp in range(128):
        i = (pp % 64) // 2
        trig[pp, 0:T] = cos[i]
        trig[pp, T:2 * T] = (-sin[i]) if pp % 2 == 0 else sin[i]
    return cbf.astype(ml_dtypes.bfloat16), cf32, trig.astype(ml_dtypes.bfloat16)


def kernel(**inputs):
    T = 2048
    B = 8
    x = np.asarray(inputs["L0_attn_x"])
    cbf, cf32, trig = make_consts(T)
    SG = T // 8
    cf32[:, 302] = SG * (np.arange(128) // 16)
    for p_ in range(128):
        cf32[p_, 304 + (p_ % 16)] = 1.0
    bias2 = np.zeros((128, 32), np.float32)
    bias2[:, 0] = np.asarray(inputs["L1_attn_B_b"])
    bias2[:, 1] = np.asarray(inputs["L1_attn_s_b2"])
    bias2[:, 2:6] = np.asarray(inputs["L1_attn_s_b1"]).reshape(4, 128).T
    bias2[:, 6:14] = np.asarray(inputs["L1_attn_C_b"]).reshape(8, 128).T
    bias2[:, 14:22] = np.asarray(inputs["L1_attn_D_b"]).reshape(8, 128).T
    bias2[:, 22:30] = np.asarray(inputs["L1_attn_g_b"]).reshape(8, 128).T
    nc = build_program(T=T, blocks=(0, 1, 2, 3))
    shared = {
        "wqkv": np.ascontiguousarray(inputs["L0_attn_w_qkv"]),
        "wout": np.ascontiguousarray(inputs["L0_attn_w_out"]),
        "wgate": np.ascontiguousarray(inputs["L0_ffn_w_gate"]),
        "wup": np.ascontiguousarray(inputs["L0_ffn_w_up"]),
        "wdown": np.ascontiguousarray(inputs["L0_ffn_w_down"]),
        "At": np.ascontiguousarray(np.asarray(inputs["L1_attn_A"]).T),
        "BwT": np.ascontiguousarray(np.asarray(inputs["L1_attn_B_w"]).T),
        "CwT": np.ascontiguousarray(np.asarray(inputs["L1_attn_C_w"]).T),
        "DwT": np.ascontiguousarray(np.asarray(inputs["L1_attn_D_w"]).T),
        "sw1T": np.ascontiguousarray(np.asarray(inputs["L1_attn_s_w1"]).T),
        "sw2T": np.ascontiguousarray(np.asarray(inputs["L1_attn_s_w2"]).T),
        "gwT": np.ascontiguousarray(np.asarray(inputs["L1_attn_g_w"]).T),
        "bias2": bias2,
        "gatewT": np.ascontiguousarray(np.asarray(inputs["L1_ffn_gate_w"]).T),
        "WgT": np.ascontiguousarray(np.asarray(inputs["L1_ffn_Wg"]).transpose(0, 2, 1)),
        "WuT": np.ascontiguousarray(np.asarray(inputs["L1_ffn_Wu"]).transpose(0, 2, 1)),
        "WdT": np.ascontiguousarray(np.asarray(inputs["L1_ffn_Wd"]).transpose(0, 2, 1)),
        "WsgT": np.ascontiguousarray(np.asarray(inputs["L1_ffn_Ws_g"]).T),
        "WsuT": np.ascontiguousarray(np.asarray(inputs["L1_ffn_Ws_u"]).T),
        "WsdT": np.ascontiguousarray(np.asarray(inputs["L1_ffn_Ws_d"]).T),
        "trig": trig, "cbf": cbf, "cf32": cf32,
    }
    in_maps = []
    for b in range(B):
        m = dict(shared)
        m["xT"] = np.ascontiguousarray(x[b].T)
        in_maps.append(m)
    res = run_bass_kernel_spmd(nc, in_maps, core_ids=list(range(B)))
    return np.stack([r["out"] for r in res.results], axis=0)
